# version 08
# baseline (speedup 1.0000x reference)
from contextlib import ExitStack

import numpy as np
import ml_dtypes

import concourse.bass as bass
import concourse.mybir as mybir
from concourse.bass_utils import run_bass_kernel_spmd

F32 = mybir.dt.float32
BF16 = mybir.dt.bfloat16
AF = mybir.ActivationFunctionType
ALU = mybir.AluOpType
AX = mybir.AxisListType

S = 2048
D = 1024
NT = 16
NCH = 8
NH = 16
HD = 64

N_SUBLAYERS = 8


class Ctr:
    def __init__(self, stack, nc, name):
        self.h = stack.enter_context(nc.semaphore(name))
        self.n = 0

    def inc(self, instr, k=1):
        instr.then_inc(self.h, k)
        self.n += k
        return self.n


class DmaStream:

    def __init__(self, stack, nc, name, serialize=True):
        self.h = stack.enter_context(nc.semaphore(name))
        self.n = 0
        self.serialize = serialize

    def start(self, eng, out, in_, **kw):
        if self.serialize and self.n:
            eng.wait_ge(self.h, self.n)
        eng.dma_start(out=out, in_=in_, **kw).then_inc(self.h, 16)
        self.n += 16
        return self.n


class Arena:

    def __init__(self, p, kb):
        self.t = p.sb("arena", [128, kb * 512], BF16)
        self.off = 0

    def reset(self):
        self.off = 0

    def _take(self, n_bf16):
        pad = (n_bf16 + 15) // 16 * 16
        ap = self.t[:, self.off:self.off + n_bf16]
        self.off += pad
        assert self.off <= self.t.shape[1], f"arena overflow {self.off}"
        return ap

    def bf16(self, shape):
        assert shape[0] == 128
        n = int(np.prod(shape[1:]))
        ap = self._take(n)
        if len(shape) == 2:
            return ap
        return ap.rearrange("p (" + " ".join(f"d{i}" for i in range(1, len(shape))) + ") -> p " +
                            " ".join(f"d{i}" for i in range(1, len(shape))),
                            **{f"d{i}": shape[i] for i in range(1, len(shape))})

    def f32(self, shape):
        assert shape[0] == 128
        n = int(np.prod(shape[1:]))
        ap = self._take(2 * n).bitcast(F32)
        if len(shape) == 2:
            return ap
        return ap.rearrange("p (" + " ".join(f"d{i}" for i in range(1, len(shape))) + ") -> p " +
                            " ".join(f"d{i}" for i in range(1, len(shape))),
                            **{f"d{i}": shape[i] for i in range(1, len(shape))})

    def i8(self, shape):
        assert shape[0] == 128 and len(shape) == 2
        n = shape[1]
        return self._take((n + 1) // 2).bitcast(mybir.dt.int8)[:, 0:n]


class P:
    def __init__(self, nc, stack, block):
        self.nc = nc
        self.stack = stack
        self.block = block
        self.sv = Ctr(stack, nc, "sv")
        self.ss = Ctr(stack, nc, "ss")
        self.st = Ctr(stack, nc, "st")
        self.sg = Ctr(stack, nc, "sg")
        self.dmisc = DmaStream(stack, nc, "dmisc", serialize=False)
        self.dx = [DmaStream(stack, nc, f"dx{i}") for i in range(2)]
        self.dw = [DmaStream(stack, nc, f"dw{i}") for i in range(2)]
        self.dlg = DmaStream(stack, nc, "dlg")
        self.dlr = [DmaStream(stack, nc, f"dlr{i}") for i in range(2)]
        self.dws = [DmaStream(stack, nc, f"dws{i}") for i in range(4)]
        self.dp = [DmaStream(stack, nc, f"dp{i}") for i in range(4)]
        self.dps = DmaStream(stack, nc, "dps")

    def sb(self, name, shape, dtype):
        return self.stack.enter_context(self.nc.sbuf_tensor(name, shape, dtype))

    def load_par(self, eng, dst, src, n=4):
        C = dst.shape[1]
        n = min(n, C)
        step = (C + n - 1) // n
        wr = []
        for i in range(n):
            c0, c1 = i * step, min(C, (i + 1) * step)
            if c0 >= c1:
                break
            s = self.dws[i % 4]
            wr.append((s.h, s.start(eng, dst[:, c0:c1], src[:, c0:c1])))
        return wr

    def ps(self, name, shape, dtype=F32):
        return self.stack.enter_context(self.nc.psum_tensor(name, shape, dtype))


def wwait(p, eng, w_ready, slot=None):
    if isinstance(w_ready, (list, tuple)):
        for s_, c_ in w_ready:
            eng.wait_ge(s_, c_)
    else:
        eng.wait_ge(p.dw[slot].h, w_ready)


def build_program():
    nc = bass.Bass("TRN2", target_bir_lowering=False)

    x_in = nc.dram_tensor("x", [S, D], F32, kind="ExternalInput").ap()
    y_out = nc.dram_tensor("y", [S, D], F32, kind="ExternalOutput").ap()
    xd = nc.dram_tensor("xd", [NT, 128, D], F32, kind="Internal").ap()

    w = {}
    shapes = [
        ("L0A_Wq", [D, D]), ("L0A_Wk", [D, D]), ("L0A_Wv", [D, D]),
        ("L0A_Wo", [D, D]), ("L0A_bo", [D]),
        ("L0F_W", [D, D]), ("L0F_Wp", [256, D]), ("L0F_bp", [256]),
        ("L0F_sim", [256, 4]), ("L0F_temp", [1]),
    ]
    shapes += [("L0F_Wd_all", [15, D]), ("L0F_Wu_all", [D, 15])]
    shapes += [("L0B_qkv_w", [D, 3 * D]), ("L0B_qkv_b", [3 * D]),
               ("L0B_o_w", [D, D]), ("L0B_o_b", [D])]
    shapes += [("L1A_Wq", [D, D]), ("L1A_Wk", [D, D]), ("L1A_Wv", [D, D]),
               ("L1A_Wo", [D, D]), ("L1A_gamma", [D]), ("L1A_beta", [D]),
               ("L1A_Wg1", [D, 64]), ("L1A_Wg2", [64, D])]
    shapes += [("L1F_Wg", [2816, D]), ("L1F_Wu", [2816, D]), ("L1F_Wd", [D, 2816])]
    shapes += [("L1B_wq", [D, 256]), ("L1B_bq", [256]),
               ("L1B_wqc", [256, 896]), ("L1B_bqc", [896]),
               ("L1B_wqr", [D, 128]), ("L1B_bqr", [128]),
               ("L1B_wkv", [D, 256]), ("L1B_bkv", [256]),
               ("L1B_wkr", [D, 128]), ("L1B_bkr", [128]),
               ("L1B_wkc", [256, 896]), ("L1B_bkc", [896]),
               ("L1B_wv", [256, D]), ("L1B_bv", [D]),
               ("L1B_wo", [D, D]), ("L1B_bo", [D]),
               ("c_ropecos", [S, 128]), ("c_ropesin", [S, 128])]
    for name, shape in shapes:
        w[name] = nc.dram_tensor(name, shape, F32, kind="ExternalInput").ap()
    w["wgT_c"] = nc.dram_tensor("wgT_c", [128, NCH, 2816], BF16, kind="Internal").ap()
    w["wuT_c"] = nc.dram_tensor("wuT_c", [128, NCH, 2816], BF16, kind="Internal").ap()
    w["wdT_c"] = nc.dram_tensor("wdT_c", [128, 22, 1024], BF16, kind="Internal").ap()

    c_ident = nc.dram_tensor("c_ident", [128, 128], BF16, kind="ExternalInput").ap()
    c_triT = nc.dram_tensor("c_triT", [128, 128], BF16, kind="ExternalInput").ap()
    w["c_dmatTc"] = nc.dram_tensor("c_dmatTc", [128, 128], F32, kind="ExternalInput").ap()
    w["c_idxp1"] = nc.dram_tensor("c_idxp1", [128, 1], F32, kind="ExternalInput").ap()
    w["c_idxrv"] = nc.dram_tensor("c_idxrv", [128, 1], F32, kind="ExternalInput").ap()

    with ExitStack() as stack:
        block = stack.enter_context(nc.Block())
        p = P(nc, stack, block)
        emit(p, x_in, y_out, xd, w, c_ident, c_triT)
    return nc


def emit(p, x_in, y_out, xd, w, c_ident, c_triT):
    nc, block = p.nc, p.block

    g = {}
    g["ident"] = p.sb("ident", [128, 128], BF16)
    g["triT"] = p.sb("triT", [128, 128], BF16)
    g["onescol"] = p.sb("onescol", [1, 128], BF16)
    g["sm"] = p.sb("sm", [128, NT], F32)
    g["sq"] = p.sb("sq", [128, NT], F32)
    g["mean"] = p.sb("mean", [128, NT], F32)
    g["var"] = p.sb("var", [128, NT], F32)
    g["rstd"] = p.sb("rstd", [128, NT], F32)
    g["nmrs"] = p.sb("nmrs", [128, NT], F32)
    g["xtile"] = [p.sb(f"xtile{i}", [128, 1024], F32) for i in range(2)]
    g["xn_tm"] = p.sb("xn_tm", [128, 2, 1024], BF16)
    g["xn_fm"] = p.sb("xn_fm", [128, NCH, S], BF16)
    g["ptb"] = [p.sb(f"ptb{i}", [128, 512], BF16) for i in range(3)]
    g["attn_fm"] = g["xn_fm"]
    _sq2 = p.sb("sq2_0", [128, 1024], BF16)
    g["sq2"] = [_sq2, _sq2]
    g["segr"] = p.sb("segr", [128, NH], F32)
    g["hmax"] = p.sb("hmax", [128, NH], F32)
    g["red1"] = p.sb("red1", [128, 1], F32)
    g["redbf"] = p.sb("redbf", [128, 1], BF16)
    g["redT"] = p.sb("redT", [1, 128], F32)
    g["csc"] = p.sb("csc", [1, 8], F32)
    g["negc8b"] = p.sb("negc8b", [1, 1], BF16)
    g["negc8"] = p.sb("negc8", [128, 1], F32)
    g["ceps"] = p.sb("ceps", [128, 1], F32)
    g["biasrow"] = p.sb("biasrow", [1, 4 * 1024], BF16)
    g["wbuf"] = [p.sb(f"wbuf{i}", [128, NCH, 1024], BF16) for i in range(2)]
    g["ps_e"] = p.ps("ps_e", [128, 512], F32)
    g["ps_s"] = [p.ps(f"ps_s{i}", [128, 512], F32) for i in range(2)]
    p.arena = Arena(p, 116)
    g["ps_pv"] = p.ps("ps_pv", [128, S], F32)
    g["ps_tp"] = p.ps("ps_tp", [128, 1024], BF16)
    g["dmatTc"] = p.sb("dmatTc", [128, 128], F32)
    g["idxp1"] = p.sb("idxp1", [128, 1], F32)
    g["idxrv"] = p.sb("idxrv", [128, 1], F32)

    @block.sync
    def _(sync):
        p.dmisc.start(sync, g["ident"][:], c_ident)
        p.dmisc.start(sync, g["triT"][:], c_triT)
        p.dmisc.start(sync, g["dmatTc"][:], w["c_dmatTc"])
        p.dmisc.start(sync, g["idxp1"][:], w["c_idxp1"])
        p.dmisc.start(sync, g["idxrv"][:], w["c_idxrv"])
        for i in range(NT):
            p.dmisc.start(sync, xd[i], x_in[i * 128:(i + 1) * 128, :])
        p.dq_consts = p.dmisc.n
    dq_x = p.dmisc.n

    @block.vector
    def _(vector):
        vector.memset(g["onescol"][:], 1.0)
        p.sv.inc(vector.memset(g["ceps"][:], 1e-5))

    rr = lambda t: (lambda: t.rearrange("(c p) o -> p c o", p=128))
    rowv = lambda t: (lambda: t.rearrange("(a d) -> a d", a=1))
    if N_SUBLAYERS >= 1:
        emit_ln(p, g, xd, dq_ready=dq_x)
        emit_mha(p, g, xd,
                 wq_src=rr(w["L0A_Wq"]), wk_src=rr(w["L0A_Wk"]),
                 wv_src=rr(w["L0A_Wv"]), wo_src=rr(w["L0A_Wo"]),
                 qkvb_src=None, ob_src=rowv(w["L0A_bo"]), scrambled=True)
    def do_ffn0():
        emit_barrier(p)
        emit_ln(p, g, xd)
        emit_ffn0(p, g, xd, W=w["L0F_W"], Wp=w["L0F_Wp"], bp=w["L0F_bp"],
                  sim=w["L0F_sim"], temp=w["L0F_temp"],
                  Wd=w["L0F_Wd_all"], Wu=w["L0F_Wu_all"])

    if N_SUBLAYERS >= 2:
        do_ffn0()
    if N_SUBLAYERS >= 3:
        emit_barrier(p)
        emit_ln(p, g, xd)
        qkv = w["L0B_qkv_w"].rearrange("(c p) o -> p c o", p=128)
        emit_mha(p, g, xd,
                 wq_src=lambda: qkv[:, :, 0:D], wk_src=lambda: qkv[:, :, D:2 * D],
                 wv_src=lambda: qkv[:, :, 2 * D:3 * D], wo_src=rr(w["L0B_o_w"]),
                 qkvb_src=rowv(w["L0B_qkv_b"]), ob_src=rowv(w["L0B_o_b"]),
                 scrambled=False)
    if N_SUBLAYERS >= 4:
        do_ffn0()
    if N_SUBLAYERS >= 5:
        emit_barrier(p)
        emit_ln(p, g, xd)
        emit_lightning(p, g, xd, Wq=w["L1A_Wq"], Wk=w["L1A_Wk"], Wv=w["L1A_Wv"],
                       Wo=w["L1A_Wo"], gamma=w["L1A_gamma"], beta=w["L1A_beta"],
                       Wg1=w["L1A_Wg1"], Wg2=w["L1A_Wg2"])

    def do_ffn1(first):
        emit_barrier(p)
        lnst = emit_ln(p, g, xd, stats_only=True)
        emit_ffn1(p, g, xd, Wg=w["L1F_Wg"], Wu=w["L1F_Wu"], Wd=w["L1F_Wd"],
                  cch=dict(wgT=w["wgT_c"], wuT=w["wuT_c"], wdT=w["wdT_c"]),
                  lnst=lnst, first=first)

    if N_SUBLAYERS >= 6:
        do_ffn1(first=True)
    if N_SUBLAYERS >= 7:
        emit_barrier(p)
        emit_ln(p, g, xd)
        emit_mla(p, g, xd,
                 W={k: w[f"L1B_{k}"] for k in
                    ["wq", "bq", "wqc", "bqc", "wqr", "bqr", "wkv", "bkv",
                     "wkr", "bkr", "wkc", "bkc", "wv", "bv", "wo", "bo"]},
                 cos_d=w["c_ropecos"], sin_d=w["c_ropesin"])
    if N_SUBLAYERS >= 8:
        do_ffn1(first=False)

    @block.sync
    def _(sync):
        sync.wait_ge(p.sv.h, p.sv.n)
        sync.wait_ge(p.ss.h, p.ss.n)
        for st_ in p.dx:
            sync.wait_ge(st_.h, st_.n)
        sync.wait_ge(p.dmisc.h, p.dmisc.n)
        for i in range(NT):
            p.dmisc.start(sync, y_out[i * 128:(i + 1) * 128, :], xd[i])
        sync.wait_ge(p.dmisc.h, p.dmisc.n)



def emit_ln(p, g, xd, dq_ready=None, stats_only=False):
    nc, block = p.nc, p.block
    xt = g["xtile"]
    sm, sq = g["sm"], g["sq"]
    ss0, sv0, st0 = p.ss.n, p.sv.n, p.st.n
    SPT = 3 if stats_only else 4
    VPT = 6 if stats_only else 7
    dqv = {}

    @block.sync
    def _(sync):
        if dq_ready is not None:
            sync.wait_ge(p.dmisc.h, dq_ready)
        for i in range(NT):
            if i >= 2:
                sync.wait_ge(p.ss.h, ss0 + SPT * (i - 2) + SPT)
                sync.wait_ge(p.sv.h, sv0 + VPT * (i - 2) + 1)
            dqv[i] = p.dx[i % 2].start(sync, xt[i % 2][:], xd[i])

    for i in range(NT):
        @block.scalar
        def _(scalar, i=i):
            scalar.wait_ge(p.dx[i % 2].h, dqv[i])
            scalar.wait_ge(p.ss.h, ss0 + SPT * i)
            p.ss.inc(scalar.activation(g["sq2"][0][:], xt[i % 2][:], AF.Square,
                                       accum_out=sq[:, i:i + 1]))

        @block.vector
        def _(vector, i=i):
            vector.wait_ge(p.dx[i % 2].h, dqv[i])
            vector.wait_ge(p.sv.h, sv0 + VPT * i)
            p.sv.inc(vector.reduce_sum(sm[:, i:i + 1].unsqueeze(2),
                                       xt[i % 2][:].rearrange(
                                           "p (a b) -> p a b", a=1), AX.X))
            vector.wait_ge(p.sv.h, p.sv.n)
            p.sv.inc(vector.tensor_scalar(g["mean"][:, i:i + 1],
                                          sm[:, i:i + 1], 1.0 / D, None,
                                          ALU.mult))
            vector.wait_ge(p.ss.h, ss0 + SPT * i + 1)
            p.sv.inc(vector.tensor_scalar(g["var"][:, i:i + 1],
                                          sq[:, i:i + 1], 1.0 / D, None,
                                          ALU.mult))
            vector.wait_ge(p.sv.h, p.sv.n)
            p.sv.inc(vector.tensor_mul(sm[:, i:i + 1], g["mean"][:, i:i + 1],
                                       g["mean"][:, i:i + 1]))
            vector.wait_ge(p.sv.h, p.sv.n)
            p.sv.inc(vector.tensor_sub(g["var"][:, i:i + 1], sm[:, i:i + 1],
                                       g["var"][:, i:i + 1]))

        @block.scalar
        def _(scalar, i=i):
            scalar.wait_ge(p.sv.h, sv0 + VPT * i + 5)
            p.ss.inc(scalar.activation(g["rstd"][:, i:i + 1],
                                       g["var"][:, i:i + 1], AF.Ln, scale=-1.0,
                                       bias=g["ceps"][:]))
            scalar.wait_ge(p.ss.h, p.ss.n)
            p.ss.inc(scalar.activation(g["rstd"][:, i:i + 1],
                                       g["rstd"][:, i:i + 1], AF.Exp,
                                       scale=-0.5))

        @block.vector
        def _(vector, i=i):
            vector.wait_ge(p.ss.h, ss0 + SPT * i + 3)
            p.sv.inc(vector.scalar_tensor_tensor(
                g["nmrs"][:, i:i + 1], g["mean"][:, i:i + 1], -1.0,
                g["rstd"][:, i:i + 1], ALU.mult, ALU.mult))

        if stats_only:
            continue

        @block.scalar
        def _(scalar, i=i):
            scalar.wait_ge(p.sv.h, sv0 + VPT * i + 6)
            if i >= 2:
                scalar.wait_ge(p.st.h, st0 + i - 1)
            p.ss.inc(scalar.activation(g["xn_tm"][:, i % 2, :], xt[i % 2][:],
                                       AF.Identity, bias=g["nmrs"][:, i:i + 1],
                                       scale=g["rstd"][:, i:i + 1]))

        @block.tensor
        def _(tensor, i=i):
            tensor.wait_ge(p.ss.h, ss0 + SPT * i + 4)
            if i >= 1:
                tensor.wait_ge(p.sv.h, sv0 + 7 * i)
            ins = None
            for c in range(NCH):
                ins = nc.tensor.transpose(g["ps_tp"][:, c * 128:(c + 1) * 128],
                                          g["xn_tm"][:, i % 2, c * 128:(c + 1) * 128],
                                          g["ident"][:])
            p.st.inc(ins)

        @block.vector
        def _(vector, i=i):
            vector.wait_ge(p.st.h, st0 + i + 1)
            p.sv.inc(vector.tensor_copy(
                g["xn_fm"][:, :, i * 128:(i + 1) * 128],
                g["ps_tp"][:].rearrange("p (c t) -> p c t", c=NCH)))
    if stats_only:
        assert p.ss.n == ss0 + 3 * NT and p.sv.n == sv0 + 6 * NT
        return dict(ss_sq=p.ss.n, sv_red16=p.sv.n, sv_stats=p.sv.n)
    assert p.ss.n == ss0 + 4 * NT and p.sv.n == sv0 + 7 * NT
    p.sv_xn = p.sv.n


def emit_proj_tm(p, g, wslot, dst_fn, src_fn, w_ready, add_bias=False,
                 bias_off=0, sv_pre=0, b_ready=0):
    nc, block = p.nc, p.block
    st0, sv0 = p.st.n, p.sv.n

    @block.tensor
    def _(tensor):
        wwait(p, tensor, w_ready, wslot)
        if add_bias and b_ready:
            tensor.wait_ge(p.dw[1].h, b_ready)
        tensor.wait_ge(p.sv.h, max(p.sv_xn, sv0, sv_pre))
        for ih in range(NT * 2):
            i, half = divmod(ih, 2)
            pb = g["ps_s"][ih % 2]
            if ih >= 2:
                tensor.wait_ge(p.sv.h, sv0 + ih - 1)
            ins = None
            for c in range(NCH):
                ins = nc.tensor.matmul(pb[:],
                                       g["xn_fm"][:, c, i * 128:(i + 1) * 128],
                                       g["wbuf"][wslot][:, c, half * 512:(half + 1) * 512],
                                       start=(c == 0),
                                       stop=(c == NCH - 1 and not add_bias))
            if add_bias:
                ins = nc.tensor.matmul(pb[:], g["onescol"][:],
                                       g["biasrow"][:, bias_off + half * 512:
                                                    bias_off + (half + 1) * 512],
                                       start=False, stop=True)
            p.st.inc(ins)

    @block.vector
    def _(vector):
        for ih in range(NT * 2):
            i, half = divmod(ih, 2)
            vector.wait_ge(p.st.h, st0 + ih + 1)
            p.sv.inc(vector.tensor_copy(dst_fn(i, half), src_fn(g["ps_s"][ih % 2], i, half)))
    return p.st.n, p.sv.n


def emit_cbound(p, g, src_tm, col, sv_ready=0):
    nc, block = p.nc, p.block
    ss0, sv0 = p.ss.n, p.sv.n

    @block.scalar
    def _(scalar):
        scalar.wait_ge(p.sv.h, max(sv_ready, sv0))
        for i in range(NT):
            if i >= 1:
                scalar.wait_ge(p.sv.h, sv0 + 1 + 2 * i)
            p.ss.inc(scalar.activation(g["sq2"][i % 2][:], src_tm[:, i, :], AF.Square))

    @block.vector
    def _(vector):
        p.sv.inc(vector.memset(g["hmax"][:], 0.0))
        for i in range(NT):
            vector.wait_ge(p.ss.h, ss0 + i + 1)
            p.sv.inc(vector.tensor_reduce(
                g["segr"][:].unsqueeze(2),
                g["sq2"][i % 2][:].rearrange("p (h d) -> p h d", d=HD), AX.X, ALU.add))
            vector.wait_ge(p.sv.h, p.sv.n)
            p.sv.inc(vector.tensor_max(g["hmax"][:], g["hmax"][:], g["segr"][:]))
            vector.wait_ge(p.sv.h, p.sv.n)
        p.sv.inc(vector.reduce_max(g["red1"][:].unsqueeze(2),
                                   g["hmax"][:].rearrange("p (a b) -> p a b", a=1), AX.X))
        vector.wait_ge(p.sv.h, p.sv.n)
        p.sv.inc(vector.tensor_copy(g["redbf"][:], g["red1"][:]))
    sv_red = p.sv.n
    st0 = p.st.n

    @block.tensor
    def _(tensor):
        tensor.wait_ge(p.sv.h, sv_red)
        p.st.inc(nc.tensor.matmul(g["ps_s"][0][0:1, 0:128], g["redbf"][:],
                                  g["ident"][:], start=True, stop=True))
    st_t = p.st.n

    @block.vector
    def _(vector):
        vector.wait_ge(p.st.h, st_t)
        p.sv.inc(vector.tensor_copy(g["redT"][:], g["ps_s"][0][0:1, 0:128]))
        vector.wait_ge(p.sv.h, p.sv.n)
        p.sv.inc(vector.reduce_max(g["csc"][:, col:col + 1].unsqueeze(2),
                                   g["redT"][:].rearrange("p (a b) -> p a b", a=1), AX.X))
    return p.sv.n


def emit_mha(p, g, xd, wq_src, wk_src, wv_src, wo_src, qkvb_src, ob_src,
             scrambled):
    nc, block = p.nc, p.block
    ar = p.arena
    ar.reset()
    g["lall"] = ar.f32([128, S])
    g["lrec"] = g["lall"]
    g["lrow_bf"] = ar.bf16([128, 1024])
    g["q_tm"] = ar.bf16([128, NT, D])
    g["k_tm"] = ar.bf16([128, NT, D])
    g["v_tm"] = ar.bf16([128, NT, NH, HD + 1])
    g["qk_fm"] = ar.bf16([128, 2, S])

    @block.vector
    def _(vector):
        p.sv.inc(vector.memset(g["v_tm"][:], 1.0))
    sv_vm = p.sv.n

    @block.gpsimd
    def _(gpsimd):
        wq_rdy = p.load_par(gpsimd, g["wbuf"][0][:], wq_src())
        g["_obr"] = p.dw[1].start(gpsimd,
                                  g["biasrow"][:, 3 * D:4 * D].unsqueeze(1),
                                  ob_src())
        qb_rdy = 0
        if qkvb_src is not None:
            qb_rdy = p.dw[1].start(gpsimd, g["biasrow"][:, 0:3 * D].unsqueeze(1),
                                   qkvb_src())
        wk_rdy = p.load_par(gpsimd, g["wbuf"][1][:], wk_src())
        g["_wrdy"] = (wq_rdy, wk_rdy, qb_rdy)
    wq_rdy, wk_rdy, qb_rdy = g.pop("_wrdy")
    has_qkvb = qkvb_src is not None

    st_q, _ = emit_proj_tm(p, g, 0,
                           lambda i, h: g["q_tm"][:, i, h * 512:(h + 1) * 512],
                           lambda pb, i, h: pb[:], wq_rdy,
                           add_bias=has_qkvb, bias_off=0, b_ready=qb_rdy)

    @block.gpsimd
    def _(gpsimd):
        gpsimd.wait_ge(p.st.h, st_q)
        g["_wv"] = p.load_par(gpsimd, g["wbuf"][0][:], wv_src())
    wv_rdy = g.pop("_wv")

    st_k, _ = emit_proj_tm(p, g, 1,
                           lambda i, h: g["k_tm"][:, i, h * 512:(h + 1) * 512],
                           lambda pb, i, h: pb[:], wk_rdy,
                           add_bias=has_qkvb, bias_off=D)

    @block.gpsimd
    def _(gpsimd):
        gpsimd.wait_ge(p.st.h, st_k)
        g["_wo"] = p.load_par(gpsimd, g["wbuf"][1][:], wo_src())
    wo_rdy = g.pop("_wo") + [(p.dw[1].h, g.pop("_obr"))]

    st_v, sv_v = emit_proj_tm(
        p, g, 0,
        lambda i, h: g["v_tm"][:, i, h * 8:(h + 1) * 8, 0:HD],
        lambda pb, i, h: pb[:].rearrange("p (a b) -> p a b", a=8), wv_rdy,
        add_bias=has_qkvb, bias_off=2 * D, sv_pre=sv_vm, b_ready=qb_rdy)

    wos_rdy = None
    if scrambled:
        @block.gpsimd
        def _(gpsimd):
            gpsimd.wait_ge(p.st.h, st_v)
            wov = wo_src().rearrange("p c o -> p c o")
            a_ = p.dws[0].start(gpsimd, g["wbuf"][0][0:64, :, :],
                                wov[64:128, :, :])
            b_ = p.dws[1].start(gpsimd, g["wbuf"][0][64:128, :, :],
                                wov[0:64, :, :])
            g["_wos"] = [(p.dws[0].h, a_), (p.dws[1].h, b_)]
        wos_rdy = g.pop("_wos")

    emit_attn_core(p, g, xd, wo_rdy, wos_rdy, scrambled, sv_qkv=sv_v)


def emit_attn_core(p, g, xd, wo_rdy, wos_rdy, scrambled, sv_qkv):
    nc, block = p.nc, p.block

    emit_cbound(p, g, g["q_tm"], 0, sv_ready=sv_qkv)
    sv_ck = emit_cbound(p, g, g["k_tm"], 1, sv_ready=sv_qkv)

    @block.vector
    def _(vector):
        vector.wait_ge(p.sv.h, sv_ck)
        p.sv.inc(vector.tensor_tensor(g["csc"][:, 2:3], g["csc"][:, 0:1],
                                      g["csc"][:, 1:2], ALU.mult))
    sv_cm = p.sv.n

    @block.scalar
    def _(scalar):
        scalar.wait_ge(p.sv.h, sv_cm)
        p.ss.inc(scalar.activation(g["csc"][:, 3:4], g["csc"][:, 2:3], AF.Ln))
        scalar.wait_ge(p.ss.h, p.ss.n)
        p.ss.inc(scalar.activation(g["csc"][:, 4:5], g["csc"][:, 3:4], AF.Exp, scale=0.5))
    ss_cs = p.ss.n

    @block.vector
    def _(vector):
        vector.wait_ge(p.ss.h, ss_cs)
        p.sv.inc(vector.tensor_scalar(g["csc"][:, 5:6], g["csc"][:, 4:5], -1.02 / 8.0,
                                      None, ALU.mult))
        vector.wait_ge(p.sv.h, p.sv.n)
        p.sv.inc(vector.tensor_copy(g["negc8b"][:], g["csc"][:, 5:6]))
    sv_cb = p.sv.n
    st_cb0 = p.st.n

    @block.tensor
    def _(tensor):
        tensor.wait_ge(p.sv.h, sv_cb)
        p.st.inc(nc.tensor.matmul(g["ps_s"][1][0:128, 0:1], g["onescol"][:],
                                  g["negc8b"][:], start=True, stop=True))
    st_cb = p.st.n

    @block.vector
    def _(vector):
        vector.wait_ge(p.st.h, st_cb)
        p.sv.inc(vector.tensor_copy(g["negc8"][:], g["ps_s"][1][0:128, 0:1]))
    sv_c = p.sv.n

    def head_items(qblk_range=4):
        for qblk in range(qblk_range):
            jmax = 4 * qblk + 3
            for j in range(jmax + 1):
                yield qblk, j, max(qblk * 512, j * 128), j == 0, j == jmax, j >= 4 * qblk

    st_c = p.st.n
    ss_c = p.ss.n
    sv_cn = p.sv.n
    dq_c = p.dlg.n
    st_tp = {}; st_sc = {}; st_pv = {}
    ss_exp = {}
    sv_tpe = {}; sv_tri = {}; sv_oev = {}; sv_lev = {}
    dq_lall = {}
    for pr in range(8):
        for gi in range(4):
            st_c += 1; st_tp[(pr, gi)] = st_c
        for hh in range(2):
            n = 0
            for qblk, j, qlo, first, last, diag in head_items():
                st_c += 1; st_sc[(pr, hh, qblk, j)] = st_c
                st_c += 1; st_pv[(pr, hh, qblk, j)] = st_c
    for pr in range(8):
        for hh in range(2):
            for qblk, j, qlo, first, last, diag in head_items():
                ss_c += 1; ss_exp[(pr, hh, qblk, j)] = ss_c
    for pr in range(8):
        for gi in range(4):
            sv_cn += 1; sv_tpe[(pr, gi)] = sv_cn
        for hh in range(2):
            for qblk, j, qlo, first, last, diag in head_items():
                if diag:
                    sv_cn += 1; sv_tri[(pr, hh, qblk, j)] = sv_cn
            sv_cn += 1; sv_oev[(pr, hh)] = sv_cn
            sv_cn += 1; sv_lev[(pr, hh)] = sv_cn
    for pr in range(8):
        for hh in range(2):
            dq_c += 16; dq_lall[(pr, hh)] = dq_c

    lin = {}
    cnt = 0
    for pr in range(8):
        for hh in range(2):
            for qblk, j, qlo, first, last, diag in head_items():
                lin[(pr, hh, qblk, j)] = cnt
                cnt += 1
    lin_rev = {v: k for k, v in lin.items()}

    pslots = [g["ps_s"][0], g["ps_s"][1], g["ps_e"]]

    @block.tensor
    def _(tensor):
        tensor.wait_ge(p.sv.h, sv_c)
        for pr in range(8):
            for gi in range(4):
                half, b = divmod(gi, 2)
                src = g["q_tm"] if half == 0 else g["k_tm"]
                if (pr, gi) != (0, 0):
                    prevk = (pr, gi - 1) if gi else (pr - 1, 3)
                    tensor.wait_ge(p.sv.h, sv_tpe[prevk])
                ins = None
                for t in range(8):
                    ins = nc.tensor.transpose(
                        g["ps_tp"][:, t * 128:(t + 1) * 128],
                        src[:, b * 8 + t, pr * 128:(pr + 1) * 128],
                        g["ident"][:])
                p.st.inc(ins)
            for hh in range(2):
                h = 2 * pr + hh
                base = hh * 64
                first_of_head = True
                for qblk, j, qlo, first, last, diag in head_items():
                    n = lin[(pr, hh, qblk, j)]
                    jp = n % 3
                    wdt = qblk * 512 + 512 - qlo
                    if first_of_head:
                        tensor.wait_ge(p.sv.h, sv_tpe[(pr, 3)])
                        first_of_head = False
                    if n >= 3:
                        tensor.wait_ge(p.ss.h, ss_exp[lin_rev[n - 3]])
                    p.st.inc(nc.tensor.matmul(
                        pslots[jp][:, 0:wdt],
                        g["qk_fm"][base:base + 64, 1, j * 128:(j + 1) * 128],
                        g["qk_fm"][base:base + 64, 0, qlo:qblk * 512 + 512],
                        start=True, stop=True))
                    if diag:
                        tensor.wait_ge(p.sv.h, sv_tri[(pr, hh, qblk, j)])
                    else:
                        tensor.wait_ge(p.ss.h, ss_exp[(pr, hh, qblk, j)])
                    if first and (pr, hh) != (0, 0):
                        prevh = (pr, hh - 1) if hh else (pr - 1, 1)
                        tensor.wait_ge(p.sv.h, sv_lev[prevh])
                    p.st.inc(nc.tensor.matmul(
                        g["ps_pv"][0:65, qlo:qblk * 512 + 512],
                        g["v_tm"][:, j, h, :],
                        g["ptb"][jp][:, 0:wdt],
                        start=first, stop=last))

    @block.scalar
    def _(scalar):
        scalar.wait_ge(p.sv.h, sv_c)
        for pr in range(8):
            for hh in range(2):
                for qblk, j, qlo, first, last, diag in head_items():
                    n = lin[(pr, hh, qblk, j)]
                    jp = n % 3
                    wdt = qblk * 512 + 512 - qlo
                    scalar.wait_ge(p.st.h, st_sc[(pr, hh, qblk, j)])
                    p.ss.inc(scalar.activation(g["ptb"][jp][:, 0:wdt],
                                               pslots[jp][:, 0:wdt], AF.Exp,
                                               scale=0.125, bias=g["negc8"][:]))
    assert p.ss.n == ss_c

    @block.vector
    def _(vector):
        for pr in range(8):
            for gi in range(4):
                half, b = divmod(gi, 2)
                vector.wait_ge(p.st.h, st_tp[(pr, gi)])
                p.sv.inc(vector.tensor_copy(
                    g["qk_fm"][:, half, b * 1024:(b + 1) * 1024], g["ps_tp"][:]))
            for hh in range(2):
                h = 2 * pr + hh
                base = hh * 64
                for qblk, j, qlo, first, last, diag in head_items():
                    if not diag:
                        continue
                    n = lin[(pr, hh, qblk, j)]
                    jp = n % 3
                    vector.wait_ge(p.ss.h, ss_exp[(pr, hh, qblk, j)])
                    p.sv.inc(vector.tensor_tensor(g["ptb"][jp][:, 0:128],
                                                  g["ptb"][jp][:, 0:128],
                                                  g["triT"][:], ALU.mult))
                vector.wait_ge(p.st.h, st_pv[(pr, hh, 3, 15)])
                p.sv.inc(vector.tensor_copy(g["attn_fm"][base:base + 64, pr, :],
                                            g["ps_pv"][0:64, :]))
                hidx = 2 * pr + hh
                if hidx >= 1:
                    vector.wait_ge(p.dlg.h, dq_lall[lin_rev_h(hidx - 1)])
                p.sv.inc(vector.tensor_copy(g["lall"][32:33, :],
                                            g["ps_pv"][64:65, :]))
    assert p.sv.n == sv_cn

    @block.sync
    def _(sync):
        for pr in range(8):
            for hh in range(2):
                h = 2 * pr + hh
                sync.wait_ge(p.sv.h, sv_lev[(pr, hh)])
                p.dlg.start(sync, g["lall"][h:h + 1, :], g["lall"][32:33, :])
    assert p.dlg.n == dq_c

    ss_lr0 = p.ss.n

    @block.scalar
    def _(scalar):
        scalar.wait_ge(p.dlg.h, dq_c)
        p.ss.inc(scalar.activation(g["lrec"][0:NH, :], g["lall"][0:NH, :], AF.Ln))
        scalar.wait_ge(p.ss.h, p.ss.n)
        p.ss.inc(scalar.activation(g["lrec"][0:NH, :], g["lrec"][0:NH, :], AF.Exp,
                                   scale=-1.0))
    ss_lr = p.ss.n

    st_n0 = p.st.n
    sv_n0 = p.sv.n
    dlr_v = {}

    @block.gpsimd
    def _(gpsimd):
        gpsimd.wait_ge(p.ss.h, ss_lr)
        for i in range(NH * 4):
            h, qblk = divmod(i, 4)
            if i >= 2:
                gpsimd.wait_ge(p.st.h, st_n0 + i - 1)
            dlr_v[i] = p.dlr[i % 2].start(
                gpsimd, g["lrow_bf"][0:1, (i % 2) * 512:(i % 2) * 512 + 512],
                g["lrec"][h:h + 1, qblk * 512:(qblk + 1) * 512])

    @block.tensor
    def _(tensor):
        for i in range(NH * 4):
            h, qblk = divmod(i, 4)
            tensor.wait_ge(p.dlr[i % 2].h, dlr_v[i])
            if i >= 2:
                tensor.wait_ge(p.sv.h, sv_n0 + i - 1)
            p.st.inc(nc.tensor.matmul(g["ps_s"][i % 2][0:64, :],
                                      g["onescol"][:, 0:64],
                                      g["lrow_bf"][0:1, (i % 2) * 512:(i % 2) * 512 + 512],
                                      start=True, stop=True))

    @block.vector
    def _(vector):
        for i in range(NH * 4):
            h, qblk = divmod(i, 4)
            pr, hh = divmod(h, 2)
            base = hh * 64
            vector.wait_ge(p.st.h, st_n0 + i + 1)
            sl = g["attn_fm"][base:base + 64, pr, qblk * 512:(qblk + 1) * 512]
            p.sv.inc(vector.tensor_tensor(sl, sl, g["ps_s"][i % 2][0:64, :], ALU.mult))
    sv_norm = p.sv.n

    st_o0 = p.st.n
    sv_o0 = p.sv.n

    @block.tensor
    def _(tensor):
        wwait(p, tensor, wo_rdy, 1)
        if scrambled:
            wwait(p, tensor, wos_rdy, 0)
        tensor.wait_ge(p.sv.h, sv_norm)
        for ih in range(NT * 2):
            i, half = divmod(ih, 2)
            pb = g["ps_s"][ih % 2]
            if ih >= 2:
                tensor.wait_ge(p.sv.h, sv_o0 + ih - 1)
            if scrambled:
                hh, pr = i % 2, i // 2
                for c in range(NCH):
                    for bpar in range(2):
                        b0 = 2 * c + bpar
                        wsrc = g["wbuf"][1] if bpar == hh else g["wbuf"][0]
                        nc.tensor.matmul(
                            pb[:],
                            g["attn_fm"][hh * 64:hh * 64 + 64, pr, b0::16],
                            wsrc[hh * 64:(hh + 1) * 64, c,
                                 half * 512:(half + 1) * 512],
                            start=(c == 0 and bpar == 0), stop=False)
            else:
                for c in range(NCH):
                    nc.tensor.matmul(pb[:],
                                     g["attn_fm"][:, c, i * 128:(i + 1) * 128],
                                     g["wbuf"][1][:, c, half * 512:(half + 1) * 512],
                                     start=(c == 0), stop=False)
            p.st.inc(nc.tensor.matmul(pb[:], g["onescol"][:],
                                      g["biasrow"][:, 3 * D + half * 512:
                                                   3 * D + (half + 1) * 512],
                                      start=False, stop=True))

    ld_v = {}
    ss_res0 = p.ss.n

    @block.sync
    def _(sync):
        sync.wait_ge(p.ss.h, ss_res0)
        for i in range(NT):
            ld_v[i] = p.dx[i % 2].start(sync, g["xtile"][i % 2][:], xd[i])
            if i >= 1:
                sync.wait_ge(p.sv.h, sv_o0 + 2 * i)
                p.dx[(i - 1) % 2].start(sync, xd[i - 1], g["xtile"][(i - 1) % 2][:])
        sync.wait_ge(p.sv.h, sv_o0 + 2 * NT)
        p.dx[(NT - 1) % 2].start(sync, xd[NT - 1], g["xtile"][(NT - 1) % 2][:])

    @block.vector
    def _(vector):
        for ih in range(NT * 2):
            i, half = divmod(ih, 2)
            vector.wait_ge(p.st.h, st_o0 + ih + 1)
            vector.wait_ge(p.dx[i % 2].h, ld_v[i])
            sl = g["xtile"][i % 2][:, half * 512:(half + 1) * 512]
            p.sv.inc(vector.tensor_add(sl, sl, g["ps_s"][ih % 2][:]))


def emit_mla(p, g, xd, W, cos_d, sin_d):
    nc, block = p.nc, p.block
    ar = p.arena
    ar.reset()

    g["q_tm"] = ar.bf16([128, NT, D])
    g["k_tm"] = ar.bf16([128, NT, D])
    g["v_tm"] = ar.bf16([128, NT, NH, HD + 1])
    U = ar.bf16([128, 9744])
    g["qk_fm"] = U[:, 0:4096].rearrange("p (c t) -> p c t", c=2)
    g["lall"] = U[:, 4096:8192].bitcast(F32)
    g["lrec"] = g["lall"]
    g["lrow_bf"] = U[:, 8192:9216]
    c_fm = g["qk_fm"]
    wsl = [U[:, 4096:6144], U[:, 6144:8192]]
    ropebuf = U[:, 8192:9216].bitcast(F32).rearrange(
        "p (s c t) -> p s c t", s=2, c=2)
    rp = U[:, 9216:9472].bitcast(F32)
    rtmp = U[:, 9472:9728].bitcast(F32)
    bcol = U[:, 9728:9744].bitcast(F32)

    wv8 = lambda s: wsl[s].rearrange("p (c o) -> p c o", c=8)
    wv2 = lambda s: wsl[s].rearrange("p (c o) -> p c o", c=2)
    rr = lambda t: t.rearrange("(c p) o -> p c o", p=128)
    rowv = lambda t: t.rearrange("(a d) -> a d", a=1)

    stg = {}

    @block.gpsimd
    def _(gpsimd):
        stg["wqr"] = p.dw[0].start(gpsimd, wv8(0)[:, :, 0:128], rr(W["wqr"]))
        stg["wkr"] = p.dw[0].start(gpsimd, wv8(0)[:, :, 128:256], rr(W["wkr"]))
        stg["wqc"] = p.dw[0].start(gpsimd, wv2(1)[:, :, 0:896], rr(W["wqc"]))
        for oc in range(2):
            p.dw[1].start(gpsimd, bcol[:, oc:oc + 1],
                          W["bq"][oc * 128:(oc + 1) * 128].rearrange(
                              "(p a) -> p a", a=1))
            p.dw[1].start(gpsimd, bcol[:, 2 + oc:3 + oc],
                          W["bkv"][oc * 128:(oc + 1) * 128].rearrange(
                              "(p a) -> p a", a=1))
        p.dw[1].start(gpsimd, g["biasrow"][:, 0:896].unsqueeze(1), rowv(W["bqc"]))
        p.dw[1].start(gpsimd, g["biasrow"][:, 896:1792].unsqueeze(1), rowv(W["bkc"]))
        p.dw[1].start(gpsimd, g["biasrow"][:, 1792:2816].unsqueeze(1), rowv(W["bv"]))
        p.dw[1].start(gpsimd, g["biasrow"][:, 2816:2944].unsqueeze(1), rowv(W["bqr"]))
        p.dw[1].start(gpsimd, g["biasrow"][:, 2944:3072].unsqueeze(1), rowv(W["bkr"]))
        p.dw[1].start(gpsimd, g["biasrow"][:, 3 * D:4 * D].unsqueeze(1), rowv(W["bo"]))
        stg["b"] = p.dw[1].n
        stg["wo"] = p.load_par(gpsimd, g["wbuf"][1][:], rr(W["wo"]))

    def stage_load(name, dst, src, st_free):
        @block.gpsimd
        def _(gpsimd):
            gpsimd.wait_ge(p.st.h, st_free)
            stg[name] = p.dw[0].start(gpsimd, dst, src)

    def rope_proj(dst_tm, wview, w_rdy, br_off):
        st0, sv0, ss0 = p.st.n, p.sv.n, p.ss.n
        dlrv = {}

        @block.gpsimd
        def _(gpsimd):
            for i in range(NT):
                if i >= 2:
                    gpsimd.wait_ge(p.sv.h, sv0 + 6 * (i - 1))
                else:
                    gpsimd.wait_ge(p.sv.h, sv0)
                p.dlr[i % 2].start(gpsimd, ropebuf[:, i % 2, 0, :],
                                   cos_d[i * 128:(i + 1) * 128, :])
                dlrv[i] = p.dlr[i % 2].start(gpsimd, ropebuf[:, i % 2, 1, :],
                                             sin_d[i * 128:(i + 1) * 128, :])

        @block.tensor
        def _(tensor):
            tensor.wait_ge(p.dw[0].h, w_rdy)
            tensor.wait_ge(p.dw[1].h, stg["b"])
            tensor.wait_ge(p.sv.h, p.sv_xn)
            for i in range(NT):
                if i >= 1:
                    tensor.wait_ge(p.sv.h, sv0 + 6 * i - 1)
                else:
                    tensor.wait_ge(p.sv.h, sv0)
                for c in range(NCH):
                    nc.tensor.matmul(g["ps_e"][:, 0:128],
                                     g["xn_fm"][:, c, i * 128:(i + 1) * 128],
                                     wview[:, c, :],
                                     start=(c == 0), stop=False)
                p.st.inc(nc.tensor.matmul(g["ps_e"][:, 0:128], g["onescol"][:],
                                          g["biasrow"][:, br_off:br_off + 128],
                                          start=False, stop=True))

        @block.vector
        def _(vector):
            for i in range(NT):
                s = i % 2
                vector.wait_ge(p.st.h, st0 + i + 1)
                vector.wait_ge(p.dlr[s].h, dlrv[i])
                vector.wait_ge(p.ss.h, ss0 + i)
                vector.wait_ge(p.sv.h, p.sv.n)
                p.sv.inc(vector.tensor_mul(rp[:, 0:64], g["ps_e"][:, 0:64],
                                           ropebuf[:, s, 0, 0:64]))
                p.sv.inc(vector.tensor_mul(rtmp[:, 0:64], g["ps_e"][:, 64:128],
                                           ropebuf[:, s, 1, 0:64]))
                vector.wait_ge(p.sv.h, p.sv.n)
                p.sv.inc(vector.tensor_sub(rp[:, 0:64], rp[:, 0:64],
                                           rtmp[:, 0:64]))
                p.sv.inc(vector.tensor_mul(rp[:, 64:128], g["ps_e"][:, 64:128],
                                           ropebuf[:, s, 0, 64:128]))
                p.sv.inc(vector.tensor_mul(rtmp[:, 64:128], g["ps_e"][:, 0:64],
                                           ropebuf[:, s, 1, 64:128]))
                vector.wait_ge(p.sv.h, p.sv.n)
                p.sv.inc(vector.tensor_add(rp[:, 64:128], rp[:, 64:128],
                                           rtmp[:, 64:128]))

        @block.scalar
        def _(scalar):
            for i in range(NT):
                scalar.wait_ge(p.sv.h, sv0 + 6 * (i + 1))
                p.ss.inc(scalar.activation(dst_tm[:, i, 896:1024], rp[:],
                                           AF.Gelu))
        return p.ss.n

    def fm_proj_gelu(wview, w_rdy, bc_idx, st_war):
        st0, ss0 = p.st.n, p.ss.n

        @block.tensor
        def _(tensor):
            tensor.wait_ge(p.dw[0].h, w_rdy)
            tensor.wait_ge(p.sv.h, p.sv_xn)
            tensor.wait_ge(p.ss.h, ss0)
            for k in range(8):
                oc, sp = divmod(k, 4)
                pb = g["ps_s"][k % 2]
                if k >= 2:
                    tensor.wait_ge(p.ss.h, ss0 + k - 1)
                ins = None
                for c in range(NCH):
                    ins = nc.tensor.matmul(pb[:],
                                           wview[:, c, oc * 128:(oc + 1) * 128],
                                           g["xn_fm"][:, c, sp * 512:(sp + 1) * 512],
                                           start=(c == 0), stop=(c == NCH - 1))
                p.st.inc(ins)
        st_mm = p.st.n

        @block.scalar
        def _(scalar):
            scalar.wait_ge(p.dw[1].h, stg["b"])
            scalar.wait_ge(p.st.h, st_war)
            for k in range(8):
                oc, sp = divmod(k, 4)
                scalar.wait_ge(p.st.h, st0 + k + 1)
                p.ss.inc(scalar.activation(
                    c_fm[:, oc, sp * 512:(sp + 1) * 512], g["ps_s"][k % 2][:],
                    AF.Gelu, bias=bcol[:, bc_idx + oc:bc_idx + oc + 1]))
        return st_mm, p.ss.n

    def tm_proj_gelu(dst_tm, w_rdy, ss_cready, br_off):
        st0, ss0 = p.st.n, p.ss.n
        spans = [(0, 512), (512, 384)]

        @block.tensor
        def _(tensor):
            tensor.wait_ge(p.dw[0].h, w_rdy)
            tensor.wait_ge(p.ss.h, ss_cready)
            for n in range(NT * 2):
                i, hf = divmod(n, 2)
                so, sw = spans[hf]
                pb = g["ps_s"][n % 2]
                if n >= 2:
                    tensor.wait_ge(p.ss.h, ss0 + n - 1)
                for c in range(2):
                    nc.tensor.matmul(pb[:, 0:sw],
                                     c_fm[:, c, i * 128:(i + 1) * 128],
                                     wv2(1)[:, c, so:so + sw],
                                     start=(c == 0), stop=False)
                p.st.inc(nc.tensor.matmul(
                    pb[:, 0:sw], g["onescol"][:],
                    g["biasrow"][:, br_off + so:br_off + so + sw],
                    start=False, stop=True))
        st_mm = p.st.n

        @block.scalar
        def _(scalar):
            for n in range(NT * 2):
                i, hf = divmod(n, 2)
                so, sw = spans[hf]
                scalar.wait_ge(p.st.h, st0 + n + 1)
                p.ss.inc(scalar.activation(dst_tm[:, i, so:so + sw],
                                           g["ps_s"][n % 2][:, 0:sw], AF.Gelu))
        return st_mm, p.ss.n

    ss_qrope = rope_proj(g["q_tm"], wv8(0)[:, :, 0:128], stg["wqr"], 2816)
    ss_krope = rope_proj(g["k_tm"], wv8(0)[:, :, 128:256], stg["wkr"], 2944)
    stage_load("wq", wv8(0)[:, :, 0:256], rr(W["wq"]), p.st.n)
    st_cq, ss_cq = fm_proj_gelu(wv8(0)[:, :, 0:256], stg["wq"], 0, 0)
    st_qc, ss_qc = tm_proj_gelu(g["q_tm"], stg["wqc"], ss_cq, 0)
    stage_load("wkv", wv8(0)[:, :, 0:256], rr(W["wkv"]), st_cq)
    st_ckv, ss_ckv = fm_proj_gelu(wv8(0)[:, :, 0:256], stg["wkv"], 2, st_qc)
    stage_load("wkc", wv2(1)[:, :, 0:896], rr(W["wkc"]), st_qc)
    st_kc, ss_kc = tm_proj_gelu(g["k_tm"], stg["wkc"], ss_ckv, 896)
    stage_load("wv", wv2(1)[:, :, 0:1024], rr(W["wv"]), st_kc)

    @block.vector
    def _(vector):
        p.sv.inc(vector.memset(g["v_tm"][:], 1.0))
    sv_vm = p.sv.n

    st0v, sv0v, ss0v = p.st.n, p.sv.n, p.ss.n

    @block.tensor
    def _(tensor):
        tensor.wait_ge(p.dw[0].h, stg["wv"])
        tensor.wait_ge(p.ss.h, ss0v)
        for n in range(NT * 2):
            i, hf = divmod(n, 2)
            pb = g["ps_s"][n % 2]
            if n >= 2:
                tensor.wait_ge(p.sv.h, sv0v + n - 1)
            for c in range(2):
                nc.tensor.matmul(pb[:],
                                 c_fm[:, c, i * 128:(i + 1) * 128],
                                 wv2(1)[:, c, hf * 512:(hf + 1) * 512],
                                 start=(c == 0), stop=False)
            p.st.inc(nc.tensor.matmul(
                pb[:], g["onescol"][:],
                g["biasrow"][:, 1792 + hf * 512:1792 + (hf + 1) * 512],
                start=False, stop=True))

    @block.vector
    def _(vector):
        vector.wait_ge(p.sv.h, sv_vm)
        for n in range(NT * 2):
            i, hf = divmod(n, 2)
            vector.wait_ge(p.st.h, st0v + n + 1)
            p.sv.inc(vector.tensor_copy(
                g["v_tm"][:, i, hf * 8:(hf + 1) * 8, 0:HD],
                g["ps_s"][n % 2][:].rearrange("p (a b) -> p a b", a=8)))
    sv_v = p.sv.n

    emit_attn_core(p, g, xd, stg["wo"] + [(p.dw[1].h, stg["b"])], None,
                   scrambled=False, sv_qkv=sv_v)


def emit_barrier(p):
    block = p.block
    sv, ss, st, sg = p.sv.n, p.ss.n, p.st.n, p.sg.n

    @block.tensor
    def _(tensor):
        tensor.wait_ge(p.sv.h, sv); tensor.wait_ge(p.ss.h, ss); tensor.wait_ge(p.sg.h, sg)

    @block.vector
    def _(vector):
        vector.wait_ge(p.st.h, st); vector.wait_ge(p.ss.h, ss); vector.wait_ge(p.sg.h, sg)

    @block.scalar
    def _(scalar):
        scalar.wait_ge(p.st.h, st); scalar.wait_ge(p.sv.h, sv); scalar.wait_ge(p.sg.h, sg)

    @block.gpsimd
    def _(gpsimd):
        gpsimd.wait_ge(p.st.h, st); gpsimd.wait_ge(p.sv.h, sv); gpsimd.wait_ge(p.ss.h, ss)

    @block.sync
    def _(sync):
        sync.wait_ge(p.st.h, st); sync.wait_ge(p.sv.h, sv); sync.wait_ge(p.ss.h, ss)


def emit_ffn0(p, g, xd, W, Wp, bp, sim, temp, Wd, Wu):
    nc, block = p.nc, p.block
    ar = p.arena
    ar.reset()
    CLAMP = float(np.log(1.0 / 0.01))

    wp_nat = ar.bf16([128, 2, 1024])
    wpT = ar.bf16([128, 8, 256])
    wd_nat = ar.bf16([128, 1024])
    wdT = ar.bf16([128, 8, 16])
    wu_nat = ar.bf16([128, 8, 16])
    wuT = ar.bf16([128, 1024])
    simb = ar.bf16([128, 2, 4])
    simTf = ar.f32([128, 256])
    simsq = ar.bf16([128, 256])
    simnT = ar.bf16([128, 256])
    simn = ar.bf16([128, 2, 4])
    tmpv = ar.f32([128, 8])
    s_bc = ar.f32([128, 1])
    s_bf = ar.bf16([128, 16])
    c38 = ar.f32([128, 1])
    ssp = ar.f32([128, 1])
    rn = ar.f32([128, 1])
    sqd = ar.bf16([128, 256])
    p_fm = [ar.bf16([128, 2, 128]) for _ in range(2)]
    dall = [ar.f32([128, 16]) for _ in range(2)]
    lgc = ar.f32([128, 4])
    esum = ar.f32([128, 1])
    er = ar.f32([128, 1])
    lw = ar.f32([128, 4])
    rm = ar.f32([128, 1])
    msk = ar.f32([128, 4])
    ret = ar.f32([128, 4])
    t_scl = [ar.bf16([128, 16]) for _ in range(2)]
    t_fm = [ar.bf16([128, 128]) for _ in range(2)]
    pn_t = [ar.bf16([128, 256]) for _ in range(2)]

    @block.gpsimd
    def _(gpsimd):
        g["_w0"] = p.load_par(gpsimd, g["wbuf"][0][:],
                              W.rearrange("(c p) o -> p c o", p=128))
        p.dw[1].start(gpsimd, g["biasrow"][:, 0:256].unsqueeze(1),
                      bp.rearrange("(a d) -> a d", a=1))
        p.dw[1].start(gpsimd, wp_nat[:], Wp.rearrange("(c p) o -> p c o", p=128))
        p.dw[1].start(gpsimd, wd_nat[0:15, :], Wd)
        p.dw[1].start(gpsimd, wu_nat[:, :, 0:15],
                      Wu.rearrange("(c p) r -> p c r", p=128))
        p.dw[1].start(gpsimd, simb[:], sim.rearrange("(c p) e -> p c e", p=128))
        g["_w1"] = p.dw[1].start(gpsimd, tmpv[0:1, 0:1],
                                 temp.rearrange("(a d) -> a d", a=1))
    w0_rdy = g.pop("_w0")
    w1_rdy = g.pop("_w1")

    st0 = p.st.n
    sv0 = p.sv.n

    @block.tensor
    def _(tensor):
        wwait(p, tensor, w0_rdy, 0)
        tensor.wait_ge(p.dw[1].h, w1_rdy)
        tensor.wait_ge(p.sv.h, sv0)
        gidx = 0
        for dc in range(8):
            if gidx >= 1:
                tensor.wait_ge(p.sv.h, sv0 + 3 + gidx)
            for oc in range(8):
                ins = nc.tensor.transpose(g["ps_tp"][:, oc * 128:(oc + 1) * 128],
                                          g["wbuf"][0][:, oc, dc * 128:(dc + 1) * 128],
                                          g["ident"][:])
            p.st.inc(ins); gidx += 1
        for dc in range(8):
            tensor.wait_ge(p.sv.h, sv0 + 3 + gidx)
            for oc in range(2):
                ins = nc.tensor.transpose(g["ps_tp"][:, oc * 128:(oc + 1) * 128],
                                          wp_nat[:, oc, dc * 128:(dc + 1) * 128],
                                          g["ident"][:])
            p.st.inc(ins); gidx += 1
        for dc in range(8):
            tensor.wait_ge(p.sv.h, sv0 + 3 + gidx)
            ins = nc.tensor.transpose(g["ps_tp"][:, 0:15],
                                      wd_nat[0:15, dc * 128:(dc + 1) * 128],
                                      g["ident"][0:15, 0:15])
            p.st.inc(ins); gidx += 1
        for dc in range(8):
            tensor.wait_ge(p.sv.h, sv0 + 3 + gidx)
            ins = nc.tensor.transpose(g["ps_tp"][0:15, 0:128],
                                      wu_nat[:, dc, 0:15], g["ident"][:])
            p.st.inc(ins); gidx += 1
        tensor.wait_ge(p.sv.h, sv0 + 3 + gidx)
        for ch in range(2):
            ins = nc.tensor.transpose(g["ps_tp"][0:4, ch * 128:(ch + 1) * 128],
                                      simb[:, ch, :], g["ident"][:])
        p.st.inc(ins); gidx += 1

    @block.vector
    def _(vector):
        p.sv.inc(vector.memset(c38[:], 1e-38))
        p.sv.inc(vector.memset(t_scl[0][:], 0.0))
        p.sv.inc(vector.memset(t_scl[1][:], 0.0))
        gidx = 3
        for dc in range(8):
            vector.wait_ge(p.st.h, st0 + (gidx - 3) + 1)
            p.sv.inc(vector.tensor_copy(g["wbuf"][1][:, dc, :], g["ps_tp"][:]))
            gidx += 1
        for dc in range(8):
            vector.wait_ge(p.st.h, st0 + (gidx - 3) + 1)
            p.sv.inc(vector.tensor_copy(wpT[:, dc, :], g["ps_tp"][:, 0:256]))
            gidx += 1
        for dc in range(8):
            vector.wait_ge(p.st.h, st0 + (gidx - 3) + 1)
            p.sv.inc(vector.tensor_copy(wdT[:, dc, 0:15], g["ps_tp"][:, 0:15]))
            gidx += 1
        for dc in range(8):
            vector.wait_ge(p.st.h, st0 + (gidx - 3) + 1)
            p.sv.inc(vector.tensor_copy(wuT[0:15, dc * 128:(dc + 1) * 128],
                                        g["ps_tp"][0:15, 0:128]))
            gidx += 1
        vector.wait_ge(p.st.h, st0 + (gidx - 3) + 1)
        p.sv.inc(vector.tensor_copy(simTf[0:4, :], g["ps_tp"][0:4, 0:256]))
    sv_wprep = p.sv.n

    ss0 = p.ss.n

    @block.scalar
    def _(scalar):
        scalar.wait_ge(p.sv.h, sv_wprep)
        p.ss.inc(scalar.activation(simsq[0:4, :], simTf[0:4, :], AF.Square,
                                   accum_out=tmpv[0:4, 3:4]))
        scalar.wait_ge(p.ss.h, p.ss.n)
        p.ss.inc(scalar.activation(tmpv[0:4, 4:5], tmpv[0:4, 3:4], AF.Ln,
                                   bias=c38[0:4, :]))
        scalar.wait_ge(p.ss.h, p.ss.n)
        p.ss.inc(scalar.activation(tmpv[0:4, 5:6], tmpv[0:4, 4:5], AF.Exp, scale=-0.5))
        p.ss.inc(scalar.activation(tmpv[0:1, 1:2], tmpv[0:1, 0:1], AF.Identity))
    ss_sim = p.ss.n

    @block.vector
    def _(vector):
        vector.wait_ge(p.ss.h, ss_sim)
        p.sv.inc(vector.tensor_scalar(tmpv[0:4, 6:7], tmpv[0:4, 5:6], 1e12, None,
                                      ALU.min))
        p.sv.inc(vector.tensor_scalar(tmpv[0:1, 2:3], tmpv[0:1, 1:2], CLAMP, None,
                                      ALU.min))
    sv_rn4 = p.sv.n

    @block.scalar
    def _(scalar):
        scalar.wait_ge(p.sv.h, sv_rn4)
        p.ss.inc(scalar.activation(simnT[0:4, :], simTf[0:4, :], AF.Identity,
                                   scale=tmpv[0:4, 6:7]))
        p.ss.inc(scalar.activation(tmpv[0:1, 3:4], tmpv[0:1, 2:3], AF.Exp))
    ss_simn = p.ss.n

    @block.vector
    def _(vector):
        vector.wait_ge(p.ss.h, ss_simn)
        p.sv.inc(vector.tensor_copy(s_bf[0:1, 0:1], tmpv[0:1, 3:4]))
    sv_sbf = p.sv.n

    st1 = p.st.n

    @block.tensor
    def _(tensor):
        tensor.wait_ge(p.sv.h, sv_sbf)
        for ch in range(2):
            nc.tensor.transpose(g["ps_tp"][:, ch * 128:ch * 128 + 4],
                                simnT[0:4, ch * 128:(ch + 1) * 128],
                                g["ident"][0:4, 0:4])
        p.st.inc(nc.tensor.matmul(g["ps_e"][0:128, 0:1], g["onescol"][:],
                                  s_bf[0:1, 0:1], start=True, stop=True))
    st_simn = p.st.n

    @block.vector
    def _(vector):
        vector.wait_ge(p.st.h, st_simn)
        p.sv.inc(vector.tensor_copy(simn[:],
                                    g["ps_tp"][:, 0:256].rearrange(
                                        "p (c t) -> p c t", c=2)[:, :, 0:4]))
        p.sv.inc(vector.tensor_copy(s_bc[:], g["ps_e"][0:128, 0:1]))
    sv_router_w = p.sv.n

    stt = {}; sst = {}; svt = {}
    st_c = p.st.n; ss_c = p.ss.n; sv_c = p.sv.n
    for i in range(NT):
        st_c += 1; stt[("proj", i)] = st_c
        st_c += 1; stt[("dall", i)] = st_c
        st_c += 1; stt[("pnT", i)] = st_c
        st_c += 1; stt[("lg", i)] = st_c
        st_c += 1; stt[("tsT", i)] = st_c
        st_c += 2; stt[("base", i)] = st_c
    for i in range(NT):
        ss_c += 1; sst[("sq", i)] = ss_c
        ss_c += 1; sst[("ln", i)] = ss_c
        ss_c += 1; sst[("ex", i)] = ss_c
        ss_c += 1; sst[("pn", i)] = ss_c
        ss_c += 1; sst[("lw", i)] = ss_c
    for i in range(NT):
        sv_c += 1; svt[("min", i)] = sv_c
        sv_c += 1; svt[("dev", i)] = sv_c
        sv_c += 1; svt[("pnev", i)] = sv_c
        sv_c += 8; svt[("rt", i)] = sv_c
        sv_c += 1; svt[("tsev", i)] = sv_c
        sv_c += 2; svt[("res", i)] = sv_c

    ld_v = {}
    ss_entry = p.ss.n

    @block.sync
    def _(sync):
        sync.wait_ge(p.ss.h, ss_entry)
        for i in range(NT):
            ld_v[i] = p.dx[i % 2].start(sync, g["xtile"][i % 2][:], xd[i])
            if i >= 1:
                sync.wait_ge(p.sv.h, svt[("res", i - 1)])
                p.dx[(i - 1) % 2].start(sync, xd[i - 1], g["xtile"][(i - 1) % 2][:])
        sync.wait_ge(p.sv.h, svt[("res", NT - 1)])
        p.dx[(NT - 1) % 2].start(sync, xd[NT - 1], g["xtile"][(NT - 1) % 2][:])

    @block.tensor
    def _(tensor):
        tensor.wait_ge(p.sv.h, sv_router_w)
        for i in range(NT):
            if i >= 1:
                tensor.wait_ge(p.ss.h, sst[("pn", i - 1)])
            for c in range(NCH):
                nc.tensor.matmul(g["ps_e"][:, 0:256],
                                 g["xn_fm"][:, c, i * 128:(i + 1) * 128],
                                 wpT[:, c, :], start=(c == 0), stop=False)
            p.st.inc(nc.tensor.matmul(g["ps_e"][:, 0:256], g["onescol"][:],
                                      g["biasrow"][:, 0:256], start=False, stop=True))
            if i >= 1:
                tensor.wait_ge(p.sv.h, svt[("dev", i - 1)])
            ins = None
            for c in range(NCH):
                ins = nc.tensor.matmul(g["ps_e"][:, 256:271],
                                       g["xn_fm"][:, c, i * 128:(i + 1) * 128],
                                       wdT[:, c, 0:15], start=(c == 0),
                                       stop=(c == NCH - 1))
            p.st.inc(ins)
            tensor.wait_ge(p.ss.h, sst[("pn", i)])
            if i >= 1:
                tensor.wait_ge(p.sv.h, svt[("pnev", i - 1)])
            ins = None
            for ch in range(2):
                ins = nc.tensor.transpose(g["ps_tp"][:, 256 + ch * 128:256 + (ch + 1) * 128],
                                          pn_t[i % 2][:, ch * 128:(ch + 1) * 128],
                                          g["ident"][:])
            p.st.inc(ins)
            tensor.wait_ge(p.sv.h, svt[("pnev", i)])
            if i >= 1:
                tensor.wait_ge(p.sv.h, svt[("rt", i - 1)])
            ins = None
            for ch in range(2):
                ins = nc.tensor.matmul(g["ps_e"][:, 384:388],
                                       p_fm[i % 2][:, ch, :], simn[:, ch, :],
                                       start=(ch == 0), stop=(ch == 1))
            p.st.inc(ins)
            tensor.wait_ge(p.sv.h, svt[("rt", i)])
            if i >= 1:
                tensor.wait_ge(p.sv.h, svt[("tsev", i - 1)])
            p.st.inc(nc.tensor.transpose(g["ps_tp"][0:16, 512:640],
                                         t_scl[i % 2][:], g["ident"][:]))
            tensor.wait_ge(p.sv.h, svt[("tsev", i)])
            if i >= 1:
                tensor.wait_ge(p.sv.h, svt[("res", i - 1)])
            ins = None
            for half in range(2):
                for c in range(NCH):
                    nc.tensor.matmul(g["ps_s"][half][:],
                                     g["xn_fm"][:, c, i * 128:(i + 1) * 128],
                                     g["wbuf"][1][:, c, half * 512:(half + 1) * 512],
                                     start=(c == 0), stop=False)
                ins = nc.tensor.matmul(g["ps_s"][half][:], t_fm[i % 2][0:15, :],
                                       wuT[0:15, half * 512:(half + 1) * 512],
                                       start=False, stop=True)
                p.st.inc(ins)

    @block.scalar
    def _(scalar):
        scalar.wait_ge(p.sv.h, sv_router_w)
        for i in range(NT):
            scalar.wait_ge(p.st.h, stt[("proj", i)])
            if i >= 1:
                scalar.wait_ge(p.ss.h, sst[("sq", i - 1)])
            p.ss.inc(scalar.activation(sqd[:, 0:256], g["ps_e"][:, 0:256], AF.Square,
                                       accum_out=ssp[:]))
            scalar.wait_ge(p.ss.h, p.ss.n)
            p.ss.inc(scalar.activation(rn[:], ssp[:], AF.Ln, bias=c38[:]))
            scalar.wait_ge(p.ss.h, p.ss.n)
            p.ss.inc(scalar.activation(rn[:], rn[:], AF.Exp, scale=-0.5))
            scalar.wait_ge(p.sv.h, svt[("min", i)])
            if i >= 2:
                scalar.wait_ge(p.st.h, stt[("pnT", i - 2)])
            p.ss.inc(scalar.activation(pn_t[i % 2][:], g["ps_e"][:, 0:256],
                                       AF.Identity, scale=rn[:]))
            scalar.wait_ge(p.sv.h, svt[("rt", i)] - 6)
            p.ss.inc(scalar.activation(lw[:], lgc[:], AF.Exp, scale=s_bc[:],
                                       accum_out=esum[:]))

    @block.vector
    def _(vector):
        for i in range(NT):
            vector.wait_ge(p.ss.h, sst[("ex", i)])
            p.sv.inc(vector.tensor_scalar(rn[:], rn[:], 1e12, None, ALU.min))
            vector.wait_ge(p.st.h, stt[("dall", i)])
            p.sv.inc(vector.tensor_copy(dall[i % 2][:, 0:15], g["ps_e"][:, 256:271]))
            vector.wait_ge(p.st.h, stt[("pnT", i)])
            p.sv.inc(vector.tensor_copy(p_fm[i % 2][:],
                                        g["ps_tp"][:, 256:512].rearrange(
                                            "p (c t) -> p c t", c=2)))
            vector.wait_ge(p.st.h, stt[("lg", i)])
            p.sv.inc(vector.reduce_max(rm[:].unsqueeze(2),
                                       g["ps_e"][:, 384:388].rearrange(
                                           "p (a b) -> p a b", a=1), AX.X))
            vector.wait_ge(p.sv.h, p.sv.n)
            p.sv.inc(vector.tensor_scalar(lgc[:], g["ps_e"][:, 384:388], rm[:],
                                          None, ALU.subtract))
            vector.wait_ge(p.ss.h, sst[("lw", i)])
            p.sv.inc(vector.reciprocal(er[:], esum[:]))
            vector.wait_ge(p.sv.h, p.sv.n)
            p.sv.inc(vector.tensor_scalar(lw[:], lw[:], er[:], None, ALU.mult))
            vector.wait_ge(p.sv.h, p.sv.n)
            p.sv.inc(vector.reduce_max(rm[:].unsqueeze(2),
                                       lw[:].rearrange("p (a b) -> p a b", a=1), AX.X))
            vector.wait_ge(p.sv.h, p.sv.n)
            p.sv.inc(vector.tensor_scalar(msk[:], lw[:], rm[:], None, ALU.is_ge))
            vector.wait_ge(p.sv.h, p.sv.n)
            p.sv.inc(vector.tensor_mul(ret[:], lw[:], msk[:]))
            vector.wait_ge(p.sv.h, p.sv.n)
            if i >= 2:
                vector.wait_ge(p.st.h, stt[("tsT", i - 2)])
            ro = 0
            ins = None
            for e in range(4):
                r = [1, 2, 4, 8][e]
                ins = vector.tensor_scalar(t_scl[i % 2][:, ro:ro + r],
                                           dall[i % 2][:, ro:ro + r],
                                           ret[:, e:e + 1], None, ALU.mult)
                ro += r
            p.sv.inc(ins)
            vector.wait_ge(p.st.h, stt[("tsT", i)])
            if i >= 2:
                vector.wait_ge(p.st.h, stt[("base", i - 2)])
            p.sv.inc(vector.tensor_copy(t_fm[i % 2][0:16, :],
                                        g["ps_tp"][0:16, 512:640]))
            vector.wait_ge(p.st.h, stt[("base", i)])
            vector.wait_ge(p.dx[i % 2].h, ld_v[i])
            for half in range(2):
                sl = g["xtile"][i % 2][:, half * 512:(half + 1) * 512]
                p.sv.inc(vector.tensor_add(sl, sl, g["ps_s"][half][:]))
    assert p.sv.n == sv_c, (p.sv.n, sv_c)
    assert p.ss.n == ss_c, (p.ss.n, ss_c)
    assert p.st.n == st_c, (p.st.n, st_c)


QVA = 5
QVB = 12
QS = 2


def emit_quant_a(p, q, src, width, wv=()):
    nc, block = p.nc, p.block
    nb = width // 16
    sv0, ss0 = p.sv.n, p.ss.n

    @block.vector
    def _(vector):
        vector.wait_ge(p.sv.h, sv0)
        for sem, cnt in wv:
            vector.wait_ge(sem, cnt)
        p.sv.inc(vector.tensor_reduce(
            q["bm"][:, 0:nb].unsqueeze(2),
            src.rearrange("p (a b) -> p a b", b=16), AX.X, ALU.max,
            apply_absolute_value=True))
        vector.wait_ge(p.sv.h, p.sv.n)
        p.sv.inc(vector.tensor_scalar(q["sc2"][:, 0:nb], q["bm"][:, 0:nb],
                                      1.0 / 6.0, 1e-12, ALU.mult, ALU.max))
        vector.wait_ge(p.sv.h, p.sv.n)
        p.sv.inc(vector.reciprocal(q["inv"][:, 0:nb], q["sc2"][:, 0:nb]))
        vector.wait_ge(p.sv.h, p.sv.n)
        p.sv.inc(vector.tensor_copy(
            q["invx"][:, 0:width].rearrange("p (a b) -> p a b", b=16),
            q["inv"][:, 0:nb].unsqueeze(2).broadcast_to([128, nb, 16])))
        vector.wait_ge(p.sv.h, p.sv.n)
        p.sv.inc(vector.tensor_mul(src, src, q["invx"][:, 0:width]))
    sv_y = p.sv.n

    @block.scalar
    def _(scalar):
        scalar.wait_ge(p.ss.h, ss0)
        scalar.wait_ge(p.sv.h, sv_y)
        p.ss.inc(scalar.activation(q["sg"][:, 0:width], src, AF.Sign))
        p.ss.inc(scalar.activation(q["ab"][:, 0:width], src, AF.Abs))
    assert p.sv.n == sv0 + QVA and p.ss.n == ss0 + QS
    return sv_y, p.ss.n


def emit_quant_b(p, q, src, dst, width, ss_cast, wv_dst=()):
    nc, block = p.nc, p.block
    nb = width // 16
    sv0 = p.sv.n

    @block.vector
    def _(vector):
        vector.wait_ge(p.sv.h, sv0)
        vector.wait_ge(p.ss.h, ss_cast)
        tA = q["tA"][:, 0:width]
        tB = q["tB"][:, 0:width]
        tC = q["tC"][:, 0:width]
        ab = q["ab"][:, 0:width]
        p.sv.inc(vector.tensor_scalar(tA, ab, 0.25, None, ALU.is_gt))
        vector.wait_ge(p.sv.h, p.sv.n)
        p.sv.inc(vector.scalar_tensor_tensor(tA, ab, 0.75, tA, ALU.is_gt,
                                             ALU.add))
        vector.wait_ge(p.sv.h, p.sv.n)
        p.sv.inc(vector.scalar_tensor_tensor(tA, ab, 1.25, tA, ALU.is_gt,
                                             ALU.add))
        vector.wait_ge(p.sv.h, p.sv.n)
        p.sv.inc(vector.scalar_tensor_tensor(tA, ab, 1.75, tA, ALU.is_gt,
                                             ALU.add))
        p.sv.inc(vector.tensor_scalar(tB, ab, 2.5, None, ALU.is_gt))
        vector.wait_ge(p.sv.h, p.sv.n)
        p.sv.inc(vector.scalar_tensor_tensor(tB, ab, 3.5, tB, ALU.is_gt,
                                             ALU.add))
        p.sv.inc(vector.tensor_scalar(tC, ab, 5.0, 2.0, ALU.is_gt, ALU.mult))
        vector.wait_ge(p.sv.h, p.sv.n)
        p.sv.inc(vector.scalar_tensor_tensor(tB, tA, 0.5, tB, ALU.mult,
                                             ALU.add))
        vector.wait_ge(p.sv.h, p.sv.n)
        p.sv.inc(vector.tensor_add(tB, tB, tC))
        vector.wait_ge(p.sv.h, p.sv.n)
        p.sv.inc(vector.tensor_mul(tB, tB, q["sg"][:, 0:width]))
        p.sv.inc(vector.tensor_copy(
            tA.rearrange("p (a b) -> p a b", b=16),
            q["sc2"][:, 0:nb].unsqueeze(2).broadcast_to([128, nb, 16])))
        vector.wait_ge(p.sv.h, p.sv.n)
        for sem, cnt in wv_dst:
            vector.wait_ge(sem, cnt)
        p.sv.inc(vector.tensor_mul(dst, tB, tA))
    assert p.sv.n == sv0 + QVB
    return p.sv.n


def emit_quant(p, q, src, dst, width, wv=(), wv_dst=()):
    sv_y, ss_cast = emit_quant_a(p, q, src, width, wv=wv)
    sv = emit_quant_b(p, q, src, dst, width, ss_cast, wv_dst=wv_dst)
    return sv, ss_cast


def emit_ffn1(p, g, xd, Wg, Wu, Wd, cch, lnst, first):
    nc, block = p.nc, p.block
    ar = p.arena
    ar.reset()

    GROUPS = [(0, 6), (6, 6), (12, 6), (18, 4)]

    wgq = ar.bf16([128, NCH, 768])
    wuq = ar.bf16([128, NCH, 768])
    wdq = ar.bf16([128, 6, 1024])
    xq_fm = ar.bf16([128, NCH, S])
    ybuf = [ar.f32([128, 1024]) for _ in range(2)]
    qout = [ar.bf16([128, 1024]) for _ in range(2)]
    hq_fm = [ar.bf16([128, 6, 128]) for _ in range(2)]
    ttile = [ar.bf16([128, 1024]) for _ in range(2)]
    ab2 = [ar.f32([128, 1024]) for _ in range(2)]
    sg2 = [ar.bf16([128, 1024]) for _ in range(2)]
    q0 = dict(bm=ar.f32([128, 64]), sc2=ar.f32([128, 64]),
              inv=ar.f32([128, 64]), invx=ar.f32([128, 1024]),
              tA=ar.bf16([128, 1024]), tB=ar.bf16([128, 1024]),
              tC=ar.bf16([128, 1024]))
    qs = [dict(q0, ab=ab2[0], sg=sg2[0]), dict(q0, ab=ab2[1], sg=sg2[1])]

    ybuf_free = [(p.sv.n, p.ss.n)] * 2
    qout_free = [p.st.n, p.st.n]
    ttile_free = [p.dps.n, p.dps.n]
    pstp_free = [p.sv.n]

    prep_stores = p.dps.n
    if first:
        prep = []
        for rc in range(22):
            prep.append((Wg[rc * 128:(rc + 1) * 128, :], 1024,
                         cch["wgT"][:, :, rc * 128:(rc + 1) * 128], NCH))
        for rc in range(22):
            prep.append((Wu[rc * 128:(rc + 1) * 128, :], 1024,
                         cch["wuT"][:, :, rc * 128:(rc + 1) * 128], NCH))
        for rc in range(8):
            for ks in range(3):
                ksw = 768 if ks == 2 else 1024
                prep.append((Wd[rc * 128:(rc + 1) * 128,
                                ks * 1024:ks * 1024 + ksw], ksw,
                             cch["wdT"][:, ks * 8:ks * 8 + ksw // 128,
                                        rc * 128:(rc + 1) * 128], ksw // 128))
        ldv, cpv, acast, bsv = {}, {}, {}, {}

        def emit_store(sync, jj):
            wdt_, dstw_, ntr_ = prep[jj][1], prep[jj][2], prep[jj][3]
            sync.wait_ge(p.sv.h, cpv[jj])
            p.dps.start(sync, dstw_,
                        ttile[jj % 2][:, 0:wdt_].rearrange(
                            "p (c t) -> p c t", c=ntr_))
            ttile_free[jj % 2] = p.dps.n

        def prep_b(j):
            wdt, ntr = prep[j][1], prep[j][3]
            bsv[j] = emit_quant_b(p, qs[j % 2], ybuf[j % 2][:, 0:wdt],
                                  qout[j % 2][:, 0:wdt], wdt, acast[j],
                                  wv_dst=[(p.st.h, qout_free[j % 2])])

            @block.tensor
            def _(tensor, j=j, wdt=wdt, ntr=ntr, pf=pstp_free[0]):
                tensor.wait_ge(p.sv.h, max(bsv[j], pf))
                ins = None
                for c in range(ntr):
                    ins = nc.tensor.transpose(
                        g["ps_tp"][:, c * 128:(c + 1) * 128],
                        qout[j % 2][:, c * 128:(c + 1) * 128], g["ident"][:])
                p.st.inc(ins)
            qout_free[j % 2] = p.st.n
            st_tr = p.st.n

            @block.vector
            def _(vector, j=j, wdt=wdt, st_tr=st_tr, tf=ttile_free[j % 2]):
                vector.wait_ge(p.st.h, st_tr)
                vector.wait_ge(p.dps.h, tf)
                cpv[j] = p.sv.inc(vector.tensor_copy(ttile[j % 2][:, 0:wdt],
                                                     g["ps_tp"][:, 0:wdt]))
            pstp_free[0] = p.sv.n

        for j, (srcw, wdt, dstw, ntr) in enumerate(prep):
            yfs, yfss = ybuf_free[j % 2]

            @block.sync
            def _(sync, j=j, srcw=srcw, wdt=wdt, yfs=yfs, yfss=yfss):
                if j >= 2:
                    sync.wait_ge(p.sv.h, bsv[j - 2])
                sync.wait_ge(p.ss.h, yfss)
                s_ = p.dp[j % 4]
                ldv[j] = (s_.h, s_.start(sync, ybuf[j % 2][:, 0:wdt], srcw))
                if j >= 2:
                    emit_store(sync, j - 2)

            if j >= 1:
                prep_b(j - 1)
            ybuf_free[j % 2] = emit_quant_a(
                p, qs[j % 2], ybuf[j % 2][:, 0:wdt], wdt, wv=[ldv[j]])
            acast[j] = ybuf_free[j % 2][1]
        prep_b(len(prep) - 1)

        @block.sync
        def _(sync):
            emit_store(sync, len(prep) - 2)
            emit_store(sync, len(prep) - 1)
        prep_stores = p.dps.n

    dq2 = {}
    ss_x0 = p.ss.n

    @block.sync
    def _(sync):
        sync.wait_ge(p.ss.h, lnst["ss_sq"])
        sync.wait_ge(p.sv.h, lnst["sv_red16"])
        for i in range(NT):
            if i >= 2:
                sync.wait_ge(p.ss.h, ss_x0 + 3 * (i - 2) + 1)
            dq2[i] = p.dx[i % 2].start(sync, g["xtile"][i % 2][:], xd[i])

    xev, nrm, xacast, xbsv = {}, {}, {}, {}

    def x_b(i):
        xbsv[i] = emit_quant_b(p, qs[i % 2], ybuf[i % 2][:], qout[i % 2][:],
                               1024, xacast[i],
                               wv_dst=[(p.st.h, qout_free[i % 2])])

        @block.tensor
        def _(tensor, i=i, pf=pstp_free[0]):
            tensor.wait_ge(p.sv.h, max(xbsv[i], pf))
            ins = None
            for c in range(NCH):
                ins = nc.tensor.transpose(g["ps_tp"][:, c * 128:(c + 1) * 128],
                                          qout[i % 2][:, c * 128:(c + 1) * 128],
                                          g["ident"][:])
            p.st.inc(ins)
        qout_free[i % 2] = p.st.n
        st_tr = p.st.n

        @block.vector
        def _(vector, i=i, st_tr=st_tr):
            vector.wait_ge(p.st.h, st_tr)
            xev[i] = p.sv.inc(vector.tensor_copy(
                xq_fm[:, :, i * 128:(i + 1) * 128],
                g["ps_tp"][:].rearrange("p (c t) -> p c t", c=NCH)))
        pstp_free[0] = p.sv.n

    for i in range(NT):
        @block.scalar
        def _(scalar, i=i, yf=ybuf_free[i % 2][0]):
            if i == 0:
                scalar.wait_ge(p.sv.h, lnst["sv_stats"])
            scalar.wait_ge(p.dx[i % 2].h, dq2[i])
            scalar.wait_ge(p.sv.h, yf)
            if i >= 2:
                scalar.wait_ge(p.sv.h, xbsv.get(i - 2, 0))
            nrm[i] = p.ss.inc(scalar.activation(
                ybuf[i % 2][:], g["xtile"][i % 2][:], AF.Identity,
                bias=g["nmrs"][:, i:i + 1], scale=g["rstd"][:, i:i + 1]))
        assert nrm[i] == ss_x0 + 3 * i + 1

        if i >= 1:
            x_b(i - 1)
        ybuf_free[i % 2] = emit_quant_a(p, qs[i % 2], ybuf[i % 2][:], 1024,
                                        wv=[(p.ss.h, nrm[i])])
        xacast[i] = ybuf_free[i % 2][1]
    x_b(NT - 1)

    wld = {}
    guv, trh, dnv, silv, hmv, evv, resv, ldx = {}, {}, {}, {}, {}, {}, {}, {}
    gacast, gbsv = {}, {}
    war = dict(res=0, sil=0, hm=0)
    for gi, (k0, ng) in enumerate(GROUPS):
        gw = ng * 128
        subs = [(0, 512), (512, gw - 512)] if gw > 512 else [(0, gw)]

        @block.gpsimd
        def _(gpsimd, gi=gi, k0=k0, ng=ng):
            if first and gi == 0:
                gpsimd.wait_ge(p.dps.h, prep_stores)
            if gi >= 1:
                gpsimd.wait_ge(p.st.h, guv[(gi - 1, NT - 1)])
                gpsimd.wait_ge(p.st.h, dnv[(gi - 1, NT - 1)])
            wr = p.load_par(gpsimd, wgq[:, :, 0:gw],
                            cch["wgT"][:, :, k0 * 128:k0 * 128 + gw], n=2)
            wr += p.load_par(gpsimd, wuq[:, :, 0:gw],
                             cch["wuT"][:, :, k0 * 128:k0 * 128 + gw], n=2)
            c1_ = p.dw[0].start(gpsimd, wdq[:, 0:ng // 2, :],
                                cch["wdT"][:, k0:k0 + ng // 2, :])
            c2_ = p.dw[1].start(gpsimd, wdq[:, ng // 2:ng, :],
                                cch["wdT"][:, k0 + ng // 2:k0 + ng, :])
            wld[gi] = wr + [(p.dw[0].h, c1_), (p.dw[1].h, c2_)]

        for ii in range(NT + 2):
            @block.tensor
            def _(tensor, gi=gi, ii=ii, gw=gw, subs=subs,
                  sil_war=war["sil"], hm_war=war["hm"]):
                if ii < NT:
                    wwait(p, tensor, wld[gi])
                    tensor.wait_ge(p.sv.h, xev[ii])
                    if sil_war:
                        tensor.wait_ge(p.ss.h, sil_war)
                        tensor.wait_ge(p.sv.h, hm_war)
                    for wq_, off in ((wgq, 0), (wuq, 1024)):
                        ins = None
                        for so, sw in subs:
                            for c in range(NCH):
                                ins = nc.tensor.matmul(
                                    g["ps_pv"][:, off + so:off + so + sw],
                                    xq_fm[:, c, ii * 128:(ii + 1) * 128],
                                    wq_[:, c, so:so + sw],
                                    start=(c == 0), stop=(c == NCH - 1))
                        p.st.inc(ins)
                    guv[(gi, ii)] = p.st.n

            if ii < NT:
                @block.scalar
                def _(scalar, gi=gi, ii=ii, gw=gw, yf=ybuf_free[ii % 2][0]):
                    scalar.wait_ge(p.st.h, guv[(gi, ii)] - 1)
                    scalar.wait_ge(p.sv.h, yf)
                    bprev = gbsv.get((gi, ii - 2), xbsv.get(ii - 2 + NT, 0))
                    if bprev:
                        scalar.wait_ge(p.sv.h, bprev)
                    silv[(gi, ii)] = p.ss.inc(scalar.activation(
                        ybuf[ii % 2][:, 0:gw], g["ps_pv"][:, 0:gw], AF.Silu))
                war["sil"] = silv[(gi, ii)]

            if ii >= 1 and ii - 1 < NT:
                gbsv[(gi, ii - 1)] = emit_quant_b(
                    p, qs[(ii - 1) % 2], ybuf[(ii - 1) % 2][:, 0:gw],
                    qout[(ii - 1) % 2][:, 0:gw], gw, gacast[(gi, ii - 1)],
                    wv_dst=[(p.st.h, qout_free[(ii - 1) % 2])])

            if ii < NT:
                @block.vector
                def _(vector, gi=gi, ii=ii, gw=gw):
                    vector.wait_ge(p.ss.h, silv[(gi, ii)])
                    vector.wait_ge(p.st.h, guv[(gi, ii)])
                    hmv[(gi, ii)] = p.sv.inc(vector.tensor_mul(
                        ybuf[ii % 2][:, 0:gw], ybuf[ii % 2][:, 0:gw],
                        g["ps_pv"][:, 1024:1024 + gw]))
                war["hm"] = hmv[(gi, ii)]
                ybuf_free[ii % 2] = emit_quant_a(
                    p, qs[ii % 2], ybuf[ii % 2][:, 0:gw], gw)
                gacast[(gi, ii)] = ybuf_free[ii % 2][1]

            if ii >= 1 and ii - 1 < NT:
                @block.tensor
                def _(tensor, gi=gi, ii=ii, ng=ng, pf=pstp_free[0]):
                    tensor.wait_ge(p.sv.h, max(gbsv[(gi, ii - 1)], pf))
                    ins = None
                    for c in range(ng):
                        ins = nc.tensor.transpose(
                            g["ps_tp"][:, c * 128:(c + 1) * 128],
                            qout[(ii - 1) % 2][:, c * 128:(c + 1) * 128],
                            g["ident"][:])
                    p.st.inc(ins)
                    trh[(gi, ii - 1)] = p.st.n
                qout_free[(ii - 1) % 2] = trh[(gi, ii - 1)]

                @block.vector
                def _(vector, gi=gi, ii=ii, ng=ng, gw=gw):
                    vector.wait_ge(p.st.h, trh[(gi, ii - 1)])
                    if (gi, ii - 3) in dnv:
                        vector.wait_ge(p.st.h, dnv[(gi, ii - 3)])
                    evv[(gi, ii - 1)] = p.sv.inc(vector.tensor_copy(
                        hq_fm[(ii - 1) % 2][:, 0:ng, :],
                        g["ps_tp"][:, 0:gw].rearrange("p (c t) -> p c t", c=ng)))
                pstp_free[0] = evv[(gi, ii - 1)]

            if ii >= 2 and ii - 2 < NT:
                @block.vector
                def _(vector, gi=gi, ii=ii):
                    vector.wait_ge(p.st.h, dnv[(gi, ii - 2)])
                    vector.wait_ge(p.dx[(ii - 2) % 2].h, ldx[(gi, ii - 2)])
                    xt = g["xtile"][(ii - 2) % 2]
                    p.sv.inc(vector.tensor_add(xt[:, 0:512], xt[:, 0:512],
                                               g["ps_e"][:]))
                    resv[(gi, ii - 2)] = p.sv.inc(vector.tensor_add(
                        xt[:, 512:1024], xt[:, 512:1024], g["ps_s"][0][:]))
                war["res"] = resv[(gi, ii - 2)]

            if ii >= 1 and ii - 1 < NT:
                @block.tensor
                def _(tensor, gi=gi, ii=ii, ng=ng, res_war=war["res"]):
                    tensor.wait_ge(p.sv.h, evv[(gi, ii - 1)])
                    if res_war:
                        tensor.wait_ge(p.sv.h, res_war)
                    for pb, off in ((g["ps_e"], 0), (g["ps_s"][0], 512)):
                        ins = None
                        for c in range(ng):
                            ins = nc.tensor.matmul(
                                pb[:], hq_fm[(ii - 1) % 2][:, c, :],
                                wdq[:, c, off:off + 512],
                                start=(c == 0), stop=(c == ng - 1))
                        p.st.inc(ins)
                    dnv[(gi, ii - 1)] = p.st.n

            @block.sync
            def _(sync, gi=gi, ii=ii):
                if ii >= 2 and ii - 2 < NT:
                    sync.wait_ge(p.sv.h, resv[(gi, ii - 2)])
                    p.dx[(ii - 2) % 2].start(sync, xd[ii - 2],
                                             g["xtile"][(ii - 2) % 2][:])
                if ii < NT:
                    if gi == 0:
                        sync.wait_ge(p.ss.h, nrm[NT - 2 + (ii % 2)])
                    ldx[(gi, ii)] = p.dx[ii % 2].start(
                        sync, g["xtile"][ii % 2][:], xd[ii])


def emit_proj_tm_act(p, g, wslot, dst_fn, w_ready, act):
    nc, block = p.nc, p.block
    st0, ss0 = p.st.n, p.ss.n

    @block.tensor
    def _(tensor):
        wwait(p, tensor, w_ready, wslot)
        tensor.wait_ge(p.sv.h, p.sv_xn)
        tensor.wait_ge(p.ss.h, ss0)
        for ih in range(NT * 2):
            i, half = divmod(ih, 2)
            pb = g["ps_s"][ih % 2]
            if ih >= 2:
                tensor.wait_ge(p.ss.h, ss0 + ih - 1)
            ins = None
            for c in range(NCH):
                ins = nc.tensor.matmul(pb[:],
                                       g["xn_fm"][:, c, i * 128:(i + 1) * 128],
                                       g["wbuf"][wslot][:, c, half * 512:(half + 1) * 512],
                                       start=(c == 0), stop=(c == NCH - 1))
            p.st.inc(ins)

    @block.scalar
    def _(scalar):
        for ih in range(NT * 2):
            i, half = divmod(ih, 2)
            scalar.wait_ge(p.st.h, st0 + ih + 1)
            p.ss.inc(scalar.activation(dst_fn(i, half), g["ps_s"][ih % 2][:], act))
    return p.st.n, p.ss.n


def emit_proj_fm_act(p, g, wslot, dst_fm, w_ready, act):
    nc, block = p.nc, p.block
    st0, ss0 = p.st.n, p.ss.n

    @block.tensor
    def _(tensor):
        wwait(p, tensor, w_ready, wslot)
        tensor.wait_ge(p.sv.h, p.sv_xn)
        tensor.wait_ge(p.ss.h, ss0)
        for k in range(32):
            oc, sp = divmod(k, 4)
            pb = g["ps_s"][k % 2]
            if k >= 2:
                tensor.wait_ge(p.ss.h, ss0 + k - 1)
            ins = None
            for c in range(NCH):
                ins = nc.tensor.matmul(pb[:],
                                       g["wbuf"][wslot][:, c, oc * 128:(oc + 1) * 128],
                                       g["xn_fm"][:, c, sp * 512:(sp + 1) * 512],
                                       start=(c == 0), stop=(c == NCH - 1))
            p.st.inc(ins)

    @block.scalar
    def _(scalar):
        for k in range(32):
            oc, sp = divmod(k, 4)
            scalar.wait_ge(p.st.h, st0 + k + 1)
            p.ss.inc(scalar.activation(dst_fm[:, oc, sp * 512:(sp + 1) * 512],
                                       g["ps_s"][k % 2][:], act))
    return p.st.n, p.ss.n


def emit_lightning(p, g, xd, Wq, Wk, Wv, Wo, gamma, beta, Wg1, Wg2):
    nc, block = p.nc, p.block
    ar = p.arena
    ar.reset()

    def build_slopes(n):
        def po2(m):
            start = 2.0 ** (-2.0 ** (-(np.log2(m) - 3)))
            return [start * start ** i for i in range(m)]
        if np.log2(n).is_integer():
            return po2(n)
        closest = 2 ** int(np.floor(np.log2(n)))
        return po2(closest) + build_slopes(2 * closest)[0::2][:n - closest]
    s_np = np.array(build_slopes(NH), dtype=np.float32)
    s_np = s_np * (1.0 - 0 / (12 - 1) + 1e-05)
    slopes = [float(v) for v in s_np]
    lamc = [float(np.exp(np.float64(-v) * 128.0)) for v in slopes]

    q_fm = ar.bf16([128, NCH, S])
    k_tm = ar.bf16([128, NT, D])
    v_tm = ar.bf16([128, NT, D])
    kpair = ar.bf16([128, S])
    maskz = ar.bf16([128, NH, 128])
    kv_f = ar.f32([128, 8, HD])
    kv_b = ar.bf16([128, 8, HD])
    g1_fm = ar.bf16([128, S])
    _og = ar.bf16([128, 1024])
    og_t = [_og, _og]
    bb_bc = ar.bf16([128, 1024])
    wg1b = ar.bf16([128, 8, 64])
    o_tm = g["xn_fm"].rearrange("p c t -> p (c t)").rearrange(
        "p (a b) -> p a b", a=NT)
    og_fm = q_fm
    lamq = g["hmax"]
    lamk = g["segr"]
    gb_bc = g["sq2"][0]

    @block.gpsimd
    def _(gpsimd):
        g["_a"] = p.load_par(gpsimd, g["wbuf"][0][:],
                             Wq.rearrange("(c p) o -> p c o", p=128))
        p.dw[1].start(gpsimd, g["biasrow"][:, 0:D].unsqueeze(1),
                      gamma.rearrange("(a d) -> a d", a=1))
        p.dw[1].start(gpsimd, g["biasrow"][:, D:2 * D].unsqueeze(1),
                      beta.rearrange("(a d) -> a d", a=1))
        g["_b"] = p.load_par(gpsimd, g["wbuf"][1][:],
                             Wk.rearrange("(c p) o -> p c o", p=128))
    wq_rdy = g.pop("_a"); wk_rdy = g.pop("_b")

    st_q, _ = emit_proj_fm_act(p, g, 0, q_fm, wq_rdy, AF.Silu)

    @block.gpsimd
    def _(gpsimd):
        gpsimd.wait_ge(p.st.h, st_q)
        g["_a"] = p.load_par(gpsimd, g["wbuf"][0][:],
                             Wv.rearrange("(c p) o -> p c o", p=128))
    wv_rdy = g.pop("_a")

    st_k, _ = emit_proj_tm_act(p, g, 1,
                               lambda i, h: k_tm[:, i, h * 512:(h + 1) * 512],
                               wk_rdy, AF.Silu)

    @block.gpsimd
    def _(gpsimd):
        gpsimd.wait_ge(p.st.h, st_k)
        p.dw[1].start(gpsimd, wg1b[:],
                      Wg1.rearrange("(c p) o -> p c o", p=128))
        g["_b"] = p.dw[1].start(gpsimd, g["wbuf"][1][0:64, 5, :], Wg2)
    wg_rdy = g.pop("_b")

    st_v, ss_v = emit_proj_tm_act(p, g, 0,
                                  lambda i, h: v_tm[:, i, h * 512:(h + 1) * 512],
                                  wv_rdy, AF.Silu)

    @block.gpsimd
    def _(gpsimd):
        gpsimd.wait_ge(p.ss.h, ss_v)
        g["_a"] = p.load_par(gpsimd, g["wbuf"][0][:],
                             Wo.rearrange("(c p) o -> p c o", p=128))
    wo_rdy = g.pop("_a")

    ss_m0 = p.ss.n
    sv_mzb = p.sv.n

    @block.scalar
    def _(scalar):
        scalar.wait_ge(p.dmisc.h, p.dq_consts)
        for h in range(NH):
            if h >= 1:
                scalar.wait_ge(p.sv.h, sv_mzb + h)
            p.ss.inc(scalar.activation(maskz[:, h, :], g["dmatTc"][:], AF.Exp,
                                       scale=-slopes[h]))
            p.ss.inc(scalar.activation(lamq[:, h:h + 1], g["idxp1"][:], AF.Exp,
                                       scale=-slopes[h]))
            p.ss.inc(scalar.activation(lamk[:, h:h + 1], g["idxrv"][:], AF.Exp,
                                       scale=-slopes[h]))

    @block.vector
    def _(vector):
        for h in range(NH):
            vector.wait_ge(p.ss.h, ss_m0 + 3 * h + 1)
            p.sv.inc(vector.tensor_tensor(maskz[:, h, :], maskz[:, h, :],
                                          g["triT"][:], ALU.mult))
        p.sv.inc(vector.memset(kv_f[:], 0.0))
        p.sv.inc(vector.memset(kv_b[:], 0.0))
    sv_mz = p.sv.n
    assert sv_mz == sv_mzb + NH + 2

    st_g0, sv_g0 = p.st.n, p.sv.n

    @block.tensor
    def _(tensor):
        tensor.wait_ge(p.dw[1].h, wg_rdy)
        tensor.wait_ge(p.sv.h, max(p.sv_xn, sv_mz))
        for sp in range(4):
            if sp >= 1:
                tensor.wait_ge(p.sv.h, sv_g0 + sp)
            ins = None
            for c in range(NCH):
                ins = nc.tensor.matmul(g["ps_e"][0:64, 0:512],
                                       wg1b[:, c, :],
                                       g["xn_fm"][:, c, sp * 512:(sp + 1) * 512],
                                       start=(c == 0), stop=(c == NCH - 1))
            p.st.inc(ins)

    @block.vector
    def _(vector):
        for sp in range(4):
            vector.wait_ge(p.st.h, st_g0 + sp + 1)
            p.sv.inc(vector.tensor_copy(g1_fm[0:64, sp * 512:(sp + 1) * 512],
                                        g["ps_e"][0:64, 0:512]))
    sv_g1 = p.sv.n

    st_gb0 = p.st.n

    @block.tensor
    def _(tensor):
        tensor.wait_ge(p.sv.h, sv_g1)
        for j in range(4):
            if j >= 1:
                tensor.wait_ge(p.sv.h, sv_g1 + j)
            off = (0 if j < 2 else D) + (j % 2) * 512
            p.st.inc(nc.tensor.matmul(g["ps_e"][:, 0:512], g["onescol"][:],
                                      g["biasrow"][:, off:off + 512],
                                      start=True, stop=True))

    @block.vector
    def _(vector):
        for j in range(4):
            vector.wait_ge(p.st.h, st_gb0 + j + 1)
            dst = gb_bc if j < 2 else bb_bc
            p.sv.inc(vector.tensor_copy(dst[:, (j % 2) * 512:(j % 2) * 512 + 512],
                                        g["ps_e"][:, 0:512]))
    sv_gb = p.sv.n

    order = [(pr, i, hh) for pr in range(8) for i in range(NT) for hh in range(2)]
    LIGHT_ORDER = {order[k]: order[k - 1] for k in range(1, len(order))}
    LIGHT_ORDER[order[0]] = None
    LIGHT_PREV_HH = {}
    _last = {}
    for _key in order:
        LIGHT_PREV_HH[_key] = _last.get(_key[2])
        _last[_key[2]] = _key
    stc = {}; svc = {}
    st_c = p.st.n; sv_c2 = p.sv.n
    for pr in range(8):
        st_c += 2; stc[("kT", pr)] = st_c
        for i in range(NT):
            for hh in range(2):
                st_c += 1; stc[("S", pr, i, hh)] = st_c
                st_c += 1; stc[("o", pr, i, hh)] = st_c
                st_c += 1; stc[("kv", pr, i, hh)] = st_c
    for pr in range(8):
        sv_c2 += 2; svc[("kTe", pr)] = sv_c2
        for i in range(NT):
            for hh in range(2):
                sv_c2 += 1; svc[("pt", pr, i, hh)] = sv_c2
                sv_c2 += 1; svc[("kl", pr, i, hh)] = sv_c2
                sv_c2 += 2; svc[("oe", pr, i, hh)] = sv_c2
                sv_c2 += 2; svc[("kf", pr, i, hh)] = sv_c2

    @block.tensor
    def _(tensor):
        tensor.wait_ge(p.ss.h, ss_v)
        tensor.wait_ge(p.sv.h, sv_gb)
        for pr in range(8):
            for b in range(2):
                if (pr, b) != (0, 0):
                    prevk = (pr, 0) if b else (pr - 1, 1)
                    tensor.wait_ge(p.sv.h, svc[("kTe", prevk[0])] - (1 - prevk[1]))
                ins = None
                for t in range(8):
                    ins = nc.tensor.transpose(
                        g["ps_tp"][:, t * 128:(t + 1) * 128],
                        k_tm[:, b * 8 + t, pr * 128:(pr + 1) * 128],
                        g["ident"][:])
                p.st.inc(ins)
            tensor.wait_ge(p.sv.h, svc[("kTe", pr)])
            for i in range(NT):
                for hh in range(2):
                    h = 2 * pr + hh
                    base = hh * 64
                    key = (pr, i, hh)
                    prev_it = LIGHT_ORDER.get(key)
                    prev_hh = LIGHT_PREV_HH.get(key)
                    if prev_it is not None and LIGHT_ORDER.get(prev_it) is not None:
                        p2 = LIGHT_ORDER[prev_it]
                        tensor.wait_ge(p.sv.h, svc[("pt",) + p2])
                    p.st.inc(nc.tensor.matmul(
                        g["ps_s"][(i * 2 + hh) % 2][:, 0:128],
                        kpair[base:base + 64, i * 128:(i + 1) * 128],
                        q_fm[base:base + 64, pr, i * 128:(i + 1) * 128],
                        start=True, stop=True))
                    nlin = pr * 32 + i * 2 + hh
                    ob = (nlin % 2) * 512
                    kb = 1024 + (nlin % 2) * 512
                    prev2 = LIGHT_ORDER.get(prev_it) if prev_it else None
                    tensor.wait_ge(p.sv.h, svc[("pt", pr, i, hh)])
                    if i >= 1:
                        tensor.wait_ge(p.sv.h, svc[("kf", pr, i - 1, hh)])
                    if prev2 is not None:
                        tensor.wait_ge(p.sv.h, svc[("oe",) + prev2])
                    nc.tensor.matmul(g["ps_pv"][:, ob:ob + 64],
                                     g["ptb"][0][:, 0:128],
                                     v_tm[:, i, h * 64:(h + 1) * 64],
                                     start=True, stop=True)
                    p.st.inc(nc.tensor.matmul(
                        g["ps_pv"][:, ob + 64:ob + 128],
                        q_fm[base:base + 64, pr, i * 128:(i + 1) * 128],
                        kv_b[base:base + 64, pr, :],
                        start=True, stop=True))
                    tensor.wait_ge(p.sv.h, svc[("kl", pr, i, hh)])
                    if prev2 is not None:
                        tensor.wait_ge(p.sv.h, svc[("kf",) + prev2])
                    p.st.inc(nc.tensor.matmul(
                        g["ps_pv"][0:64, kb:kb + 64],
                        g["ptb"][1][:, 0:64],
                        v_tm[:, i, h * 64:(h + 1) * 64],
                        start=True, stop=True))

    @block.vector
    def _(vector):
        vector.wait_ge(p.ss.h, ss_m0 + 3 * NH)
        for pr in range(8):
            for b in range(2):
                vector.wait_ge(p.st.h, stc[("kT", pr)] - 1 + b)
                p.sv.inc(vector.tensor_copy(kpair[:, b * 1024:(b + 1) * 1024],
                                            g["ps_tp"][:]))
            for i in range(NT):
                for hh in range(2):
                    h = 2 * pr + hh
                    base = hh * 64
                    vector.wait_ge(p.st.h, stc[("S", pr, i, hh)])
                    p.sv.inc(vector.tensor_tensor(
                        g["ptb"][0][:, 0:128],
                        g["ps_s"][(i * 2 + hh) % 2][:, 0:128],
                        maskz[:, h, :], ALU.mult))
                    prev_it = LIGHT_ORDER.get((pr, i, hh))
                    if prev_it is not None:
                        vector.wait_ge(p.st.h, stc[("kv",) + prev_it])
                    p.sv.inc(vector.tensor_scalar(
                        g["ptb"][1][:, 0:64],
                        k_tm[:, i, h * 64:(h + 1) * 64],
                        lamk[:, h:h + 1], None, ALU.mult))
                    nlin = pr * 32 + i * 2 + hh
                    ob = (nlin % 2) * 512
                    kb = 1024 + (nlin % 2) * 512
                    vector.wait_ge(p.st.h, stc[("o", pr, i, hh)])
                    p.sv.inc(vector.tensor_scalar(
                        o_tm[:, i, h * 64:(h + 1) * 64],
                        g["ps_pv"][:, ob + 64:ob + 128], lamq[:, h:h + 1],
                        None, ALU.mult))
                    vector.wait_ge(p.sv.h, p.sv.n)
                    p.sv.inc(vector.tensor_add(
                        o_tm[:, i, h * 64:(h + 1) * 64],
                        o_tm[:, i, h * 64:(h + 1) * 64],
                        g["ps_pv"][:, ob:ob + 64]))
                    vector.wait_ge(p.st.h, stc[("kv", pr, i, hh)])
                    p.sv.inc(vector.scalar_tensor_tensor(
                        kv_f[base:base + 64, pr, :],
                        kv_f[base:base + 64, pr, :], lamc[h],
                        g["ps_pv"][0:64, kb:kb + 64], ALU.mult, ALU.add))
                    vector.wait_ge(p.sv.h, p.sv.n)
                    p.sv.inc(vector.tensor_copy(kv_b[base:base + 64, pr, :],
                                                kv_f[base:base + 64, pr, :]))
    assert p.st.n == st_c, (p.st.n, st_c)
    assert p.sv.n == sv_c2, (p.sv.n, sv_c2)

    sm, sq = g["sm"], g["sq"]
    ss_s0 = p.ss.n
    sv_s0 = p.sv.n

    @block.scalar
    def _(scalar):
        for i in range(NT):
            scalar.wait_ge(p.sv.h, svc[("oe", pr, i, 1)] if False else sv_c2)
            if i >= 1:
                scalar.wait_ge(p.ss.h, ss_s0 + i)
            p.ss.inc(scalar.activation(og_t[0][:], o_tm[:, i, :], AF.Square,
                                       accum_out=sq[:, i:i + 1]))
    ss_sq = p.ss.n

    @block.vector
    def _(vector):
        for i in range(NT):
            p.sv.inc(vector.reduce_sum(sm[:, i:i + 1].unsqueeze(2),
                                       o_tm[:, i, :].unsqueeze(1), AX.X))
        vector.wait_ge(p.ss.h, ss_sq)
        vector.wait_ge(p.sv.h, p.sv.n)
        p.sv.inc(vector.tensor_scalar(g["mean"][:], sm[:], 1.0 / D, None, ALU.mult))
        p.sv.inc(vector.tensor_scalar(g["var"][:], sq[:], 1.0 / D, None, ALU.mult))
        vector.wait_ge(p.sv.h, p.sv.n)
        p.sv.inc(vector.tensor_mul(sm[:], g["mean"][:], g["mean"][:]))
        vector.wait_ge(p.sv.h, p.sv.n)
        p.sv.inc(vector.tensor_sub(g["var"][:], sm[:], g["var"][:]))
    sv_var = p.sv.n

    @block.scalar
    def _(scalar):
        scalar.wait_ge(p.sv.h, sv_var)
        p.ss.inc(scalar.activation(g["rstd"][:], g["var"][:], AF.Ln, scale=-1.0,
                                   bias=g["ceps"][:]))
        scalar.wait_ge(p.ss.h, p.ss.n)
        p.ss.inc(scalar.activation(g["rstd"][:], g["rstd"][:], AF.Exp, scale=-0.5))
    ss_rstd = p.ss.n

    @block.vector
    def _(vector):
        vector.wait_ge(p.ss.h, ss_rstd)
        p.sv.inc(vector.scalar_tensor_tensor(g["nmrs"][:], g["mean"][:], -1.0,
                                             g["rstd"][:], ALU.mult, ALU.mult))
    sv_stats = p.sv.n

    stg = {}; ssg = {}; svg = {}
    st_c = p.st.n; ss_c = p.ss.n; sv_c3 = p.sv.n
    for i in range(NT):
        st_c += 2; stg[("g2", i)] = st_c
        st_c += 1; stg[("ogT", i)] = st_c
    for i in range(NT):
        ss_c += 2; ssg[("sig", i)] = ss_c
        ss_c += 1; ssg[("on", i)] = ss_c
    for i in range(NT):
        sv_c3 += 3; svg[("og", i)] = sv_c3
        sv_c3 += 1; svg[("ogTe", i)] = sv_c3

    @block.tensor
    def _(tensor):
        wwait(p, tensor, wo_rdy, 0)
        for i in range(NT):
            if i >= 1:
                tensor.wait_ge(p.sv.h, svg[("og", i - 1)])
            for half in range(2):
                p.st.inc(nc.tensor.matmul(
                    g["ps_s"][half][:],
                    g1_fm[0:64, i * 128:(i + 1) * 128],
                    g["wbuf"][1][0:64, 5, half * 512:(half + 1) * 512],
                    start=True, stop=True))
            tensor.wait_ge(p.sv.h, svg[("og", i)])
            if i >= 2:
                tensor.wait_ge(p.sv.h, svg[("ogTe", i - 2)] + 0)
            ins = None
            for c in range(NCH):
                ins = nc.tensor.transpose(g["ps_tp"][:, c * 128:(c + 1) * 128],
                                          og_t[i % 2][:, c * 128:(c + 1) * 128],
                                          g["ident"][:])
            p.st.inc(ins)

    @block.scalar
    def _(scalar):
        scalar.wait_ge(p.sv.h, sv_stats)
        for i in range(NT):
            scalar.wait_ge(p.st.h, stg[("g2", i)])
            p.ss.inc(scalar.activation(g["ps_s"][0][:], g["ps_s"][0][:], AF.Sigmoid))
            p.ss.inc(scalar.activation(g["ps_s"][1][:], g["ps_s"][1][:], AF.Sigmoid))
            if i >= 1:
                scalar.wait_ge(p.st.h, stg[("ogT", i - 1)])
            p.ss.inc(scalar.activation(og_t[i % 2][:], o_tm[:, i, :], AF.Identity,
                                       bias=g["nmrs"][:, i:i + 1],
                                       scale=g["rstd"][:, i:i + 1]))

    @block.vector
    def _(vector):
        for i in range(NT):
            vector.wait_ge(p.ss.h, ssg[("on", i)])
            p.sv.inc(vector.tensor_mul(og_t[i % 2][:], og_t[i % 2][:], gb_bc[:]))
            vector.wait_ge(p.sv.h, p.sv.n)
            p.sv.inc(vector.tensor_add(og_t[i % 2][:], og_t[i % 2][:], bb_bc[:]))
            vector.wait_ge(p.sv.h, p.sv.n)
            for half in range(2):
                ins = vector.tensor_tensor(
                    og_t[i % 2][:, half * 512:(half + 1) * 512],
                    og_t[i % 2][:, half * 512:(half + 1) * 512],
                    g["ps_s"][half][:], ALU.mult)
            p.sv.inc(ins)
            vector.wait_ge(p.st.h, stg[("ogT", i)])
            p.sv.inc(vector.tensor_copy(
                og_fm[:, :, i * 128:(i + 1) * 128],
                g["ps_tp"][:].rearrange("p (c t) -> p c t", c=NCH)))
    assert p.st.n == st_c
    assert p.ss.n == ss_c
    assert p.sv.n == sv_c3, (p.sv.n, sv_c3)
    sv_ogfm = p.sv.n

    st_o0 = p.st.n
    sv_o0 = p.sv.n
    ld_v = {}
    ss_entry = p.ss.n

    @block.tensor
    def _(tensor):
        tensor.wait_ge(p.sv.h, sv_ogfm)
        for ih in range(NT * 2):
            i, half = divmod(ih, 2)
            pb = g["ps_s"][ih % 2]
            if ih >= 2:
                tensor.wait_ge(p.sv.h, sv_o0 + ih - 1)
            ins = None
            for c in range(NCH):
                ins = nc.tensor.matmul(pb[:],
                                       og_fm[:, c, i * 128:(i + 1) * 128],
                                       g["wbuf"][0][:, c, half * 512:(half + 1) * 512],
                                       start=(c == 0), stop=(c == NCH - 1))
            p.st.inc(ins)

    @block.sync
    def _(sync):
        sync.wait_ge(p.ss.h, ss_entry)
        for i in range(NT):
            ld_v[i] = p.dx[i % 2].start(sync, g["xtile"][i % 2][:], xd[i])
            if i >= 1:
                sync.wait_ge(p.sv.h, sv_o0 + 2 * i)
                p.dx[(i - 1) % 2].start(sync, xd[i - 1], g["xtile"][(i - 1) % 2][:])
        sync.wait_ge(p.sv.h, sv_o0 + 2 * NT)
        p.dx[(NT - 1) % 2].start(sync, xd[NT - 1], g["xtile"][(NT - 1) % 2][:])

    @block.vector
    def _(vector):
        for ih in range(NT * 2):
            i, half = divmod(ih, 2)
            vector.wait_ge(p.st.h, st_o0 + ih + 1)
            vector.wait_ge(p.dx[i % 2].h, ld_v[i])
            sl = g["xtile"][i % 2][:, half * 512:(half + 1) * 512]
            p.sv.inc(vector.tensor_add(sl, sl, g["ps_s"][ih % 2][:]))


def lin_rev_h(hidx):
    return divmod(hidx, 2)



_PROG = None


def _constants():
    ident = np.eye(128).astype(ml_dtypes.bfloat16)
    triT = np.triu(np.ones((128, 128))).astype(ml_dtypes.bfloat16)
    a = np.arange(128)
    dmatTc = np.maximum(a[None, :] - a[:, None], 0).astype(np.float32)
    idxp1 = (a + 1.0).astype(np.float32).reshape(128, 1)
    idxrv = (127.0 - a).astype(np.float32).reshape(128, 1)
    import jax
    import jax.numpy as jnp
    with jax.default_device(jax.devices("cpu")[0]):
        inv_freq = 1.0 / 10000.0 ** (jnp.arange(0, 128, 2, dtype=jnp.float32) / 128)
        t = jnp.arange(S, dtype=jnp.float32)
        freqs = t[:, None] * inv_freq[None, :]
        emb = jnp.concatenate([freqs, freqs], axis=-1)
        rc = np.ascontiguousarray(np.asarray(jnp.cos(emb), np.float32))
        rs = np.ascontiguousarray(np.asarray(jnp.sin(emb), np.float32))
    return {"c_ident": ident, "c_triT": triT, "c_dmatTc": dmatTc,
            "c_idxp1": idxp1, "c_idxrv": idxrv,
            "c_ropecos": rc, "c_ropesin": rs}


def _base_map(inputs):
    consts = _constants()
    x_full = np.ascontiguousarray(inputs["L0_attn_a_x"], dtype=np.float32)
    ca = lambda k: np.ascontiguousarray(inputs[k], np.float32)
    base = {
        "L0A_Wq": ca("L0_attn_a_Wq"), "L0A_Wk": ca("L0_attn_a_Wk"),
        "L0A_Wv": ca("L0_attn_a_Wv"), "L0A_Wo": ca("L0_attn_a_Wo"),
        "L0A_bo": ca("L0_attn_a_bo"),
        "L0F_W": ca("L0_ffn_W"), "L0F_Wp": ca("L0_ffn_Wp"),
        "L0F_bp": ca("L0_ffn_bp"), "L0F_sim": ca("L0_ffn_sim"),
        "L0F_temp": ca("L0_ffn_temp"),
    }
    base["L0F_Wd_all"] = np.ascontiguousarray(np.concatenate(
        [inputs[f"L0_ffn_Wd{e}"] for e in range(4)], axis=0), np.float32)
    base["L0F_Wu_all"] = np.ascontiguousarray(np.concatenate(
        [inputs[f"L0_ffn_Wu{e}"] for e in range(4)], axis=1), np.float32)
    base["L0B_qkv_w"] = ca("L0_attn_b_qkv_w")
    base["L0B_qkv_b"] = ca("L0_attn_b_qkv_b")
    base["L0B_o_w"] = ca("L0_attn_b_o_w")
    base["L0B_o_b"] = ca("L0_attn_b_o_b")
    for nm in ["Wq", "Wk", "Wv", "Wo", "gamma", "beta", "Wg1", "Wg2"]:
        base[f"L1A_{nm}"] = ca(f"L1_attn_a_{nm}")
    base["L1F_Wg"] = ca("L1_ffn_w_gate")
    base["L1F_Wu"] = ca("L1_ffn_w_up")
    base["L1F_Wd"] = ca("L1_ffn_w_down")
    for nm in ["wq", "bq", "wqc", "bqc", "wqr", "bqr", "wkv", "bkv", "wkr",
               "bkr", "wkc", "bkc", "wv", "bv", "wo", "bo"]:
        base[f"L1B_{nm}"] = ca(f"L1_attn_b_{nm}")
    base.update(consts)
    return base, x_full


def kernel(**inputs):
    global _PROG
    if _PROG is None:
        _PROG = build_program()
    nc = _PROG

    base, x_full = _base_map(inputs)
    in_maps = []
    for c in range(8):
        m = dict(base)
        m["x"] = np.ascontiguousarray(x_full[c])
        in_maps.append(m)
    res = run_bass_kernel_spmd(nc, in_maps, core_ids=list(range(8)))
    out = np.stack([np.asarray(r["y"], np.float32) for r in res.results], axis=0)
    return out


# version 12
# speedup vs baseline: 1.2253x; 1.2253x over previous
from contextlib import ExitStack

import numpy as np
import ml_dtypes

import concourse.bass as bass
import concourse.mybir as mybir
from concourse.bass_utils import run_bass_kernel_spmd

F32 = mybir.dt.float32
BF16 = mybir.dt.bfloat16
AF = mybir.ActivationFunctionType
ALU = mybir.AluOpType
AX = mybir.AxisListType

S = 2048
D = 1024
NT = 16
NCH = 8
NH = 16
HD = 64

N_SUBLAYERS = 8


class Ctr:
    def __init__(self, stack, nc, name):
        self.h = stack.enter_context(nc.semaphore(name))
        self.n = 0

    def inc(self, instr, k=1):
        instr.then_inc(self.h, k)
        self.n += k
        return self.n


class DmaStream:

    def __init__(self, stack, nc, name, serialize=True):
        self.h = stack.enter_context(nc.semaphore(name))
        self.n = 0
        self.serialize = serialize

    def start(self, eng, out, in_, **kw):
        if self.serialize and self.n:
            eng.wait_ge(self.h, self.n)
        eng.dma_start(out=out, in_=in_, **kw).then_inc(self.h, 16)
        self.n += 16
        return self.n


class Arena:

    def __init__(self, p, kb):
        self.t = p.sb("arena", [128, kb * 512], BF16)
        self.off = 0

    def reset(self):
        self.off = 0

    def _take(self, n_bf16):
        pad = (n_bf16 + 15) // 16 * 16
        ap = self.t[:, self.off:self.off + n_bf16]
        self.off += pad
        assert self.off <= self.t.shape[1], f"arena overflow {self.off}"
        return ap

    def bf16(self, shape):
        assert shape[0] == 128
        n = int(np.prod(shape[1:]))
        ap = self._take(n)
        if len(shape) == 2:
            return ap
        return ap.rearrange("p (" + " ".join(f"d{i}" for i in range(1, len(shape))) + ") -> p " +
                            " ".join(f"d{i}" for i in range(1, len(shape))),
                            **{f"d{i}": shape[i] for i in range(1, len(shape))})

    def f32(self, shape):
        assert shape[0] == 128
        n = int(np.prod(shape[1:]))
        ap = self._take(2 * n).bitcast(F32)
        if len(shape) == 2:
            return ap
        return ap.rearrange("p (" + " ".join(f"d{i}" for i in range(1, len(shape))) + ") -> p " +
                            " ".join(f"d{i}" for i in range(1, len(shape))),
                            **{f"d{i}": shape[i] for i in range(1, len(shape))})

    def i8(self, shape):
        assert shape[0] == 128 and len(shape) == 2
        n = shape[1]
        return self._take((n + 1) // 2).bitcast(mybir.dt.int8)[:, 0:n]


class EngProxy:
    __slots__ = ("_eng", "_seen")

    def __init__(self, eng, seen):
        object.__setattr__(self, "_eng", eng)
        object.__setattr__(self, "_seen", seen)

    def wait_ge(self, sem, n):
        k = id(sem)
        if self._seen.get(k, -1) >= n:
            return
        self._seen[k] = n
        return self._eng.wait_ge(sem, n)

    def __getattr__(self, a):
        return getattr(self._eng, a)


class BlockProxy:
    def __init__(self, block, seen_all):
        self._block = block
        self._seen_all = seen_all

    def _wrap(self, name):
        return getattr(self._block, name)

    @property
    def vector(self):
        return self._wrap("vector")

    @property
    def scalar(self):
        return self._wrap("scalar")

    @property
    def tensor(self):
        return self._wrap("tensor")

    @property
    def sync(self):
        return self._wrap("sync")

    @property
    def gpsimd(self):
        return self._wrap("gpsimd")


class P:
    def __init__(self, nc, stack, block):
        self.nc = nc
        self.stack = stack
        self.seen = {k: {} for k in ("vector", "scalar", "tensor", "sync",
                                     "gpsimd")}
        self.block = BlockProxy(block, self.seen)
        self.sv = Ctr(stack, nc, "sv")
        self.ss = Ctr(stack, nc, "ss")
        self.st = Ctr(stack, nc, "st")
        self.sg = Ctr(stack, nc, "sg")
        self.dmisc = DmaStream(stack, nc, "dmisc", serialize=False)
        self.dx = [DmaStream(stack, nc, f"dx{i}") for i in range(2)]
        self.dw = [DmaStream(stack, nc, f"dw{i}") for i in range(2)]
        self.dlg = DmaStream(stack, nc, "dlg")
        self.dlr = [DmaStream(stack, nc, f"dlr{i}") for i in range(2)]
        self.dws = [DmaStream(stack, nc, f"dws{i}") for i in range(4)]
        self.dp = [DmaStream(stack, nc, f"dp{i}") for i in range(4)]
        self.dps = DmaStream(stack, nc, "dps")

    def sb(self, name, shape, dtype):
        return self.stack.enter_context(self.nc.sbuf_tensor(name, shape, dtype))

    def load_par(self, eng, dst, src, n=4):
        C = dst.shape[1]
        n = min(n, C)
        step = (C + n - 1) // n
        wr = []
        for i in range(n):
            c0, c1 = i * step, min(C, (i + 1) * step)
            if c0 >= c1:
                break
            s = self.dws[i % 4]
            wr.append((s.h, s.start(eng, dst[:, c0:c1], src[:, c0:c1])))
        return wr

    def ps(self, name, shape, dtype=F32):
        return self.stack.enter_context(self.nc.psum_tensor(name, shape, dtype))


def wwait(p, eng, w_ready, slot=None):
    if isinstance(w_ready, (list, tuple)):
        for s_, c_ in w_ready:
            eng.wait_ge(s_, c_)
    else:
        eng.wait_ge(p.dw[slot].h, w_ready)


def build_program():
    nc = bass.Bass("TRN2", target_bir_lowering=False)

    x_in = nc.dram_tensor("x", [S, D], F32, kind="ExternalInput").ap()
    y_out = nc.dram_tensor("y", [S, D], F32, kind="ExternalOutput").ap()
    xd = nc.dram_tensor("xd", [NT, 128, D], F32, kind="Internal").ap()

    w = {}
    shapes = [
        ("L0A_Wq", [D, D]), ("L0A_Wk", [D, D]), ("L0A_Wv", [D, D]),
        ("L0A_Wo", [D, D]), ("L0A_bo", [D]),
        ("L0F_W", [D, D]), ("L0F_Wp", [256, D]), ("L0F_bp", [256]),
        ("L0F_sim", [256, 4]), ("L0F_temp", [1]),
    ]
    shapes += [("L0F_Wd_all", [15, D]), ("L0F_Wu_all", [D, 15])]
    shapes += [("L0B_qkv_w", [D, 3 * D]), ("L0B_qkv_b", [3 * D]),
               ("L0B_o_w", [D, D]), ("L0B_o_b", [D])]
    shapes += [("L1A_Wq", [D, D]), ("L1A_Wk", [D, D]), ("L1A_Wv", [D, D]),
               ("L1A_Wo", [D, D]), ("L1A_gamma", [D]), ("L1A_beta", [D]),
               ("L1A_Wg1", [D, 64]), ("L1A_Wg2", [64, D])]
    shapes += [("L1F_Wg", [2816, D]), ("L1F_Wu", [2816, D]), ("L1F_Wd", [D, 2816])]
    shapes += [("L1B_wq", [D, 256]), ("L1B_bq", [256]),
               ("L1B_wqc", [256, 896]), ("L1B_bqc", [896]),
               ("L1B_wqr", [D, 128]), ("L1B_bqr", [128]),
               ("L1B_wkv", [D, 256]), ("L1B_bkv", [256]),
               ("L1B_wkr", [D, 128]), ("L1B_bkr", [128]),
               ("L1B_wkc", [256, 896]), ("L1B_bkc", [896]),
               ("L1B_wv", [256, D]), ("L1B_bv", [D]),
               ("L1B_wo", [D, D]), ("L1B_bo", [D]),
               ("c_ropecos", [S, 128]), ("c_ropesin", [S, 128])]
    for name, shape in shapes:
        w[name] = nc.dram_tensor(name, shape, F32, kind="ExternalInput").ap()
    w["wgT_c"] = nc.dram_tensor("wgT_c", [128, NCH, 2816], BF16, kind="Internal").ap()
    w["wuT_c"] = nc.dram_tensor("wuT_c", [128, NCH, 2816], BF16, kind="Internal").ap()
    w["wdT_c"] = nc.dram_tensor("wdT_c", [128, 22, 1024], BF16, kind="Internal").ap()

    c_ident = nc.dram_tensor("c_ident", [128, 128], BF16, kind="ExternalInput").ap()
    c_triT = nc.dram_tensor("c_triT", [128, 128], BF16, kind="ExternalInput").ap()
    w["c_dmatTc"] = nc.dram_tensor("c_dmatTc", [128, 128], F32, kind="ExternalInput").ap()
    w["c_idxp1"] = nc.dram_tensor("c_idxp1", [128, 1], F32, kind="ExternalInput").ap()
    w["c_idxrv"] = nc.dram_tensor("c_idxrv", [128, 1], F32, kind="ExternalInput").ap()

    with ExitStack() as stack:
        block = stack.enter_context(nc.Block())
        p = P(nc, stack, block)
        emit(p, x_in, y_out, xd, w, c_ident, c_triT)
    return nc


def emit(p, x_in, y_out, xd, w, c_ident, c_triT):
    nc, block = p.nc, p.block

    g = {}
    g["ident"] = p.sb("ident", [128, 128], BF16)
    g["triT"] = p.sb("triT", [128, 128], BF16)
    g["onescol"] = p.sb("onescol", [1, 128], BF16)
    g["sm"] = p.sb("sm", [128, NT], F32)
    g["sq"] = p.sb("sq", [128, NT], F32)
    g["mean"] = p.sb("mean", [128, NT], F32)
    g["var"] = p.sb("var", [128, NT], F32)
    g["rstd"] = p.sb("rstd", [128, NT], F32)
    g["nmrs"] = p.sb("nmrs", [128, NT], F32)
    g["xtile"] = [p.sb(f"xtile{i}", [128, 1024], F32) for i in range(2)]
    g["xn_tm"] = p.sb("xn_tm", [128, 2, 1024], BF16)
    g["xn_fm"] = p.sb("xn_fm", [128, NCH, S], BF16)
    g["ptb"] = [p.sb(f"ptb{i}", [128, 512], BF16) for i in range(3)]
    g["attn_fm"] = g["xn_fm"]
    _sq2 = p.sb("sq2_0", [128, 1024], BF16)
    g["sq2"] = [_sq2, _sq2]
    g["segr"] = p.sb("segr", [128, NH], F32)
    g["hmax"] = p.sb("hmax", [128, NH], F32)
    g["red1"] = p.sb("red1", [128, 2], F32)
    g["redbf"] = p.sb("redbf", [128, 2], BF16)
    g["redT"] = p.sb("redT", [1, 128], F32)
    g["csc"] = p.sb("csc", [1, 8], F32)
    g["negc8b"] = p.sb("negc8b", [1, 1], BF16)
    g["negc8"] = p.sb("negc8", [128, 1], F32)
    g["ceps"] = p.sb("ceps", [128, 1], F32)
    g["biasrow"] = p.sb("biasrow", [1, 4 * 1024], BF16)
    g["wbuf"] = [p.sb(f"wbuf{i}", [128, NCH, 1024], BF16) for i in range(2)]
    g["ps_e"] = p.ps("ps_e", [128, 512], F32)
    g["ps_s"] = [p.ps(f"ps_s{i}", [128, 512], F32) for i in range(2)]
    p.arena = Arena(p, 116)
    g["ps_pv"] = p.ps("ps_pv", [128, S], F32)
    g["ps_tp"] = p.ps("ps_tp", [128, 1024], BF16)
    g["dmatTc"] = p.sb("dmatTc", [128, 128], F32)
    g["idxp1"] = p.sb("idxp1", [128, 1], F32)
    g["idxrv"] = p.sb("idxrv", [128, 1], F32)

    @block.sync
    def _(sync):
        p.dmisc.start(sync, g["ident"][:], c_ident)
        p.dmisc.start(sync, g["triT"][:], c_triT)
        p.dmisc.start(sync, g["dmatTc"][:], w["c_dmatTc"])
        p.dmisc.start(sync, g["idxp1"][:], w["c_idxp1"])
        p.dmisc.start(sync, g["idxrv"][:], w["c_idxrv"])
        for i in range(NT):
            p.dmisc.start(sync, xd[i], x_in[i * 128:(i + 1) * 128, :])
        p.dq_consts = p.dmisc.n
    dq_x = p.dmisc.n

    @block.vector
    def _(vector):
        vector.memset(g["onescol"][:], 1.0)
        p.sv.inc(vector.memset(g["ceps"][:], 1e-5))

    rr = lambda t: (lambda: t.rearrange("(c p) o -> p c o", p=128))
    rowv = lambda t: (lambda: t.rearrange("(a d) -> a d", a=1))
    if N_SUBLAYERS >= 1:
        emit_ln(p, g, xd, dq_ready=dq_x)
        emit_mha(p, g, xd,
                 wq_src=rr(w["L0A_Wq"]), wk_src=rr(w["L0A_Wk"]),
                 wv_src=rr(w["L0A_Wv"]), wo_src=rr(w["L0A_Wo"]),
                 qkvb_src=None, ob_src=rowv(w["L0A_bo"]), scrambled=True)
    def do_ffn0():
        emit_barrier(p)
        emit_ln(p, g, xd)
        emit_ffn0(p, g, xd, W=w["L0F_W"], Wp=w["L0F_Wp"], bp=w["L0F_bp"],
                  sim=w["L0F_sim"], temp=w["L0F_temp"],
                  Wd=w["L0F_Wd_all"], Wu=w["L0F_Wu_all"])

    if N_SUBLAYERS >= 2:
        do_ffn0()
    if N_SUBLAYERS >= 3:
        emit_barrier(p)
        emit_ln(p, g, xd)
        qkv = w["L0B_qkv_w"].rearrange("(c p) o -> p c o", p=128)
        emit_mha(p, g, xd,
                 wq_src=lambda: qkv[:, :, 0:D], wk_src=lambda: qkv[:, :, D:2 * D],
                 wv_src=lambda: qkv[:, :, 2 * D:3 * D], wo_src=rr(w["L0B_o_w"]),
                 qkvb_src=rowv(w["L0B_qkv_b"]), ob_src=rowv(w["L0B_o_b"]),
                 scrambled=False)
    if N_SUBLAYERS >= 4:
        do_ffn0()
    if N_SUBLAYERS >= 5:
        emit_barrier(p)
        emit_ln(p, g, xd)
        emit_lightning(p, g, xd, Wq=w["L1A_Wq"], Wk=w["L1A_Wk"], Wv=w["L1A_Wv"],
                       Wo=w["L1A_Wo"], gamma=w["L1A_gamma"], beta=w["L1A_beta"],
                       Wg1=w["L1A_Wg1"], Wg2=w["L1A_Wg2"])

    def do_ffn1(first):
        emit_barrier(p)
        lnst = emit_ln(p, g, xd, stats_only=True)
        emit_ffn1(p, g, xd, Wg=w["L1F_Wg"], Wu=w["L1F_Wu"], Wd=w["L1F_Wd"],
                  cch=dict(wgT=w["wgT_c"], wuT=w["wuT_c"], wdT=w["wdT_c"]),
                  lnst=lnst, first=first)

    if N_SUBLAYERS >= 6:
        do_ffn1(first=True)
    if N_SUBLAYERS >= 7:
        emit_barrier(p)
        emit_ln(p, g, xd)
        emit_mla(p, g, xd,
                 W={k: w[f"L1B_{k}"] for k in
                    ["wq", "bq", "wqc", "bqc", "wqr", "bqr", "wkv", "bkv",
                     "wkr", "bkr", "wkc", "bkc", "wv", "bv", "wo", "bo"]},
                 cos_d=w["c_ropecos"], sin_d=w["c_ropesin"])
    if N_SUBLAYERS >= 8:
        do_ffn1(first=False)

    @block.sync
    def _(sync):
        sync.wait_ge(p.sv.h, p.sv.n)
        sync.wait_ge(p.ss.h, p.ss.n)
        for st_ in p.dx:
            sync.wait_ge(st_.h, st_.n)
        sync.wait_ge(p.dmisc.h, p.dmisc.n)
        for i in range(NT):
            p.dmisc.start(sync, y_out[i * 128:(i + 1) * 128, :], xd[i])
        sync.wait_ge(p.dmisc.h, p.dmisc.n)



def emit_ln(p, g, xd, dq_ready=None, stats_only=False):
    nc, block = p.nc, p.block
    xt = g["xtile"]
    nb = len(xt)
    sm, sq = g["sm"], g["sq"]

    ss0, sv0 = p.ss.n, p.sv.n
    dqv = {}

    @block.sync
    def _(sync):
        if dq_ready is not None:
            sync.wait_ge(p.dmisc.h, dq_ready)
        for i in range(NT):
            if i >= nb:
                sync.wait_ge(p.ss.h, ss0 + (i - nb + 1))
                sync.wait_ge(p.sv.h, sv0 + (i - nb + 1))
            dqv[("a", i)] = p.dx[i % nb].start(sync, xt[i % nb][:], xd[i])

    @block.scalar
    def _(scalar):
        for i in range(NT):
            scalar.wait_ge(p.dx[i % nb].h, dqv[("a", i)])
            if i >= 1:
                scalar.wait_ge(p.ss.h, ss0 + i)
            p.ss.inc(scalar.activation(g["sq2"][0][:], xt[i % nb][:], AF.Square,
                                       accum_out=sq[:, i:i + 1]))
    ss_sq = p.ss.n

    @block.vector
    def _(vector):
        for i in range(NT):
            vector.wait_ge(p.dx[i % nb].h, dqv[("a", i)])
            p.sv.inc(vector.reduce_sum(sm[:, i:i + 1].unsqueeze(2),
                                       xt[i % nb][:].rearrange("p (a b) -> p a b", a=1),
                                       AX.X))
        vector.wait_ge(p.ss.h, ss_sq)
        vector.wait_ge(p.sv.h, sv0 + NT)
        p.sv.inc(vector.tensor_scalar(g["mean"][:], sm[:], 1.0 / D, None, ALU.mult))
        p.sv.inc(vector.tensor_scalar(g["var"][:], sq[:], 1.0 / D, None, ALU.mult))
        vector.wait_ge(p.sv.h, p.sv.n)
        p.sv.inc(vector.tensor_mul(sm[:], g["mean"][:], g["mean"][:]))
        vector.wait_ge(p.sv.h, p.sv.n)
        p.sv.inc(vector.tensor_sub(g["var"][:], sm[:], g["var"][:]))
    sv_var = p.sv.n

    @block.scalar
    def _(scalar):
        scalar.wait_ge(p.sv.h, sv_var)
        p.ss.inc(scalar.activation(g["rstd"][:], g["var"][:], AF.Ln, scale=-1.0,
                                   bias=g["ceps"][:]))
        scalar.wait_ge(p.ss.h, p.ss.n)
        p.ss.inc(scalar.activation(g["rstd"][:], g["rstd"][:], AF.Exp, scale=-0.5))
    ss_rstd = p.ss.n

    @block.vector
    def _(vector):
        vector.wait_ge(p.ss.h, ss_rstd)
        p.sv.inc(vector.scalar_tensor_tensor(g["nmrs"][:], g["mean"][:], -1.0, g["rstd"][:],
                                             ALU.mult, ALU.mult))
    sv_stats = p.sv.n

    if stats_only:
        return dict(ss_sq=ss_sq, sv_red16=sv0 + NT, sv_stats=sv_stats)

    ss1 = p.ss.n
    sv_red16 = sv0 + NT

    @block.sync
    def _(sync):
        sync.wait_ge(p.ss.h, ss_sq)
        sync.wait_ge(p.sv.h, sv_red16)
        for i in range(NT):
            if i >= nb:
                sync.wait_ge(p.ss.h, ss1 + (i - nb + 1))
            dqv[("b", i)] = p.dx[i % nb].start(sync, xt[i % nb][:], xd[i])

    st0 = p.st.n

    @block.scalar
    def _(scalar):
        scalar.wait_ge(p.sv.h, sv_stats)
        for i in range(NT):
            scalar.wait_ge(p.dx[i % nb].h, dqv[("b", i)])
            if i >= 2:
                scalar.wait_ge(p.st.h, st0 + i - 1)
            p.ss.inc(scalar.activation(g["xn_tm"][:, i % 2, :], xt[i % nb][:], AF.Identity,
                                       bias=g["nmrs"][:, i:i + 1],
                                       scale=g["rstd"][:, i:i + 1]))

    sv1 = p.sv.n

    @block.tensor
    def _(tensor):
        for i in range(NT):
            tensor.wait_ge(p.ss.h, ss1 + i + 1)
            if i >= 1:
                tensor.wait_ge(p.sv.h, sv1 + i)
            ins = None
            for c in range(NCH):
                ins = nc.tensor.transpose(g["ps_tp"][:, c * 128:(c + 1) * 128],
                                          g["xn_tm"][:, i % 2, c * 128:(c + 1) * 128],
                                          g["ident"][:])
            p.st.inc(ins)

    @block.vector
    def _(vector):
        for i in range(NT):
            vector.wait_ge(p.st.h, st0 + i + 1)
            p.sv.inc(vector.tensor_copy(
                g["xn_fm"][:, :, i * 128:(i + 1) * 128],
                g["ps_tp"][:].rearrange("p (c t) -> p c t", c=NCH)))
    p.sv_xn = p.sv.n


def emit_proj_tm(p, g, wslot, dst_fn, src_fn, w_ready, add_bias=False,
                 bias_off=0, sv_pre=0, b_ready=0):
    nc, block = p.nc, p.block
    st0, sv0 = p.st.n, p.sv.n

    @block.tensor
    def _(tensor):
        wwait(p, tensor, w_ready, wslot)
        if add_bias and b_ready:
            tensor.wait_ge(p.dw[1].h, b_ready)
        tensor.wait_ge(p.sv.h, max(p.sv_xn, sv0, sv_pre))
        for ih in range(NT * 2):
            i, half = divmod(ih, 2)
            pb = g["ps_s"][ih % 2]
            if ih >= 2:
                tensor.wait_ge(p.sv.h, sv0 + ih - 1)
            ins = None
            for c in range(NCH):
                ins = nc.tensor.matmul(pb[:],
                                       g["xn_fm"][:, c, i * 128:(i + 1) * 128],
                                       g["wbuf"][wslot][:, c, half * 512:(half + 1) * 512],
                                       start=(c == 0),
                                       stop=(c == NCH - 1 and not add_bias))
            if add_bias:
                ins = nc.tensor.matmul(pb[:], g["onescol"][:],
                                       g["biasrow"][:, bias_off + half * 512:
                                                    bias_off + (half + 1) * 512],
                                       start=False, stop=True)
            p.st.inc(ins)

    @block.vector
    def _(vector):
        for ih in range(NT * 2):
            i, half = divmod(ih, 2)
            vector.wait_ge(p.st.h, st0 + ih + 1)
            p.sv.inc(vector.tensor_copy(dst_fn(i, half), src_fn(g["ps_s"][ih % 2], i, half)))
    return p.st.n, p.sv.n


def emit_cbound_pre(p, g, src_tm, col, sv_ready=0):
    nc, block = p.nc, p.block
    ss0, sv0 = p.ss.n, p.sv.n

    @block.scalar
    def _(scalar):
        scalar.wait_ge(p.sv.h, max(sv_ready, sv0))
        for i in range(NT):
            if i >= 1:
                scalar.wait_ge(p.sv.h, sv0 + 1 + 2 * i)
            p.ss.inc(scalar.activation(g["sq2"][i % 2][:], src_tm[:, i, :], AF.Square))

    @block.vector
    def _(vector):
        vector.wait_ge(p.sv.h, sv0)
        p.sv.inc(vector.memset(g["hmax"][:], 0.0))
        for i in range(NT):
            vector.wait_ge(p.ss.h, ss0 + i + 1)
            p.sv.inc(vector.tensor_reduce(
                g["segr"][:].unsqueeze(2),
                g["sq2"][i % 2][:].rearrange("p (h d) -> p h d", d=HD), AX.X, ALU.add))
            vector.wait_ge(p.sv.h, p.sv.n)
            p.sv.inc(vector.tensor_max(g["hmax"][:], g["hmax"][:], g["segr"][:]))
            vector.wait_ge(p.sv.h, p.sv.n)
        p.sv.inc(vector.reduce_max(g["red1"][:, col:col + 1].unsqueeze(2),
                                   g["hmax"][:].rearrange("p (a b) -> p a b", a=1), AX.X))
        vector.wait_ge(p.sv.h, p.sv.n)
        p.sv.inc(vector.tensor_copy(g["redbf"][:, col:col + 1],
                                    g["red1"][:, col:col + 1]))
    return p.sv.n


def emit_cbound_fin(p, g, col, sv_red):
    nc, block = p.nc, p.block
    st0 = p.st.n

    @block.tensor
    def _(tensor):
        tensor.wait_ge(p.sv.h, sv_red)
        p.st.inc(nc.tensor.matmul(g["ps_e"][0:1, col * 128:col * 128 + 128],
                                  g["redbf"][:, col:col + 1],
                                  g["ident"][:], start=True, stop=True))
    st_t = p.st.n

    @block.vector
    def _(vector):
        vector.wait_ge(p.st.h, st_t)
        vector.wait_ge(p.sv.h, p.sv.n)
        p.sv.inc(vector.tensor_copy(g["redT"][:],
                                    g["ps_e"][0:1, col * 128:col * 128 + 128]))
        vector.wait_ge(p.sv.h, p.sv.n)
        p.sv.inc(vector.reduce_max(
            g["csc"][:, col:col + 1].unsqueeze(2),
            g["redT"][:].rearrange("p (a b) -> p a b", a=1), AX.X))
    return p.sv.n


def emit_mha(p, g, xd, wq_src, wk_src, wv_src, wo_src, qkvb_src, ob_src,
             scrambled):
    nc, block = p.nc, p.block
    ar = p.arena
    ar.reset()
    g["lall"] = ar.f32([128, S])
    g["lrec"] = g["lall"]
    g["lrow_bf"] = ar.bf16([128, 1024])
    g["q_tm"] = ar.bf16([128, NT, D])
    g["k_tm"] = ar.bf16([128, NT, D])
    g["v_tm"] = ar.bf16([128, NT, NH, HD + 1])
    g["qk_fm"] = ar.bf16([128, 2, S])

    @block.vector
    def _(vector):
        p.sv.inc(vector.memset(g["v_tm"][:], 1.0))
    sv_vm = p.sv.n

    @block.gpsimd
    def _(gpsimd):
        wq_rdy = p.load_par(gpsimd, g["wbuf"][0][:], wq_src())
        g["_obr"] = p.dw[1].start(gpsimd,
                                  g["biasrow"][:, 3 * D:4 * D].unsqueeze(1),
                                  ob_src())
        qb_rdy = 0
        if qkvb_src is not None:
            qb_rdy = p.dw[1].start(gpsimd, g["biasrow"][:, 0:3 * D].unsqueeze(1),
                                   qkvb_src())
        wk_rdy = p.load_par(gpsimd, g["wbuf"][1][:], wk_src())
        g["_wrdy"] = (wq_rdy, wk_rdy, qb_rdy)
    wq_rdy, wk_rdy, qb_rdy = g.pop("_wrdy")
    has_qkvb = qkvb_src is not None

    st_q, sv_q = emit_proj_tm(p, g, 0,
                              lambda i, h: g["q_tm"][:, i, h * 512:(h + 1) * 512],
                              lambda pb, i, h: pb[:], wq_rdy,
                              add_bias=has_qkvb, bias_off=0, b_ready=qb_rdy)

    @block.gpsimd
    def _(gpsimd):
        gpsimd.wait_ge(p.st.h, st_q)
        g["_wv"] = p.load_par(gpsimd, g["wbuf"][0][:], wv_src())
    wv_rdy = g.pop("_wv")

    rq = emit_cbound_pre(p, g, g["q_tm"], 0, sv_ready=sv_q)

    st_k, sv_k = emit_proj_tm(p, g, 1,
                              lambda i, h: g["k_tm"][:, i, h * 512:(h + 1) * 512],
                              lambda pb, i, h: pb[:], wk_rdy,
                              add_bias=has_qkvb, bias_off=D)

    @block.gpsimd
    def _(gpsimd):
        gpsimd.wait_ge(p.st.h, st_k)
        g["_wo"] = p.load_par(gpsimd, g["wbuf"][1][:], wo_src())
    wo_rdy = g.pop("_wo") + [(p.dw[1].h, g.pop("_obr"))]

    st_v, sv_v = emit_proj_tm(
        p, g, 0,
        lambda i, h: g["v_tm"][:, i, h * 8:(h + 1) * 8, 0:HD],
        lambda pb, i, h: pb[:].rearrange("p (a b) -> p a b", a=8), wv_rdy,
        add_bias=has_qkvb, bias_off=2 * D, sv_pre=sv_vm, b_ready=qb_rdy)

    rk = emit_cbound_pre(p, g, g["k_tm"], 1, sv_ready=sv_k)

    wos_rdy = None
    if scrambled:
        @block.gpsimd
        def _(gpsimd):
            gpsimd.wait_ge(p.st.h, st_v)
            wov = wo_src().rearrange("p c o -> p c o")
            a_ = p.dws[0].start(gpsimd, g["wbuf"][0][0:64, :, :],
                                wov[64:128, :, :])
            b_ = p.dws[1].start(gpsimd, g["wbuf"][0][64:128, :, :],
                                wov[0:64, :, :])
            g["_wos"] = [(p.dws[0].h, a_), (p.dws[1].h, b_)]
        wos_rdy = g.pop("_wos")

    emit_attn_core(p, g, xd, wo_rdy, wos_rdy, scrambled, sv_qkv=sv_v,
                   cb=(rq, rk))


def emit_attn_core(p, g, xd, wo_rdy, wos_rdy, scrambled, sv_qkv, cb=None):
    nc, block = p.nc, p.block

    if cb is None:
        rq = emit_cbound_pre(p, g, g["q_tm"], 0, sv_ready=sv_qkv)
        rk = emit_cbound_pre(p, g, g["k_tm"], 1, sv_ready=sv_qkv)
    else:
        rq, rk = cb
    emit_cbound_fin(p, g, 0, rq)
    sv_ck = emit_cbound_fin(p, g, 1, rk)

    @block.vector
    def _(vector):
        vector.wait_ge(p.sv.h, sv_ck)
        p.sv.inc(vector.tensor_tensor(g["csc"][:, 2:3], g["csc"][:, 0:1],
                                      g["csc"][:, 1:2], ALU.mult))
    sv_cm = p.sv.n

    @block.scalar
    def _(scalar):
        scalar.wait_ge(p.sv.h, sv_cm)
        p.ss.inc(scalar.activation(g["csc"][:, 3:4], g["csc"][:, 2:3], AF.Ln))
        scalar.wait_ge(p.ss.h, p.ss.n)
        p.ss.inc(scalar.activation(g["csc"][:, 4:5], g["csc"][:, 3:4], AF.Exp, scale=0.5))
    ss_cs = p.ss.n

    @block.vector
    def _(vector):
        vector.wait_ge(p.ss.h, ss_cs)
        p.sv.inc(vector.tensor_scalar(g["csc"][:, 5:6], g["csc"][:, 4:5], -1.02 / 8.0,
                                      None, ALU.mult))
        vector.wait_ge(p.sv.h, p.sv.n)
        p.sv.inc(vector.tensor_copy(g["negc8b"][:], g["csc"][:, 5:6]))
    sv_cb = p.sv.n
    st_cb0 = p.st.n

    @block.tensor
    def _(tensor):
        tensor.wait_ge(p.sv.h, sv_cb)
        p.st.inc(nc.tensor.matmul(g["ps_s"][1][0:128, 0:1], g["onescol"][:],
                                  g["negc8b"][:], start=True, stop=True))
    st_cb = p.st.n

    @block.vector
    def _(vector):
        vector.wait_ge(p.st.h, st_cb)
        p.sv.inc(vector.tensor_copy(g["negc8"][:], g["ps_s"][1][0:128, 0:1]))
    sv_c = p.sv.n

    def head_items(qblk_range=4):
        for qblk in range(qblk_range):
            jmax = 4 * qblk + 3
            for j in range(jmax + 1):
                yield qblk, j, max(qblk * 512, j * 128), j == 0, j == jmax, j >= 4 * qblk

    st_c = p.st.n
    ss_c = p.ss.n
    sv_cn = p.sv.n
    dq_c = p.dlg.n
    st_tp = {}; st_sc = {}; st_pv = {}
    ss_exp = {}
    sv_tpe = {}; sv_tri = {}; sv_oev = {}; sv_lev = {}
    dq_lall = {}
    for pr in range(8):
        for gi in range(4):
            st_c += 1; st_tp[(pr, gi)] = st_c
        for hh in range(2):
            n = 0
            for qblk, j, qlo, first, last, diag in head_items():
                st_c += 1; st_sc[(pr, hh, qblk, j)] = st_c
                st_c += 1; st_pv[(pr, hh, qblk, j)] = st_c
    for pr in range(8):
        for hh in range(2):
            for qblk, j, qlo, first, last, diag in head_items():
                ss_c += 1; ss_exp[(pr, hh, qblk, j)] = ss_c
    for pr in range(8):
        for gi in range(4):
            sv_cn += 1; sv_tpe[(pr, gi)] = sv_cn
        for hh in range(2):
            for qblk, j, qlo, first, last, diag in head_items():
                if diag:
                    sv_cn += 1; sv_tri[(pr, hh, qblk, j)] = sv_cn
            sv_cn += 1; sv_oev[(pr, hh)] = sv_cn
            sv_cn += 1; sv_lev[(pr, hh)] = sv_cn
    for pr in range(8):
        for hh in range(2):
            dq_c += 16; dq_lall[(pr, hh)] = dq_c

    lin = {}
    cnt = 0
    for pr in range(8):
        for hh in range(2):
            for qblk, j, qlo, first, last, diag in head_items():
                lin[(pr, hh, qblk, j)] = cnt
                cnt += 1
    lin_rev = {v: k for k, v in lin.items()}

    pslots = [g["ps_s"][0], g["ps_s"][1], g["ps_e"]]

    @block.tensor
    def _(tensor):
        tensor.wait_ge(p.sv.h, sv_c)
        for pr in range(8):
            for gi in range(4):
                half, b = divmod(gi, 2)
                src = g["q_tm"] if half == 0 else g["k_tm"]
                if (pr, gi) != (0, 0):
                    prevk = (pr, gi - 1) if gi else (pr - 1, 3)
                    tensor.wait_ge(p.sv.h, sv_tpe[prevk])
                ins = None
                for t in range(8):
                    ins = nc.tensor.transpose(
                        g["ps_tp"][:, t * 128:(t + 1) * 128],
                        src[:, b * 8 + t, pr * 128:(pr + 1) * 128],
                        g["ident"][:])
                p.st.inc(ins)
            for hh in range(2):
                h = 2 * pr + hh
                base = hh * 64
                first_of_head = True
                for qblk, j, qlo, first, last, diag in head_items():
                    n = lin[(pr, hh, qblk, j)]
                    jp = n % 3
                    wdt = qblk * 512 + 512 - qlo
                    if first_of_head:
                        tensor.wait_ge(p.sv.h, sv_tpe[(pr, 3)])
                        first_of_head = False
                    if n >= 3:
                        tensor.wait_ge(p.ss.h, ss_exp[lin_rev[n - 3]])
                    p.st.inc(nc.tensor.matmul(
                        pslots[jp][:, 0:wdt],
                        g["qk_fm"][base:base + 64, 1, j * 128:(j + 1) * 128],
                        g["qk_fm"][base:base + 64, 0, qlo:qblk * 512 + 512],
                        start=True, stop=True))
                    if diag:
                        tensor.wait_ge(p.sv.h, sv_tri[(pr, hh, qblk, j)])
                    else:
                        tensor.wait_ge(p.ss.h, ss_exp[(pr, hh, qblk, j)])
                    if first and (pr, hh) != (0, 0):
                        prevh = (pr, hh - 1) if hh else (pr - 1, 1)
                        tensor.wait_ge(p.sv.h, sv_lev[prevh])
                    p.st.inc(nc.tensor.matmul(
                        g["ps_pv"][0:65, qlo:qblk * 512 + 512],
                        g["v_tm"][:, j, h, :],
                        g["ptb"][jp][:, 0:wdt],
                        start=first, stop=last))

    @block.scalar
    def _(scalar):
        scalar.wait_ge(p.sv.h, sv_c)
        for pr in range(8):
            for hh in range(2):
                for qblk, j, qlo, first, last, diag in head_items():
                    n = lin[(pr, hh, qblk, j)]
                    jp = n % 3
                    wdt = qblk * 512 + 512 - qlo
                    scalar.wait_ge(p.st.h, st_sc[(pr, hh, qblk, j)])
                    p.ss.inc(scalar.activation(g["ptb"][jp][:, 0:wdt],
                                               pslots[jp][:, 0:wdt], AF.Exp,
                                               scale=0.125, bias=g["negc8"][:]))
    assert p.ss.n == ss_c

    @block.vector
    def _(vector):
        for pr in range(8):
            for gi in range(4):
                half, b = divmod(gi, 2)
                vector.wait_ge(p.st.h, st_tp[(pr, gi)])
                p.sv.inc(vector.tensor_copy(
                    g["qk_fm"][:, half, b * 1024:(b + 1) * 1024], g["ps_tp"][:]))
            for hh in range(2):
                h = 2 * pr + hh
                base = hh * 64
                for qblk, j, qlo, first, last, diag in head_items():
                    if not diag:
                        continue
                    n = lin[(pr, hh, qblk, j)]
                    jp = n % 3
                    vector.wait_ge(p.ss.h, ss_exp[(pr, hh, qblk, j)])
                    p.sv.inc(vector.tensor_tensor(g["ptb"][jp][:, 0:128],
                                                  g["ptb"][jp][:, 0:128],
                                                  g["triT"][:], ALU.mult))
                vector.wait_ge(p.st.h, st_pv[(pr, hh, 3, 15)])
                p.sv.inc(vector.tensor_copy(g["attn_fm"][base:base + 64, pr, :],
                                            g["ps_pv"][0:64, :]))
                hidx = 2 * pr + hh
                if hidx >= 1:
                    vector.wait_ge(p.dlg.h, dq_lall[lin_rev_h(hidx - 1)])
                p.sv.inc(vector.tensor_copy(g["lall"][32:33, :],
                                            g["ps_pv"][64:65, :]))
    assert p.sv.n == sv_cn

    @block.sync
    def _(sync):
        for pr in range(8):
            for hh in range(2):
                h = 2 * pr + hh
                sync.wait_ge(p.sv.h, sv_lev[(pr, hh)])
                p.dlg.start(sync, g["lall"][h:h + 1, :], g["lall"][32:33, :])
    assert p.dlg.n == dq_c

    ss_lr0 = p.ss.n

    @block.scalar
    def _(scalar):
        scalar.wait_ge(p.dlg.h, dq_c)
        p.ss.inc(scalar.activation(g["lrec"][0:NH, :], g["lall"][0:NH, :], AF.Ln))
        scalar.wait_ge(p.ss.h, p.ss.n)
        p.ss.inc(scalar.activation(g["lrec"][0:NH, :], g["lrec"][0:NH, :], AF.Exp,
                                   scale=-1.0))
    ss_lr = p.ss.n

    st_n0 = p.st.n
    sv_n0 = p.sv.n
    dlr_v = {}

    @block.gpsimd
    def _(gpsimd):
        gpsimd.wait_ge(p.ss.h, ss_lr)
        for i in range(NH * 4):
            h, qblk = divmod(i, 4)
            if i >= 2:
                gpsimd.wait_ge(p.st.h, st_n0 + i - 1)
            dlr_v[i] = p.dlr[i % 2].start(
                gpsimd, g["lrow_bf"][0:1, (i % 2) * 512:(i % 2) * 512 + 512],
                g["lrec"][h:h + 1, qblk * 512:(qblk + 1) * 512])

    @block.tensor
    def _(tensor):
        for i in range(NH * 4):
            h, qblk = divmod(i, 4)
            tensor.wait_ge(p.dlr[i % 2].h, dlr_v[i])
            if i >= 2:
                tensor.wait_ge(p.sv.h, sv_n0 + i - 1)
            p.st.inc(nc.tensor.matmul(g["ps_s"][i % 2][0:64, :],
                                      g["onescol"][:, 0:64],
                                      g["lrow_bf"][0:1, (i % 2) * 512:(i % 2) * 512 + 512],
                                      start=True, stop=True))

    @block.vector
    def _(vector):
        for i in range(NH * 4):
            h, qblk = divmod(i, 4)
            pr, hh = divmod(h, 2)
            base = hh * 64
            vector.wait_ge(p.st.h, st_n0 + i + 1)
            sl = g["attn_fm"][base:base + 64, pr, qblk * 512:(qblk + 1) * 512]
            p.sv.inc(vector.tensor_tensor(sl, sl, g["ps_s"][i % 2][0:64, :], ALU.mult))
    sv_norm = p.sv.n

    st_o0 = p.st.n
    sv_o0 = p.sv.n

    @block.tensor
    def _(tensor):
        wwait(p, tensor, wo_rdy, 1)
        if scrambled:
            wwait(p, tensor, wos_rdy, 0)
        tensor.wait_ge(p.sv.h, sv_norm)
        for ih in range(NT * 2):
            i, half = divmod(ih, 2)
            pb = g["ps_s"][ih % 2]
            if ih >= 2:
                tensor.wait_ge(p.sv.h, sv_o0 + ih - 1)
            if scrambled:
                hh, pr = i % 2, i // 2
                for c in range(NCH):
                    for bpar in range(2):
                        b0 = 2 * c + bpar
                        wsrc = g["wbuf"][1] if bpar == hh else g["wbuf"][0]
                        nc.tensor.matmul(
                            pb[:],
                            g["attn_fm"][hh * 64:hh * 64 + 64, pr, b0::16],
                            wsrc[hh * 64:(hh + 1) * 64, c,
                                 half * 512:(half + 1) * 512],
                            start=(c == 0 and bpar == 0), stop=False)
            else:
                for c in range(NCH):
                    nc.tensor.matmul(pb[:],
                                     g["attn_fm"][:, c, i * 128:(i + 1) * 128],
                                     g["wbuf"][1][:, c, half * 512:(half + 1) * 512],
                                     start=(c == 0), stop=False)
            p.st.inc(nc.tensor.matmul(pb[:], g["onescol"][:],
                                      g["biasrow"][:, 3 * D + half * 512:
                                                   3 * D + (half + 1) * 512],
                                      start=False, stop=True))

    ld_v = {}
    ss_res0 = p.ss.n

    @block.sync
    def _(sync):
        sync.wait_ge(p.ss.h, ss_res0)
        for i in range(NT):
            ld_v[i] = p.dx[i % 2].start(sync, g["xtile"][i % 2][:], xd[i])
            if i >= 1:
                sync.wait_ge(p.sv.h, sv_o0 + 2 * i)
                p.dx[(i - 1) % 2].start(sync, xd[i - 1], g["xtile"][(i - 1) % 2][:])
        sync.wait_ge(p.sv.h, sv_o0 + 2 * NT)
        p.dx[(NT - 1) % 2].start(sync, xd[NT - 1], g["xtile"][(NT - 1) % 2][:])

    @block.vector
    def _(vector):
        for ih in range(NT * 2):
            i, half = divmod(ih, 2)
            vector.wait_ge(p.st.h, st_o0 + ih + 1)
            vector.wait_ge(p.dx[i % 2].h, ld_v[i])
            sl = g["xtile"][i % 2][:, half * 512:(half + 1) * 512]
            p.sv.inc(vector.tensor_add(sl, sl, g["ps_s"][ih % 2][:]))


def emit_mla(p, g, xd, W, cos_d, sin_d):
    nc, block = p.nc, p.block
    ar = p.arena
    ar.reset()

    g["q_tm"] = ar.bf16([128, NT, D])
    g["k_tm"] = ar.bf16([128, NT, D])
    g["v_tm"] = ar.bf16([128, NT, NH, HD + 1])
    U = ar.bf16([128, 9744])
    g["qk_fm"] = U[:, 0:4096].rearrange("p (c t) -> p c t", c=2)
    g["lall"] = U[:, 4096:8192].bitcast(F32)
    g["lrec"] = g["lall"]
    g["lrow_bf"] = U[:, 8192:9216]
    c_fm = g["qk_fm"]
    wsl = [U[:, 4096:6144], U[:, 6144:8192]]
    ropebuf = U[:, 8192:9216].bitcast(F32).rearrange(
        "p (s c t) -> p s c t", s=2, c=2)
    rp = U[:, 9216:9472].bitcast(F32)
    rtmp = U[:, 9472:9728].bitcast(F32)
    bcol = U[:, 9728:9744].bitcast(F32)

    wv8 = lambda s: wsl[s].rearrange("p (c o) -> p c o", c=8)
    wv2 = lambda s: wsl[s].rearrange("p (c o) -> p c o", c=2)
    rr = lambda t: t.rearrange("(c p) o -> p c o", p=128)
    rowv = lambda t: t.rearrange("(a d) -> a d", a=1)

    stg = {}

    @block.gpsimd
    def _(gpsimd):
        stg["wqr"] = p.dw[0].start(gpsimd, wv8(0)[:, :, 0:128], rr(W["wqr"]))
        stg["wkr"] = p.dw[0].start(gpsimd, wv8(0)[:, :, 128:256], rr(W["wkr"]))
        stg["wqc"] = p.dw[0].start(gpsimd, wv2(1)[:, :, 0:896], rr(W["wqc"]))
        for oc in range(2):
            p.dw[1].start(gpsimd, bcol[:, oc:oc + 1],
                          W["bq"][oc * 128:(oc + 1) * 128].rearrange(
                              "(p a) -> p a", a=1))
            p.dw[1].start(gpsimd, bcol[:, 2 + oc:3 + oc],
                          W["bkv"][oc * 128:(oc + 1) * 128].rearrange(
                              "(p a) -> p a", a=1))
        p.dw[1].start(gpsimd, g["biasrow"][:, 0:896].unsqueeze(1), rowv(W["bqc"]))
        p.dw[1].start(gpsimd, g["biasrow"][:, 896:1792].unsqueeze(1), rowv(W["bkc"]))
        p.dw[1].start(gpsimd, g["biasrow"][:, 1792:2816].unsqueeze(1), rowv(W["bv"]))
        p.dw[1].start(gpsimd, g["biasrow"][:, 2816:2944].unsqueeze(1), rowv(W["bqr"]))
        p.dw[1].start(gpsimd, g["biasrow"][:, 2944:3072].unsqueeze(1), rowv(W["bkr"]))
        p.dw[1].start(gpsimd, g["biasrow"][:, 3 * D:4 * D].unsqueeze(1), rowv(W["bo"]))
        stg["b"] = p.dw[1].n
        stg["wo"] = p.load_par(gpsimd, g["wbuf"][1][:], rr(W["wo"]))

    def stage_load(name, dst, src, st_free):
        @block.gpsimd
        def _(gpsimd):
            gpsimd.wait_ge(p.st.h, st_free)
            stg[name] = p.dw[0].start(gpsimd, dst, src)

    def rope_proj(dst_tm, wview, w_rdy, br_off):
        st0, sv0, ss0 = p.st.n, p.sv.n, p.ss.n
        dlrv = {}

        @block.gpsimd
        def _(gpsimd):
            for i in range(NT):
                if i >= 2:
                    gpsimd.wait_ge(p.sv.h, sv0 + 6 * (i - 1))
                else:
                    gpsimd.wait_ge(p.sv.h, sv0)
                p.dlr[i % 2].start(gpsimd, ropebuf[:, i % 2, 0, :],
                                   cos_d[i * 128:(i + 1) * 128, :])
                dlrv[i] = p.dlr[i % 2].start(gpsimd, ropebuf[:, i % 2, 1, :],
                                             sin_d[i * 128:(i + 1) * 128, :])

        @block.tensor
        def _(tensor):
            tensor.wait_ge(p.dw[0].h, w_rdy)
            tensor.wait_ge(p.dw[1].h, stg["b"])
            tensor.wait_ge(p.sv.h, p.sv_xn)
            for i in range(NT):
                if i >= 1:
                    tensor.wait_ge(p.sv.h, sv0 + 6 * i - 1)
                else:
                    tensor.wait_ge(p.sv.h, sv0)
                for c in range(NCH):
                    nc.tensor.matmul(g["ps_e"][:, 0:128],
                                     g["xn_fm"][:, c, i * 128:(i + 1) * 128],
                                     wview[:, c, :],
                                     start=(c == 0), stop=False)
                p.st.inc(nc.tensor.matmul(g["ps_e"][:, 0:128], g["onescol"][:],
                                          g["biasrow"][:, br_off:br_off + 128],
                                          start=False, stop=True))

        @block.vector
        def _(vector):
            for i in range(NT):
                s = i % 2
                vector.wait_ge(p.st.h, st0 + i + 1)
                vector.wait_ge(p.dlr[s].h, dlrv[i])
                vector.wait_ge(p.ss.h, ss0 + i)
                vector.wait_ge(p.sv.h, p.sv.n)
                p.sv.inc(vector.tensor_mul(rp[:, 0:64], g["ps_e"][:, 0:64],
                                           ropebuf[:, s, 0, 0:64]))
                p.sv.inc(vector.tensor_mul(rtmp[:, 0:64], g["ps_e"][:, 64:128],
                                           ropebuf[:, s, 1, 0:64]))
                vector.wait_ge(p.sv.h, p.sv.n)
                p.sv.inc(vector.tensor_sub(rp[:, 0:64], rp[:, 0:64],
                                           rtmp[:, 0:64]))
                p.sv.inc(vector.tensor_mul(rp[:, 64:128], g["ps_e"][:, 64:128],
                                           ropebuf[:, s, 0, 64:128]))
                p.sv.inc(vector.tensor_mul(rtmp[:, 64:128], g["ps_e"][:, 0:64],
                                           ropebuf[:, s, 1, 64:128]))
                vector.wait_ge(p.sv.h, p.sv.n)
                p.sv.inc(vector.tensor_add(rp[:, 64:128], rp[:, 64:128],
                                           rtmp[:, 64:128]))

        @block.scalar
        def _(scalar):
            for i in range(NT):
                scalar.wait_ge(p.sv.h, sv0 + 6 * (i + 1))
                p.ss.inc(scalar.activation(dst_tm[:, i, 896:1024], rp[:],
                                           AF.Gelu))
        return p.ss.n

    def fm_proj_gelu(wview, w_rdy, bc_idx, st_war):
        st0, ss0 = p.st.n, p.ss.n

        @block.tensor
        def _(tensor):
            tensor.wait_ge(p.dw[0].h, w_rdy)
            tensor.wait_ge(p.sv.h, p.sv_xn)
            tensor.wait_ge(p.ss.h, ss0)
            for k in range(8):
                oc, sp = divmod(k, 4)
                pb = g["ps_s"][k % 2]
                if k >= 2:
                    tensor.wait_ge(p.ss.h, ss0 + k - 1)
                ins = None
                for c in range(NCH):
                    ins = nc.tensor.matmul(pb[:],
                                           wview[:, c, oc * 128:(oc + 1) * 128],
                                           g["xn_fm"][:, c, sp * 512:(sp + 1) * 512],
                                           start=(c == 0), stop=(c == NCH - 1))
                p.st.inc(ins)
        st_mm = p.st.n

        @block.scalar
        def _(scalar):
            scalar.wait_ge(p.dw[1].h, stg["b"])
            scalar.wait_ge(p.st.h, st_war)
            for k in range(8):
                oc, sp = divmod(k, 4)
                scalar.wait_ge(p.st.h, st0 + k + 1)
                p.ss.inc(scalar.activation(
                    c_fm[:, oc, sp * 512:(sp + 1) * 512], g["ps_s"][k % 2][:],
                    AF.Gelu, bias=bcol[:, bc_idx + oc:bc_idx + oc + 1]))
        return st_mm, p.ss.n

    def tm_proj_gelu(dst_tm, w_rdy, ss_cready, br_off):
        st0, ss0 = p.st.n, p.ss.n
        spans = [(0, 512), (512, 384)]

        @block.tensor
        def _(tensor):
            tensor.wait_ge(p.dw[0].h, w_rdy)
            tensor.wait_ge(p.ss.h, ss_cready)
            for n in range(NT * 2):
                i, hf = divmod(n, 2)
                so, sw = spans[hf]
                pb = g["ps_s"][n % 2]
                if n >= 2:
                    tensor.wait_ge(p.ss.h, ss0 + n - 1)
                for c in range(2):
                    nc.tensor.matmul(pb[:, 0:sw],
                                     c_fm[:, c, i * 128:(i + 1) * 128],
                                     wv2(1)[:, c, so:so + sw],
                                     start=(c == 0), stop=False)
                p.st.inc(nc.tensor.matmul(
                    pb[:, 0:sw], g["onescol"][:],
                    g["biasrow"][:, br_off + so:br_off + so + sw],
                    start=False, stop=True))
        st_mm = p.st.n

        @block.scalar
        def _(scalar):
            for n in range(NT * 2):
                i, hf = divmod(n, 2)
                so, sw = spans[hf]
                scalar.wait_ge(p.st.h, st0 + n + 1)
                p.ss.inc(scalar.activation(dst_tm[:, i, so:so + sw],
                                           g["ps_s"][n % 2][:, 0:sw], AF.Gelu))
        return st_mm, p.ss.n

    ss_qrope = rope_proj(g["q_tm"], wv8(0)[:, :, 0:128], stg["wqr"], 2816)
    ss_krope = rope_proj(g["k_tm"], wv8(0)[:, :, 128:256], stg["wkr"], 2944)
    stage_load("wq", wv8(0)[:, :, 0:256], rr(W["wq"]), p.st.n)
    st_cq, ss_cq = fm_proj_gelu(wv8(0)[:, :, 0:256], stg["wq"], 0, 0)
    st_qc, ss_qc = tm_proj_gelu(g["q_tm"], stg["wqc"], ss_cq, 0)
    stage_load("wkv", wv8(0)[:, :, 0:256], rr(W["wkv"]), st_cq)
    st_ckv, ss_ckv = fm_proj_gelu(wv8(0)[:, :, 0:256], stg["wkv"], 2, st_qc)
    stage_load("wkc", wv2(1)[:, :, 0:896], rr(W["wkc"]), st_qc)
    st_kc, ss_kc = tm_proj_gelu(g["k_tm"], stg["wkc"], ss_ckv, 896)
    stage_load("wv", wv2(1)[:, :, 0:1024], rr(W["wv"]), st_kc)

    @block.vector
    def _(vector):
        p.sv.inc(vector.memset(g["v_tm"][:], 1.0))
    sv_vm = p.sv.n

    st0v, sv0v, ss0v = p.st.n, p.sv.n, p.ss.n

    @block.tensor
    def _(tensor):
        tensor.wait_ge(p.dw[0].h, stg["wv"])
        tensor.wait_ge(p.ss.h, ss0v)
        for n in range(NT * 2):
            i, hf = divmod(n, 2)
            pb = g["ps_s"][n % 2]
            if n >= 2:
                tensor.wait_ge(p.sv.h, sv0v + n - 1)
            for c in range(2):
                nc.tensor.matmul(pb[:],
                                 c_fm[:, c, i * 128:(i + 1) * 128],
                                 wv2(1)[:, c, hf * 512:(hf + 1) * 512],
                                 start=(c == 0), stop=False)
            p.st.inc(nc.tensor.matmul(
                pb[:], g["onescol"][:],
                g["biasrow"][:, 1792 + hf * 512:1792 + (hf + 1) * 512],
                start=False, stop=True))

    @block.vector
    def _(vector):
        vector.wait_ge(p.sv.h, sv_vm)
        for n in range(NT * 2):
            i, hf = divmod(n, 2)
            vector.wait_ge(p.st.h, st0v + n + 1)
            p.sv.inc(vector.tensor_copy(
                g["v_tm"][:, i, hf * 8:(hf + 1) * 8, 0:HD],
                g["ps_s"][n % 2][:].rearrange("p (a b) -> p a b", a=8)))
    sv_v = p.sv.n

    emit_attn_core(p, g, xd, stg["wo"] + [(p.dw[1].h, stg["b"])], None,
                   scrambled=False, sv_qkv=sv_v)


def emit_barrier(p):
    block = p.block
    sv, ss, st, sg = p.sv.n, p.ss.n, p.st.n, p.sg.n

    @block.tensor
    def _(tensor):
        tensor.wait_ge(p.sv.h, sv); tensor.wait_ge(p.ss.h, ss); tensor.wait_ge(p.sg.h, sg)

    @block.vector
    def _(vector):
        vector.wait_ge(p.st.h, st); vector.wait_ge(p.ss.h, ss); vector.wait_ge(p.sg.h, sg)

    @block.scalar
    def _(scalar):
        scalar.wait_ge(p.st.h, st); scalar.wait_ge(p.sv.h, sv); scalar.wait_ge(p.sg.h, sg)

    @block.gpsimd
    def _(gpsimd):
        gpsimd.wait_ge(p.st.h, st); gpsimd.wait_ge(p.sv.h, sv); gpsimd.wait_ge(p.ss.h, ss)

    @block.sync
    def _(sync):
        sync.wait_ge(p.st.h, st); sync.wait_ge(p.sv.h, sv); sync.wait_ge(p.ss.h, ss)


def emit_ffn0(p, g, xd, W, Wp, bp, sim, temp, Wd, Wu):
    nc, block = p.nc, p.block
    ar = p.arena
    ar.reset()
    CLAMP = float(np.log(1.0 / 0.01))

    wp_nat = ar.bf16([128, 2, 1024])
    wpT = ar.bf16([128, 8, 256])
    wd_nat = ar.bf16([128, 1024])
    wdT = ar.bf16([128, 8, 16])
    wu_nat = ar.bf16([128, 8, 16])
    wuT = ar.bf16([128, 1024])
    simb = ar.bf16([128, 2, 4])
    simTf = ar.f32([128, 256])
    simsq = ar.bf16([128, 256])
    simnT = ar.bf16([128, 256])
    simn = ar.bf16([128, 2, 4])
    tmpv = ar.f32([128, 8])
    s_bc = ar.f32([128, 1])
    s_bf = ar.bf16([128, 16])
    c38 = ar.f32([128, 1])
    ssp = ar.f32([128, 1])
    rn = ar.f32([128, 1])
    sqd = ar.bf16([128, 256])
    p_fm = [ar.bf16([128, 2, 128]) for _ in range(2)]
    dall = [ar.f32([128, 16]) for _ in range(2)]
    lgc = ar.f32([128, 4])
    esum = ar.f32([128, 1])
    er = ar.f32([128, 1])
    lw = ar.f32([128, 4])
    rm = ar.f32([128, 1])
    msk = ar.f32([128, 4])
    ret = ar.f32([128, 4])
    t_scl = [ar.bf16([128, 16]) for _ in range(2)]
    t_fm = [ar.bf16([128, 128]) for _ in range(2)]
    pn_t = [ar.bf16([128, 256]) for _ in range(2)]

    @block.gpsimd
    def _(gpsimd):
        g["_w0"] = p.load_par(gpsimd, g["wbuf"][0][:],
                              W.rearrange("(c p) o -> p c o", p=128))
        p.dw[1].start(gpsimd, g["biasrow"][:, 0:256].unsqueeze(1),
                      bp.rearrange("(a d) -> a d", a=1))
        p.dw[1].start(gpsimd, wp_nat[:], Wp.rearrange("(c p) o -> p c o", p=128))
        p.dw[1].start(gpsimd, wd_nat[0:15, :], Wd)
        p.dw[1].start(gpsimd, wu_nat[:, :, 0:15],
                      Wu.rearrange("(c p) r -> p c r", p=128))
        p.dw[1].start(gpsimd, simb[:], sim.rearrange("(c p) e -> p c e", p=128))
        g["_w1"] = p.dw[1].start(gpsimd, tmpv[0:1, 0:1],
                                 temp.rearrange("(a d) -> a d", a=1))
    w0_rdy = g.pop("_w0")
    w1_rdy = g.pop("_w1")

    st0 = p.st.n
    sv0 = p.sv.n

    @block.tensor
    def _(tensor):
        wwait(p, tensor, w0_rdy, 0)
        tensor.wait_ge(p.dw[1].h, w1_rdy)
        tensor.wait_ge(p.sv.h, sv0)
        gidx = 0
        for dc in range(8):
            if gidx >= 1:
                tensor.wait_ge(p.sv.h, sv0 + 3 + gidx)
            for oc in range(8):
                ins = nc.tensor.transpose(g["ps_tp"][:, oc * 128:(oc + 1) * 128],
                                          g["wbuf"][0][:, oc, dc * 128:(dc + 1) * 128],
                                          g["ident"][:])
            p.st.inc(ins); gidx += 1
        for dc in range(8):
            tensor.wait_ge(p.sv.h, sv0 + 3 + gidx)
            for oc in range(2):
                ins = nc.tensor.transpose(g["ps_tp"][:, oc * 128:(oc + 1) * 128],
                                          wp_nat[:, oc, dc * 128:(dc + 1) * 128],
                                          g["ident"][:])
            p.st.inc(ins); gidx += 1
        for dc in range(8):
            tensor.wait_ge(p.sv.h, sv0 + 3 + gidx)
            ins = nc.tensor.transpose(g["ps_tp"][:, 0:15],
                                      wd_nat[0:15, dc * 128:(dc + 1) * 128],
                                      g["ident"][0:15, 0:15])
            p.st.inc(ins); gidx += 1
        for dc in range(8):
            tensor.wait_ge(p.sv.h, sv0 + 3 + gidx)
            ins = nc.tensor.transpose(g["ps_tp"][0:15, 0:128],
                                      wu_nat[:, dc, 0:15], g["ident"][:])
            p.st.inc(ins); gidx += 1
        tensor.wait_ge(p.sv.h, sv0 + 3 + gidx)
        for ch in range(2):
            ins = nc.tensor.transpose(g["ps_tp"][0:4, ch * 128:(ch + 1) * 128],
                                      simb[:, ch, :], g["ident"][:])
        p.st.inc(ins); gidx += 1

    @block.vector
    def _(vector):
        p.sv.inc(vector.memset(c38[:], 1e-38))
        p.sv.inc(vector.memset(t_scl[0][:], 0.0))
        p.sv.inc(vector.memset(t_scl[1][:], 0.0))
        gidx = 3
        for dc in range(8):
            vector.wait_ge(p.st.h, st0 + (gidx - 3) + 1)
            p.sv.inc(vector.tensor_copy(g["wbuf"][1][:, dc, :], g["ps_tp"][:]))
            gidx += 1
        for dc in range(8):
            vector.wait_ge(p.st.h, st0 + (gidx - 3) + 1)
            p.sv.inc(vector.tensor_copy(wpT[:, dc, :], g["ps_tp"][:, 0:256]))
            gidx += 1
        for dc in range(8):
            vector.wait_ge(p.st.h, st0 + (gidx - 3) + 1)
            p.sv.inc(vector.tensor_copy(wdT[:, dc, 0:15], g["ps_tp"][:, 0:15]))
            gidx += 1
        for dc in range(8):
            vector.wait_ge(p.st.h, st0 + (gidx - 3) + 1)
            p.sv.inc(vector.tensor_copy(wuT[0:15, dc * 128:(dc + 1) * 128],
                                        g["ps_tp"][0:15, 0:128]))
            gidx += 1
        vector.wait_ge(p.st.h, st0 + (gidx - 3) + 1)
        p.sv.inc(vector.tensor_copy(simTf[0:4, :], g["ps_tp"][0:4, 0:256]))
    sv_wprep = p.sv.n

    ss0 = p.ss.n

    @block.scalar
    def _(scalar):
        scalar.wait_ge(p.sv.h, sv_wprep)
        p.ss.inc(scalar.activation(simsq[0:4, :], simTf[0:4, :], AF.Square,
                                   accum_out=tmpv[0:4, 3:4]))
        scalar.wait_ge(p.ss.h, p.ss.n)
        p.ss.inc(scalar.activation(tmpv[0:4, 4:5], tmpv[0:4, 3:4], AF.Ln,
                                   bias=c38[0:4, :]))
        scalar.wait_ge(p.ss.h, p.ss.n)
        p.ss.inc(scalar.activation(tmpv[0:4, 5:6], tmpv[0:4, 4:5], AF.Exp, scale=-0.5))
        p.ss.inc(scalar.activation(tmpv[0:1, 1:2], tmpv[0:1, 0:1], AF.Identity))
    ss_sim = p.ss.n

    @block.vector
    def _(vector):
        vector.wait_ge(p.ss.h, ss_sim)
        p.sv.inc(vector.tensor_scalar(tmpv[0:4, 6:7], tmpv[0:4, 5:6], 1e12, None,
                                      ALU.min))
        p.sv.inc(vector.tensor_scalar(tmpv[0:1, 2:3], tmpv[0:1, 1:2], CLAMP, None,
                                      ALU.min))
    sv_rn4 = p.sv.n

    @block.scalar
    def _(scalar):
        scalar.wait_ge(p.sv.h, sv_rn4)
        p.ss.inc(scalar.activation(simnT[0:4, :], simTf[0:4, :], AF.Identity,
                                   scale=tmpv[0:4, 6:7]))
        p.ss.inc(scalar.activation(tmpv[0:1, 3:4], tmpv[0:1, 2:3], AF.Exp))
    ss_simn = p.ss.n

    @block.vector
    def _(vector):
        vector.wait_ge(p.ss.h, ss_simn)
        p.sv.inc(vector.tensor_copy(s_bf[0:1, 0:1], tmpv[0:1, 3:4]))
    sv_sbf = p.sv.n

    st1 = p.st.n

    @block.tensor
    def _(tensor):
        tensor.wait_ge(p.sv.h, sv_sbf)
        for ch in range(2):
            nc.tensor.transpose(g["ps_tp"][:, ch * 128:ch * 128 + 4],
                                simnT[0:4, ch * 128:(ch + 1) * 128],
                                g["ident"][0:4, 0:4])
        p.st.inc(nc.tensor.matmul(g["ps_e"][0:128, 0:1], g["onescol"][:],
                                  s_bf[0:1, 0:1], start=True, stop=True))
    st_simn = p.st.n

    @block.vector
    def _(vector):
        vector.wait_ge(p.st.h, st_simn)
        p.sv.inc(vector.tensor_copy(simn[:],
                                    g["ps_tp"][:, 0:256].rearrange(
                                        "p (c t) -> p c t", c=2)[:, :, 0:4]))
        p.sv.inc(vector.tensor_copy(s_bc[:], g["ps_e"][0:128, 0:1]))
    sv_router_w = p.sv.n

    stt = {}; sst = {}; svt = {}
    st_c = p.st.n; ss_c = p.ss.n; sv_c = p.sv.n
    for i in range(NT):
        st_c += 1; stt[("proj", i)] = st_c
        st_c += 1; stt[("dall", i)] = st_c
        st_c += 1; stt[("pnT", i)] = st_c
        st_c += 1; stt[("lg", i)] = st_c
        st_c += 1; stt[("tsT", i)] = st_c
        st_c += 2; stt[("base", i)] = st_c
    for i in range(NT):
        ss_c += 1; sst[("sq", i)] = ss_c
        ss_c += 1; sst[("ln", i)] = ss_c
        ss_c += 1; sst[("ex", i)] = ss_c
        ss_c += 1; sst[("pn", i)] = ss_c
        ss_c += 1; sst[("lw", i)] = ss_c
    for i in range(NT):
        sv_c += 1; svt[("min", i)] = sv_c
        sv_c += 1; svt[("dev", i)] = sv_c
        sv_c += 1; svt[("pnev", i)] = sv_c
        sv_c += 8; svt[("rt", i)] = sv_c
        sv_c += 1; svt[("tsev", i)] = sv_c
        sv_c += 2; svt[("res", i)] = sv_c

    ld_v = {}
    ss_entry = p.ss.n

    @block.sync
    def _(sync):
        sync.wait_ge(p.ss.h, ss_entry)
        for i in range(NT):
            ld_v[i] = p.dx[i % 2].start(sync, g["xtile"][i % 2][:], xd[i])
            if i >= 1:
                sync.wait_ge(p.sv.h, svt[("res", i - 1)])
                p.dx[(i - 1) % 2].start(sync, xd[i - 1], g["xtile"][(i - 1) % 2][:])
        sync.wait_ge(p.sv.h, svt[("res", NT - 1)])
        p.dx[(NT - 1) % 2].start(sync, xd[NT - 1], g["xtile"][(NT - 1) % 2][:])

    @block.tensor
    def _(tensor):
        tensor.wait_ge(p.sv.h, sv_router_w)
        for i in range(NT):
            if i >= 1:
                tensor.wait_ge(p.ss.h, sst[("pn", i - 1)])
            for c in range(NCH):
                nc.tensor.matmul(g["ps_e"][:, 0:256],
                                 g["xn_fm"][:, c, i * 128:(i + 1) * 128],
                                 wpT[:, c, :], start=(c == 0), stop=False)
            p.st.inc(nc.tensor.matmul(g["ps_e"][:, 0:256], g["onescol"][:],
                                      g["biasrow"][:, 0:256], start=False, stop=True))
            if i >= 1:
                tensor.wait_ge(p.sv.h, svt[("dev", i - 1)])
            ins = None
            for c in range(NCH):
                ins = nc.tensor.matmul(g["ps_e"][:, 256:271],
                                       g["xn_fm"][:, c, i * 128:(i + 1) * 128],
                                       wdT[:, c, 0:15], start=(c == 0),
                                       stop=(c == NCH - 1))
            p.st.inc(ins)
            tensor.wait_ge(p.ss.h, sst[("pn", i)])
            if i >= 1:
                tensor.wait_ge(p.sv.h, svt[("pnev", i - 1)])
            ins = None
            for ch in range(2):
                ins = nc.tensor.transpose(g["ps_tp"][:, 256 + ch * 128:256 + (ch + 1) * 128],
                                          pn_t[i % 2][:, ch * 128:(ch + 1) * 128],
                                          g["ident"][:])
            p.st.inc(ins)
            tensor.wait_ge(p.sv.h, svt[("pnev", i)])
            if i >= 1:
                tensor.wait_ge(p.sv.h, svt[("rt", i - 1)])
            ins = None
            for ch in range(2):
                ins = nc.tensor.matmul(g["ps_e"][:, 384:388],
                                       p_fm[i % 2][:, ch, :], simn[:, ch, :],
                                       start=(ch == 0), stop=(ch == 1))
            p.st.inc(ins)
            tensor.wait_ge(p.sv.h, svt[("rt", i)])
            if i >= 1:
                tensor.wait_ge(p.sv.h, svt[("tsev", i - 1)])
            p.st.inc(nc.tensor.transpose(g["ps_tp"][0:16, 512:640],
                                         t_scl[i % 2][:], g["ident"][:]))
            tensor.wait_ge(p.sv.h, svt[("tsev", i)])
            if i >= 1:
                tensor.wait_ge(p.sv.h, svt[("res", i - 1)])
            ins = None
            for half in range(2):
                for c in range(NCH):
                    nc.tensor.matmul(g["ps_s"][half][:],
                                     g["xn_fm"][:, c, i * 128:(i + 1) * 128],
                                     g["wbuf"][1][:, c, half * 512:(half + 1) * 512],
                                     start=(c == 0), stop=False)
                ins = nc.tensor.matmul(g["ps_s"][half][:], t_fm[i % 2][0:15, :],
                                       wuT[0:15, half * 512:(half + 1) * 512],
                                       start=False, stop=True)
                p.st.inc(ins)

    @block.scalar
    def _(scalar):
        scalar.wait_ge(p.sv.h, sv_router_w)
        for i in range(NT):
            scalar.wait_ge(p.st.h, stt[("proj", i)])
            if i >= 1:
                scalar.wait_ge(p.ss.h, sst[("sq", i - 1)])
            p.ss.inc(scalar.activation(sqd[:, 0:256], g["ps_e"][:, 0:256], AF.Square,
                                       accum_out=ssp[:]))
            scalar.wait_ge(p.ss.h, p.ss.n)
            p.ss.inc(scalar.activation(rn[:], ssp[:], AF.Ln, bias=c38[:]))
            scalar.wait_ge(p.ss.h, p.ss.n)
            p.ss.inc(scalar.activation(rn[:], rn[:], AF.Exp, scale=-0.5))
            scalar.wait_ge(p.sv.h, svt[("min", i)])
            if i >= 2:
                scalar.wait_ge(p.st.h, stt[("pnT", i - 2)])
            p.ss.inc(scalar.activation(pn_t[i % 2][:], g["ps_e"][:, 0:256],
                                       AF.Identity, scale=rn[:]))
            scalar.wait_ge(p.sv.h, svt[("rt", i)] - 6)
            p.ss.inc(scalar.activation(lw[:], lgc[:], AF.Exp, scale=s_bc[:],
                                       accum_out=esum[:]))

    @block.vector
    def _(vector):
        for i in range(NT):
            vector.wait_ge(p.ss.h, sst[("ex", i)])
            p.sv.inc(vector.tensor_scalar(rn[:], rn[:], 1e12, None, ALU.min))
            vector.wait_ge(p.st.h, stt[("dall", i)])
            p.sv.inc(vector.tensor_copy(dall[i % 2][:, 0:15], g["ps_e"][:, 256:271]))
            vector.wait_ge(p.st.h, stt[("pnT", i)])
            p.sv.inc(vector.tensor_copy(p_fm[i % 2][:],
                                        g["ps_tp"][:, 256:512].rearrange(
                                            "p (c t) -> p c t", c=2)))
            vector.wait_ge(p.st.h, stt[("lg", i)])
            p.sv.inc(vector.reduce_max(rm[:].unsqueeze(2),
                                       g["ps_e"][:, 384:388].rearrange(
                                           "p (a b) -> p a b", a=1), AX.X))
            vector.wait_ge(p.sv.h, p.sv.n)
            p.sv.inc(vector.tensor_scalar(lgc[:], g["ps_e"][:, 384:388], rm[:],
                                          None, ALU.subtract))
            vector.wait_ge(p.ss.h, sst[("lw", i)])
            p.sv.inc(vector.reciprocal(er[:], esum[:]))
            vector.wait_ge(p.sv.h, p.sv.n)
            p.sv.inc(vector.tensor_scalar(lw[:], lw[:], er[:], None, ALU.mult))
            vector.wait_ge(p.sv.h, p.sv.n)
            p.sv.inc(vector.reduce_max(rm[:].unsqueeze(2),
                                       lw[:].rearrange("p (a b) -> p a b", a=1), AX.X))
            vector.wait_ge(p.sv.h, p.sv.n)
            p.sv.inc(vector.tensor_scalar(msk[:], lw[:], rm[:], None, ALU.is_ge))
            vector.wait_ge(p.sv.h, p.sv.n)
            p.sv.inc(vector.tensor_mul(ret[:], lw[:], msk[:]))
            vector.wait_ge(p.sv.h, p.sv.n)
            if i >= 2:
                vector.wait_ge(p.st.h, stt[("tsT", i - 2)])
            ro = 0
            ins = None
            for e in range(4):
                r = [1, 2, 4, 8][e]
                ins = vector.tensor_scalar(t_scl[i % 2][:, ro:ro + r],
                                           dall[i % 2][:, ro:ro + r],
                                           ret[:, e:e + 1], None, ALU.mult)
                ro += r
            p.sv.inc(ins)
            vector.wait_ge(p.st.h, stt[("tsT", i)])
            if i >= 2:
                vector.wait_ge(p.st.h, stt[("base", i - 2)])
            p.sv.inc(vector.tensor_copy(t_fm[i % 2][0:16, :],
                                        g["ps_tp"][0:16, 512:640]))
            vector.wait_ge(p.st.h, stt[("base", i)])
            vector.wait_ge(p.dx[i % 2].h, ld_v[i])
            for half in range(2):
                sl = g["xtile"][i % 2][:, half * 512:(half + 1) * 512]
                p.sv.inc(vector.tensor_add(sl, sl, g["ps_s"][half][:]))
    assert p.sv.n == sv_c, (p.sv.n, sv_c)
    assert p.ss.n == ss_c, (p.ss.n, ss_c)
    assert p.st.n == st_c, (p.st.n, st_c)


QVA = 5
QVB = 12
QS = 2


def emit_quant_a(p, q, src, width, wv=()):
    nc, block = p.nc, p.block
    nb = width // 16
    sv0, ss0 = p.sv.n, p.ss.n

    @block.vector
    def _(vector):
        vector.wait_ge(p.sv.h, sv0)
        for sem, cnt in wv:
            vector.wait_ge(sem, cnt)
        p.sv.inc(vector.tensor_reduce(
            q["bm"][:, 0:nb].unsqueeze(2),
            src.rearrange("p (a b) -> p a b", b=16), AX.X, ALU.max,
            apply_absolute_value=True))
        vector.wait_ge(p.sv.h, p.sv.n)
        p.sv.inc(vector.tensor_scalar(q["sc2"][:, 0:nb], q["bm"][:, 0:nb],
                                      1.0 / 6.0, 1e-12, ALU.mult, ALU.max))
        vector.wait_ge(p.sv.h, p.sv.n)
        p.sv.inc(vector.reciprocal(q["inv"][:, 0:nb], q["sc2"][:, 0:nb]))
        vector.wait_ge(p.sv.h, p.sv.n)
        p.sv.inc(vector.tensor_copy(
            q["invx"][:, 0:width].rearrange("p (a b) -> p a b", b=16),
            q["inv"][:, 0:nb].unsqueeze(2).broadcast_to([128, nb, 16])))
        vector.wait_ge(p.sv.h, p.sv.n)
        p.sv.inc(vector.tensor_mul(src, src, q["invx"][:, 0:width]))
    sv_y = p.sv.n

    @block.scalar
    def _(scalar):
        scalar.wait_ge(p.ss.h, ss0)
        scalar.wait_ge(p.sv.h, sv_y)
        p.ss.inc(scalar.activation(q["sg"][:, 0:width], src, AF.Sign))
        p.ss.inc(scalar.activation(q["ab"][:, 0:width], src, AF.Abs))
    assert p.sv.n == sv0 + QVA and p.ss.n == ss0 + QS
    return sv_y, p.ss.n


def emit_quant_b(p, q, src, dst, width, ss_cast, wv_dst=()):
    nc, block = p.nc, p.block
    nb = width // 16
    sv0 = p.sv.n

    @block.vector
    def _(vector):
        vector.wait_ge(p.sv.h, sv0)
        vector.wait_ge(p.ss.h, ss_cast)
        tA = q["tA"][:, 0:width]
        tB = q["tB"][:, 0:width]
        tC = q["tC"][:, 0:width]
        ab = q["ab"][:, 0:width]
        p.sv.inc(vector.tensor_scalar(tA, ab, 0.25, None, ALU.is_gt))
        vector.wait_ge(p.sv.h, p.sv.n)
        p.sv.inc(vector.scalar_tensor_tensor(tA, ab, 0.75, tA, ALU.is_gt,
                                             ALU.add))
        vector.wait_ge(p.sv.h, p.sv.n)
        p.sv.inc(vector.scalar_tensor_tensor(tA, ab, 1.25, tA, ALU.is_gt,
                                             ALU.add))
        vector.wait_ge(p.sv.h, p.sv.n)
        p.sv.inc(vector.scalar_tensor_tensor(tA, ab, 1.75, tA, ALU.is_gt,
                                             ALU.add))
        p.sv.inc(vector.tensor_scalar(tB, ab, 2.5, None, ALU.is_gt))
        vector.wait_ge(p.sv.h, p.sv.n)
        p.sv.inc(vector.scalar_tensor_tensor(tB, ab, 3.5, tB, ALU.is_gt,
                                             ALU.add))
        p.sv.inc(vector.tensor_scalar(tC, ab, 5.0, 2.0, ALU.is_gt, ALU.mult))
        vector.wait_ge(p.sv.h, p.sv.n)
        p.sv.inc(vector.scalar_tensor_tensor(tB, tA, 0.5, tB, ALU.mult,
                                             ALU.add))
        vector.wait_ge(p.sv.h, p.sv.n)
        p.sv.inc(vector.tensor_add(tB, tB, tC))
        vector.wait_ge(p.sv.h, p.sv.n)
        p.sv.inc(vector.tensor_mul(tB, tB, q["sg"][:, 0:width]))
        p.sv.inc(vector.tensor_copy(
            tA.rearrange("p (a b) -> p a b", b=16),
            q["sc2"][:, 0:nb].unsqueeze(2).broadcast_to([128, nb, 16])))
        vector.wait_ge(p.sv.h, p.sv.n)
        for sem, cnt in wv_dst:
            vector.wait_ge(sem, cnt)
        p.sv.inc(vector.tensor_mul(dst, tB, tA))
    assert p.sv.n == sv0 + QVB
    return p.sv.n


def emit_quant(p, q, src, dst, width, wv=(), wv_dst=()):
    sv_y, ss_cast = emit_quant_a(p, q, src, width, wv=wv)
    sv = emit_quant_b(p, q, src, dst, width, ss_cast, wv_dst=wv_dst)
    return sv, ss_cast


def emit_ffn1(p, g, xd, Wg, Wu, Wd, cch, lnst, first):
    nc, block = p.nc, p.block
    ar = p.arena
    ar.reset()

    GROUPS = [(0, 6), (6, 6), (12, 6), (18, 4)]

    wgq = ar.bf16([128, NCH, 768])
    wuq = ar.bf16([128, NCH, 768])
    wdq = ar.bf16([128, 6, 1024])
    xq_fm = ar.bf16([128, NCH, S])
    ybuf = [ar.f32([128, 1024]) for _ in range(2)]
    qout = [ar.bf16([128, 1024]) for _ in range(2)]
    hq_fm = [ar.bf16([128, 6, 128]) for _ in range(2)]
    ttile = [ar.bf16([128, 1024]) for _ in range(2)]
    ab2 = [ar.f32([128, 1024]) for _ in range(2)]
    sg2 = [ar.bf16([128, 1024]) for _ in range(2)]
    q0 = dict(bm=ar.f32([128, 64]), sc2=ar.f32([128, 64]),
              inv=ar.f32([128, 64]), invx=ar.f32([128, 1024]),
              tA=ar.bf16([128, 1024]), tB=ar.bf16([128, 1024]),
              tC=ar.bf16([128, 1024]))
    qs = [dict(q0, ab=ab2[0], sg=sg2[0]), dict(q0, ab=ab2[1], sg=sg2[1])]

    ybuf_free = [(p.sv.n, p.ss.n)] * 2
    qout_free = [p.st.n, p.st.n]
    ttile_free = [p.dps.n, p.dps.n]
    pstp_free = [p.sv.n]

    prep_stores = p.dps.n
    if first:
        prep = []
        for rc in range(22):
            prep.append((Wg[rc * 128:(rc + 1) * 128, :], 1024,
                         cch["wgT"][:, :, rc * 128:(rc + 1) * 128], NCH))
        for rc in range(22):
            prep.append((Wu[rc * 128:(rc + 1) * 128, :], 1024,
                         cch["wuT"][:, :, rc * 128:(rc + 1) * 128], NCH))
        for rc in range(8):
            for ks in range(3):
                ksw = 768 if ks == 2 else 1024
                prep.append((Wd[rc * 128:(rc + 1) * 128,
                                ks * 1024:ks * 1024 + ksw], ksw,
                             cch["wdT"][:, ks * 8:ks * 8 + ksw // 128,
                                        rc * 128:(rc + 1) * 128], ksw // 128))
        ldv, cpv, acast, bsv = {}, {}, {}, {}

        def emit_store(sync, jj):
            wdt_, dstw_, ntr_ = prep[jj][1], prep[jj][2], prep[jj][3]
            sync.wait_ge(p.sv.h, cpv[jj])
            p.dps.start(sync, dstw_,
                        ttile[jj % 2][:, 0:wdt_].rearrange(
                            "p (c t) -> p c t", c=ntr_))
            ttile_free[jj % 2] = p.dps.n

        def prep_b(j):
            wdt, ntr = prep[j][1], prep[j][3]
            bsv[j] = emit_quant_b(p, qs[j % 2], ybuf[j % 2][:, 0:wdt],
                                  qout[j % 2][:, 0:wdt], wdt, acast[j],
                                  wv_dst=[(p.st.h, qout_free[j % 2])])

            @block.tensor
            def _(tensor, j=j, wdt=wdt, ntr=ntr, pf=pstp_free[0]):
                tensor.wait_ge(p.sv.h, max(bsv[j], pf))
                ins = None
                for c in range(ntr):
                    ins = nc.tensor.transpose(
                        g["ps_tp"][:, c * 128:(c + 1) * 128],
                        qout[j % 2][:, c * 128:(c + 1) * 128], g["ident"][:])
                p.st.inc(ins)
            qout_free[j % 2] = p.st.n
            st_tr = p.st.n

            @block.vector
            def _(vector, j=j, wdt=wdt, st_tr=st_tr, tf=ttile_free[j % 2]):
                vector.wait_ge(p.st.h, st_tr)
                vector.wait_ge(p.dps.h, tf)
                cpv[j] = p.sv.inc(vector.tensor_copy(ttile[j % 2][:, 0:wdt],
                                                     g["ps_tp"][:, 0:wdt]))
            pstp_free[0] = p.sv.n

        for j, (srcw, wdt, dstw, ntr) in enumerate(prep):
            yfs, yfss = ybuf_free[j % 2]

            @block.sync
            def _(sync, j=j, srcw=srcw, wdt=wdt, yfs=yfs, yfss=yfss):
                if j >= 2:
                    sync.wait_ge(p.sv.h, bsv[j - 2])
                sync.wait_ge(p.ss.h, yfss)
                s_ = p.dp[j % 4]
                ldv[j] = (s_.h, s_.start(sync, ybuf[j % 2][:, 0:wdt], srcw))
                if j >= 2:
                    emit_store(sync, j - 2)

            if j >= 1:
                prep_b(j - 1)
            ybuf_free[j % 2] = emit_quant_a(
                p, qs[j % 2], ybuf[j % 2][:, 0:wdt], wdt, wv=[ldv[j]])
            acast[j] = ybuf_free[j % 2][1]
        prep_b(len(prep) - 1)

        @block.sync
        def _(sync):
            emit_store(sync, len(prep) - 2)
            emit_store(sync, len(prep) - 1)
        prep_stores = p.dps.n

    dq2 = {}
    ss_x0 = p.ss.n

    @block.sync
    def _(sync):
        sync.wait_ge(p.ss.h, lnst["ss_sq"])
        sync.wait_ge(p.sv.h, lnst["sv_red16"])
        for i in range(NT):
            if i >= 2:
                sync.wait_ge(p.ss.h, ss_x0 + 3 * (i - 2) + 1)
            dq2[i] = p.dx[i % 2].start(sync, g["xtile"][i % 2][:], xd[i])

    xev, nrm, xacast, xbsv = {}, {}, {}, {}

    def x_b(i):
        xbsv[i] = emit_quant_b(p, qs[i % 2], ybuf[i % 2][:], qout[i % 2][:],
                               1024, xacast[i],
                               wv_dst=[(p.st.h, qout_free[i % 2])])

        @block.tensor
        def _(tensor, i=i, pf=pstp_free[0]):
            tensor.wait_ge(p.sv.h, max(xbsv[i], pf))
            ins = None
            for c in range(NCH):
                ins = nc.tensor.transpose(g["ps_tp"][:, c * 128:(c + 1) * 128],
                                          qout[i % 2][:, c * 128:(c + 1) * 128],
                                          g["ident"][:])
            p.st.inc(ins)
        qout_free[i % 2] = p.st.n
        st_tr = p.st.n

        @block.vector
        def _(vector, i=i, st_tr=st_tr):
            vector.wait_ge(p.st.h, st_tr)
            xev[i] = p.sv.inc(vector.tensor_copy(
                xq_fm[:, :, i * 128:(i + 1) * 128],
                g["ps_tp"][:].rearrange("p (c t) -> p c t", c=NCH)))
        pstp_free[0] = p.sv.n

    for i in range(NT):
        @block.scalar
        def _(scalar, i=i, yf=ybuf_free[i % 2][0]):
            if i == 0:
                scalar.wait_ge(p.sv.h, lnst["sv_stats"])
            scalar.wait_ge(p.dx[i % 2].h, dq2[i])
            scalar.wait_ge(p.sv.h, yf)
            if i >= 2:
                scalar.wait_ge(p.sv.h, xbsv.get(i - 2, 0))
            nrm[i] = p.ss.inc(scalar.activation(
                ybuf[i % 2][:], g["xtile"][i % 2][:], AF.Identity,
                bias=g["nmrs"][:, i:i + 1], scale=g["rstd"][:, i:i + 1]))
        assert nrm[i] == ss_x0 + 3 * i + 1

        if i >= 1:
            x_b(i - 1)
        ybuf_free[i % 2] = emit_quant_a(p, qs[i % 2], ybuf[i % 2][:], 1024,
                                        wv=[(p.ss.h, nrm[i])])
        xacast[i] = ybuf_free[i % 2][1]
    x_b(NT - 1)

    wld = {}
    guv, trh, dnv, silv, hmv, evv, resv, ldx = {}, {}, {}, {}, {}, {}, {}, {}
    gacast, gbsv = {}, {}
    war = dict(res=0, sil=0, hm=0)
    for gi, (k0, ng) in enumerate(GROUPS):
        gw = ng * 128
        subs = [(0, 512), (512, gw - 512)] if gw > 512 else [(0, gw)]

        @block.gpsimd
        def _(gpsimd, gi=gi, k0=k0, ng=ng):
            if first and gi == 0:
                gpsimd.wait_ge(p.dps.h, prep_stores)
            if gi >= 1:
                gpsimd.wait_ge(p.st.h, guv[(gi - 1, NT - 1)])
                gpsimd.wait_ge(p.st.h, dnv[(gi - 1, NT - 1)])
            wr = p.load_par(gpsimd, wgq[:, :, 0:gw],
                            cch["wgT"][:, :, k0 * 128:k0 * 128 + gw], n=2)
            wr += p.load_par(gpsimd, wuq[:, :, 0:gw],
                             cch["wuT"][:, :, k0 * 128:k0 * 128 + gw], n=2)
            c1_ = p.dw[0].start(gpsimd, wdq[:, 0:ng // 2, :],
                                cch["wdT"][:, k0:k0 + ng // 2, :])
            c2_ = p.dw[1].start(gpsimd, wdq[:, ng // 2:ng, :],
                                cch["wdT"][:, k0 + ng // 2:k0 + ng, :])
            wld[gi] = wr + [(p.dw[0].h, c1_), (p.dw[1].h, c2_)]

        for ii in range(NT + 2):
            @block.tensor
            def _(tensor, gi=gi, ii=ii, gw=gw, subs=subs,
                  sil_war=war["sil"], hm_war=war["hm"]):
                if ii < NT:
                    wwait(p, tensor, wld[gi])
                    tensor.wait_ge(p.sv.h, xev[ii])
                    if sil_war:
                        tensor.wait_ge(p.ss.h, sil_war)
                        tensor.wait_ge(p.sv.h, hm_war)
                    for wq_, off in ((wgq, 0), (wuq, 1024)):
                        ins = None
                        for so, sw in subs:
                            for c in range(NCH):
                                ins = nc.tensor.matmul(
                                    g["ps_pv"][:, off + so:off + so + sw],
                                    xq_fm[:, c, ii * 128:(ii + 1) * 128],
                                    wq_[:, c, so:so + sw],
                                    start=(c == 0), stop=(c == NCH - 1))
                        p.st.inc(ins)
                    guv[(gi, ii)] = p.st.n

            if ii < NT:
                @block.scalar
                def _(scalar, gi=gi, ii=ii, gw=gw, yf=ybuf_free[ii % 2][0]):
                    scalar.wait_ge(p.st.h, guv[(gi, ii)] - 1)
                    scalar.wait_ge(p.sv.h, yf)
                    bprev = gbsv.get((gi, ii - 2), xbsv.get(ii - 2 + NT, 0))
                    if bprev:
                        scalar.wait_ge(p.sv.h, bprev)
                    silv[(gi, ii)] = p.ss.inc(scalar.activation(
                        ybuf[ii % 2][:, 0:gw], g["ps_pv"][:, 0:gw], AF.Silu))
                war["sil"] = silv[(gi, ii)]

            if ii >= 1 and ii - 1 < NT:
                gbsv[(gi, ii - 1)] = emit_quant_b(
                    p, qs[(ii - 1) % 2], ybuf[(ii - 1) % 2][:, 0:gw],
                    qout[(ii - 1) % 2][:, 0:gw], gw, gacast[(gi, ii - 1)],
                    wv_dst=[(p.st.h, qout_free[(ii - 1) % 2])])

            if ii < NT:
                @block.vector
                def _(vector, gi=gi, ii=ii, gw=gw):
                    vector.wait_ge(p.ss.h, silv[(gi, ii)])
                    vector.wait_ge(p.st.h, guv[(gi, ii)])
                    hmv[(gi, ii)] = p.sv.inc(vector.tensor_mul(
                        ybuf[ii % 2][:, 0:gw], ybuf[ii % 2][:, 0:gw],
                        g["ps_pv"][:, 1024:1024 + gw]))
                war["hm"] = hmv[(gi, ii)]
                ybuf_free[ii % 2] = emit_quant_a(
                    p, qs[ii % 2], ybuf[ii % 2][:, 0:gw], gw)
                gacast[(gi, ii)] = ybuf_free[ii % 2][1]

            if ii >= 1 and ii - 1 < NT:
                @block.tensor
                def _(tensor, gi=gi, ii=ii, ng=ng, pf=pstp_free[0]):
                    tensor.wait_ge(p.sv.h, max(gbsv[(gi, ii - 1)], pf))
                    ins = None
                    for c in range(ng):
                        ins = nc.tensor.transpose(
                            g["ps_tp"][:, c * 128:(c + 1) * 128],
                            qout[(ii - 1) % 2][:, c * 128:(c + 1) * 128],
                            g["ident"][:])
                    p.st.inc(ins)
                    trh[(gi, ii - 1)] = p.st.n
                qout_free[(ii - 1) % 2] = trh[(gi, ii - 1)]

                @block.vector
                def _(vector, gi=gi, ii=ii, ng=ng, gw=gw):
                    vector.wait_ge(p.st.h, trh[(gi, ii - 1)])
                    if (gi, ii - 3) in dnv:
                        vector.wait_ge(p.st.h, dnv[(gi, ii - 3)])
                    evv[(gi, ii - 1)] = p.sv.inc(vector.tensor_copy(
                        hq_fm[(ii - 1) % 2][:, 0:ng, :],
                        g["ps_tp"][:, 0:gw].rearrange("p (c t) -> p c t", c=ng)))
                pstp_free[0] = evv[(gi, ii - 1)]

            if ii >= 2 and ii - 2 < NT:
                @block.vector
                def _(vector, gi=gi, ii=ii):
                    vector.wait_ge(p.st.h, dnv[(gi, ii - 2)])
                    vector.wait_ge(p.dx[(ii - 2) % 2].h, ldx[(gi, ii - 2)])
                    xt = g["xtile"][(ii - 2) % 2]
                    p.sv.inc(vector.tensor_add(xt[:, 0:512], xt[:, 0:512],
                                               g["ps_e"][:]))
                    resv[(gi, ii - 2)] = p.sv.inc(vector.tensor_add(
                        xt[:, 512:1024], xt[:, 512:1024], g["ps_s"][0][:]))
                war["res"] = resv[(gi, ii - 2)]

            if ii >= 1 and ii - 1 < NT:
                @block.tensor
                def _(tensor, gi=gi, ii=ii, ng=ng, res_war=war["res"]):
                    tensor.wait_ge(p.sv.h, evv[(gi, ii - 1)])
                    if res_war:
                        tensor.wait_ge(p.sv.h, res_war)
                    for pb, off in ((g["ps_e"], 0), (g["ps_s"][0], 512)):
                        ins = None
                        for c in range(ng):
                            ins = nc.tensor.matmul(
                                pb[:], hq_fm[(ii - 1) % 2][:, c, :],
                                wdq[:, c, off:off + 512],
                                start=(c == 0), stop=(c == ng - 1))
                        p.st.inc(ins)
                    dnv[(gi, ii - 1)] = p.st.n

            @block.sync
            def _(sync, gi=gi, ii=ii):
                if ii >= 2 and ii - 2 < NT:
                    sync.wait_ge(p.sv.h, resv[(gi, ii - 2)])
                    p.dx[(ii - 2) % 2].start(sync, xd[ii - 2],
                                             g["xtile"][(ii - 2) % 2][:])
                if ii < NT:
                    if gi == 0:
                        sync.wait_ge(p.ss.h, nrm[NT - 2 + (ii % 2)])
                    ldx[(gi, ii)] = p.dx[ii % 2].start(
                        sync, g["xtile"][ii % 2][:], xd[ii])


def emit_proj_tm_act(p, g, wslot, dst_fn, w_ready, act):
    nc, block = p.nc, p.block
    st0, ss0 = p.st.n, p.ss.n

    @block.tensor
    def _(tensor):
        wwait(p, tensor, w_ready, wslot)
        tensor.wait_ge(p.sv.h, p.sv_xn)
        tensor.wait_ge(p.ss.h, ss0)
        for ih in range(NT * 2):
            i, half = divmod(ih, 2)
            pb = g["ps_s"][ih % 2]
            if ih >= 2:
                tensor.wait_ge(p.ss.h, ss0 + ih - 1)
            ins = None
            for c in range(NCH):
                ins = nc.tensor.matmul(pb[:],
                                       g["xn_fm"][:, c, i * 128:(i + 1) * 128],
                                       g["wbuf"][wslot][:, c, half * 512:(half + 1) * 512],
                                       start=(c == 0), stop=(c == NCH - 1))
            p.st.inc(ins)

    @block.scalar
    def _(scalar):
        for ih in range(NT * 2):
            i, half = divmod(ih, 2)
            scalar.wait_ge(p.st.h, st0 + ih + 1)
            p.ss.inc(scalar.activation(dst_fn(i, half), g["ps_s"][ih % 2][:], act))
    return p.st.n, p.ss.n


def emit_proj_fm_act(p, g, wslot, dst_fm, w_ready, act):
    nc, block = p.nc, p.block
    st0, ss0 = p.st.n, p.ss.n

    @block.tensor
    def _(tensor):
        wwait(p, tensor, w_ready, wslot)
        tensor.wait_ge(p.sv.h, p.sv_xn)
        tensor.wait_ge(p.ss.h, ss0)
        for k in range(32):
            oc, sp = divmod(k, 4)
            pb = g["ps_s"][k % 2]
            if k >= 2:
                tensor.wait_ge(p.ss.h, ss0 + k - 1)
            ins = None
            for c in range(NCH):
                ins = nc.tensor.matmul(pb[:],
                                       g["wbuf"][wslot][:, c, oc * 128:(oc + 1) * 128],
                                       g["xn_fm"][:, c, sp * 512:(sp + 1) * 512],
                                       start=(c == 0), stop=(c == NCH - 1))
            p.st.inc(ins)

    @block.scalar
    def _(scalar):
        for k in range(32):
            oc, sp = divmod(k, 4)
            scalar.wait_ge(p.st.h, st0 + k + 1)
            p.ss.inc(scalar.activation(dst_fm[:, oc, sp * 512:(sp + 1) * 512],
                                       g["ps_s"][k % 2][:], act))
    return p.st.n, p.ss.n


def emit_lightning(p, g, xd, Wq, Wk, Wv, Wo, gamma, beta, Wg1, Wg2):
    nc, block = p.nc, p.block
    ar = p.arena
    ar.reset()

    def build_slopes(n):
        def po2(m):
            start = 2.0 ** (-2.0 ** (-(np.log2(m) - 3)))
            return [start * start ** i for i in range(m)]
        if np.log2(n).is_integer():
            return po2(n)
        closest = 2 ** int(np.floor(np.log2(n)))
        return po2(closest) + build_slopes(2 * closest)[0::2][:n - closest]
    s_np = np.array(build_slopes(NH), dtype=np.float32)
    s_np = s_np * (1.0 - 0 / (12 - 1) + 1e-05)
    slopes = [float(v) for v in s_np]
    lamc = [float(np.exp(np.float64(-v) * 128.0)) for v in slopes]

    q_fm = ar.bf16([128, NCH, S])
    k_tm = ar.bf16([128, NT, D])
    v_tm = ar.bf16([128, NT, D])
    kpair = ar.bf16([128, S])
    maskz = ar.bf16([128, NH, 128])
    kv_f = ar.f32([128, 8, HD])
    kv_b = ar.bf16([128, 8, HD])
    g1_fm = ar.bf16([128, S])
    _og = ar.bf16([128, 1024])
    og_t = [_og, _og]
    bb_bc = ar.bf16([128, 1024])
    wg1b = ar.bf16([128, 8, 64])
    o_tm = g["xn_fm"].rearrange("p c t -> p (c t)").rearrange(
        "p (a b) -> p a b", a=NT)
    og_fm = q_fm
    lamq = g["hmax"]
    lamk = g["segr"]
    gb_bc = g["sq2"][0]

    @block.gpsimd
    def _(gpsimd):
        g["_a"] = p.load_par(gpsimd, g["wbuf"][0][:],
                             Wq.rearrange("(c p) o -> p c o", p=128))
        p.dw[1].start(gpsimd, g["biasrow"][:, 0:D].unsqueeze(1),
                      gamma.rearrange("(a d) -> a d", a=1))
        p.dw[1].start(gpsimd, g["biasrow"][:, D:2 * D].unsqueeze(1),
                      beta.rearrange("(a d) -> a d", a=1))
        g["_b"] = p.load_par(gpsimd, g["wbuf"][1][:],
                             Wk.rearrange("(c p) o -> p c o", p=128))
    wq_rdy = g.pop("_a"); wk_rdy = g.pop("_b")

    st_q, _ = emit_proj_fm_act(p, g, 0, q_fm, wq_rdy, AF.Silu)

    @block.gpsimd
    def _(gpsimd):
        gpsimd.wait_ge(p.st.h, st_q)
        g["_a"] = p.load_par(gpsimd, g["wbuf"][0][:],
                             Wv.rearrange("(c p) o -> p c o", p=128))
    wv_rdy = g.pop("_a")

    st_k, _ = emit_proj_tm_act(p, g, 1,
                               lambda i, h: k_tm[:, i, h * 512:(h + 1) * 512],
                               wk_rdy, AF.Silu)

    @block.gpsimd
    def _(gpsimd):
        gpsimd.wait_ge(p.st.h, st_k)
        p.dw[1].start(gpsimd, wg1b[:],
                      Wg1.rearrange("(c p) o -> p c o", p=128))
        g["_b"] = p.dw[1].start(gpsimd, g["wbuf"][1][0:64, 5, :], Wg2)
    wg_rdy = g.pop("_b")

    st_v, ss_v = emit_proj_tm_act(p, g, 0,
                                  lambda i, h: v_tm[:, i, h * 512:(h + 1) * 512],
                                  wv_rdy, AF.Silu)

    @block.gpsimd
    def _(gpsimd):
        gpsimd.wait_ge(p.ss.h, ss_v)
        g["_a"] = p.load_par(gpsimd, g["wbuf"][0][:],
                             Wo.rearrange("(c p) o -> p c o", p=128))
    wo_rdy = g.pop("_a")

    ss_m0 = p.ss.n
    sv_mzb = p.sv.n

    @block.scalar
    def _(scalar):
        scalar.wait_ge(p.dmisc.h, p.dq_consts)
        for h in range(NH):
            if h >= 1:
                scalar.wait_ge(p.sv.h, sv_mzb + h)
            p.ss.inc(scalar.activation(maskz[:, h, :], g["dmatTc"][:], AF.Exp,
                                       scale=-slopes[h]))
            p.ss.inc(scalar.activation(lamq[:, h:h + 1], g["idxp1"][:], AF.Exp,
                                       scale=-slopes[h]))
            p.ss.inc(scalar.activation(lamk[:, h:h + 1], g["idxrv"][:], AF.Exp,
                                       scale=-slopes[h]))

    @block.vector
    def _(vector):
        for h in range(NH):
            vector.wait_ge(p.ss.h, ss_m0 + 3 * h + 1)
            p.sv.inc(vector.tensor_tensor(maskz[:, h, :], maskz[:, h, :],
                                          g["triT"][:], ALU.mult))
        p.sv.inc(vector.memset(kv_f[:], 0.0))
        p.sv.inc(vector.memset(kv_b[:], 0.0))
    sv_mz = p.sv.n
    assert sv_mz == sv_mzb + NH + 2

    st_g0, sv_g0 = p.st.n, p.sv.n

    @block.tensor
    def _(tensor):
        tensor.wait_ge(p.dw[1].h, wg_rdy)
        tensor.wait_ge(p.sv.h, max(p.sv_xn, sv_mz))
        for sp in range(4):
            if sp >= 1:
                tensor.wait_ge(p.sv.h, sv_g0 + sp)
            ins = None
            for c in range(NCH):
                ins = nc.tensor.matmul(g["ps_e"][0:64, 0:512],
                                       wg1b[:, c, :],
                                       g["xn_fm"][:, c, sp * 512:(sp + 1) * 512],
                                       start=(c == 0), stop=(c == NCH - 1))
            p.st.inc(ins)

    @block.vector
    def _(vector):
        for sp in range(4):
            vector.wait_ge(p.st.h, st_g0 + sp + 1)
            p.sv.inc(vector.tensor_copy(g1_fm[0:64, sp * 512:(sp + 1) * 512],
                                        g["ps_e"][0:64, 0:512]))
    sv_g1 = p.sv.n

    st_gb0 = p.st.n

    @block.tensor
    def _(tensor):
        tensor.wait_ge(p.sv.h, sv_g1)
        for j in range(4):
            if j >= 1:
                tensor.wait_ge(p.sv.h, sv_g1 + j)
            off = (0 if j < 2 else D) + (j % 2) * 512
            p.st.inc(nc.tensor.matmul(g["ps_e"][:, 0:512], g["onescol"][:],
                                      g["biasrow"][:, off:off + 512],
                                      start=True, stop=True))

    @block.vector
    def _(vector):
        for j in range(4):
            vector.wait_ge(p.st.h, st_gb0 + j + 1)
            dst = gb_bc if j < 2 else bb_bc
            p.sv.inc(vector.tensor_copy(dst[:, (j % 2) * 512:(j % 2) * 512 + 512],
                                        g["ps_e"][:, 0:512]))
    sv_gb = p.sv.n

    order = [(pr, i, hh) for pr in range(8) for i in range(NT) for hh in range(2)]
    LIGHT_ORDER = {order[k]: order[k - 1] for k in range(1, len(order))}
    LIGHT_ORDER[order[0]] = None
    LIGHT_PREV_HH = {}
    _last = {}
    for _key in order:
        LIGHT_PREV_HH[_key] = _last.get(_key[2])
        _last[_key[2]] = _key
    stc = {}; svc = {}
    st_c = p.st.n; sv_c2 = p.sv.n
    for pr in range(8):
        st_c += 2; stc[("kT", pr)] = st_c
        for i in range(NT):
            for hh in range(2):
                st_c += 1; stc[("S", pr, i, hh)] = st_c
                st_c += 1; stc[("o", pr, i, hh)] = st_c
                st_c += 1; stc[("kv", pr, i, hh)] = st_c
    for pr in range(8):
        sv_c2 += 2; svc[("kTe", pr)] = sv_c2
        for i in range(NT):
            for hh in range(2):
                sv_c2 += 1; svc[("pt", pr, i, hh)] = sv_c2
                sv_c2 += 1; svc[("kl", pr, i, hh)] = sv_c2
                sv_c2 += 2; svc[("oe", pr, i, hh)] = sv_c2
                sv_c2 += 2; svc[("kf", pr, i, hh)] = sv_c2

    @block.tensor
    def _(tensor):
        tensor.wait_ge(p.ss.h, ss_v)
        tensor.wait_ge(p.sv.h, sv_gb)
        for pr in range(8):
            for b in range(2):
                if (pr, b) != (0, 0):
                    prevk = (pr, 0) if b else (pr - 1, 1)
                    tensor.wait_ge(p.sv.h, svc[("kTe", prevk[0])] - (1 - prevk[1]))
                ins = None
                for t in range(8):
                    ins = nc.tensor.transpose(
                        g["ps_tp"][:, t * 128:(t + 1) * 128],
                        k_tm[:, b * 8 + t, pr * 128:(pr + 1) * 128],
                        g["ident"][:])
                p.st.inc(ins)
            tensor.wait_ge(p.sv.h, svc[("kTe", pr)])
            for i in range(NT):
                for hh in range(2):
                    h = 2 * pr + hh
                    base = hh * 64
                    key = (pr, i, hh)
                    prev_it = LIGHT_ORDER.get(key)
                    prev_hh = LIGHT_PREV_HH.get(key)
                    if prev_it is not None and LIGHT_ORDER.get(prev_it) is not None:
                        p2 = LIGHT_ORDER[prev_it]
                        tensor.wait_ge(p.sv.h, svc[("pt",) + p2])
                    p.st.inc(nc.tensor.matmul(
                        g["ps_s"][(i * 2 + hh) % 2][:, 0:128],
                        kpair[base:base + 64, i * 128:(i + 1) * 128],
                        q_fm[base:base + 64, pr, i * 128:(i + 1) * 128],
                        start=True, stop=True))
                    nlin = pr * 32 + i * 2 + hh
                    ob = (nlin % 2) * 512
                    kb = 1024 + (nlin % 2) * 512
                    prev2 = LIGHT_ORDER.get(prev_it) if prev_it else None
                    tensor.wait_ge(p.sv.h, svc[("pt", pr, i, hh)])
                    if i >= 1:
                        tensor.wait_ge(p.sv.h, svc[("kf", pr, i - 1, hh)])
                    if prev2 is not None:
                        tensor.wait_ge(p.sv.h, svc[("oe",) + prev2])
                    nc.tensor.matmul(g["ps_pv"][:, ob:ob + 64],
                                     g["ptb"][0][:, 0:128],
                                     v_tm[:, i, h * 64:(h + 1) * 64],
                                     start=True, stop=True)
                    p.st.inc(nc.tensor.matmul(
                        g["ps_pv"][:, ob + 64:ob + 128],
                        q_fm[base:base + 64, pr, i * 128:(i + 1) * 128],
                        kv_b[base:base + 64, pr, :],
                        start=True, stop=True))
                    tensor.wait_ge(p.sv.h, svc[("kl", pr, i, hh)])
                    if prev2 is not None:
                        tensor.wait_ge(p.sv.h, svc[("kf",) + prev2])
                    p.st.inc(nc.tensor.matmul(
                        g["ps_pv"][0:64, kb:kb + 64],
                        g["ptb"][1][:, 0:64],
                        v_tm[:, i, h * 64:(h + 1) * 64],
                        start=True, stop=True))

    @block.vector
    def _(vector):
        vector.wait_ge(p.ss.h, ss_m0 + 3 * NH)
        for pr in range(8):
            for b in range(2):
                vector.wait_ge(p.st.h, stc[("kT", pr)] - 1 + b)
                p.sv.inc(vector.tensor_copy(kpair[:, b * 1024:(b + 1) * 1024],
                                            g["ps_tp"][:]))
            for i in range(NT):
                for hh in range(2):
                    h = 2 * pr + hh
                    base = hh * 64
                    vector.wait_ge(p.st.h, stc[("S", pr, i, hh)])
                    p.sv.inc(vector.tensor_tensor(
                        g["ptb"][0][:, 0:128],
                        g["ps_s"][(i * 2 + hh) % 2][:, 0:128],
                        maskz[:, h, :], ALU.mult))
                    prev_it = LIGHT_ORDER.get((pr, i, hh))
                    if prev_it is not None:
                        vector.wait_ge(p.st.h, stc[("kv",) + prev_it])
                    p.sv.inc(vector.tensor_scalar(
                        g["ptb"][1][:, 0:64],
                        k_tm[:, i, h * 64:(h + 1) * 64],
                        lamk[:, h:h + 1], None, ALU.mult))
                    nlin = pr * 32 + i * 2 + hh
                    ob = (nlin % 2) * 512
                    kb = 1024 + (nlin % 2) * 512
                    vector.wait_ge(p.st.h, stc[("o", pr, i, hh)])
                    p.sv.inc(vector.tensor_scalar(
                        o_tm[:, i, h * 64:(h + 1) * 64],
                        g["ps_pv"][:, ob + 64:ob + 128], lamq[:, h:h + 1],
                        None, ALU.mult))
                    vector.wait_ge(p.sv.h, p.sv.n)
                    p.sv.inc(vector.tensor_add(
                        o_tm[:, i, h * 64:(h + 1) * 64],
                        o_tm[:, i, h * 64:(h + 1) * 64],
                        g["ps_pv"][:, ob:ob + 64]))
                    vector.wait_ge(p.st.h, stc[("kv", pr, i, hh)])
                    p.sv.inc(vector.scalar_tensor_tensor(
                        kv_f[base:base + 64, pr, :],
                        kv_f[base:base + 64, pr, :], lamc[h],
                        g["ps_pv"][0:64, kb:kb + 64], ALU.mult, ALU.add))
                    vector.wait_ge(p.sv.h, p.sv.n)
                    p.sv.inc(vector.tensor_copy(kv_b[base:base + 64, pr, :],
                                                kv_f[base:base + 64, pr, :]))
    assert p.st.n == st_c, (p.st.n, st_c)
    assert p.sv.n == sv_c2, (p.sv.n, sv_c2)

    sm, sq = g["sm"], g["sq"]
    ss_s0 = p.ss.n
    sv_s0 = p.sv.n

    @block.scalar
    def _(scalar):
        for i in range(NT):
            scalar.wait_ge(p.sv.h, svc[("oe", pr, i, 1)] if False else sv_c2)
            if i >= 1:
                scalar.wait_ge(p.ss.h, ss_s0 + i)
            p.ss.inc(scalar.activation(og_t[0][:], o_tm[:, i, :], AF.Square,
                                       accum_out=sq[:, i:i + 1]))
    ss_sq = p.ss.n

    @block.vector
    def _(vector):
        for i in range(NT):
            p.sv.inc(vector.reduce_sum(sm[:, i:i + 1].unsqueeze(2),
                                       o_tm[:, i, :].unsqueeze(1), AX.X))
        vector.wait_ge(p.ss.h, ss_sq)
        vector.wait_ge(p.sv.h, p.sv.n)
        p.sv.inc(vector.tensor_scalar(g["mean"][:], sm[:], 1.0 / D, None, ALU.mult))
        p.sv.inc(vector.tensor_scalar(g["var"][:], sq[:], 1.0 / D, None, ALU.mult))
        vector.wait_ge(p.sv.h, p.sv.n)
        p.sv.inc(vector.tensor_mul(sm[:], g["mean"][:], g["mean"][:]))
        vector.wait_ge(p.sv.h, p.sv.n)
        p.sv.inc(vector.tensor_sub(g["var"][:], sm[:], g["var"][:]))
    sv_var = p.sv.n

    @block.scalar
    def _(scalar):
        scalar.wait_ge(p.sv.h, sv_var)
        p.ss.inc(scalar.activation(g["rstd"][:], g["var"][:], AF.Ln, scale=-1.0,
                                   bias=g["ceps"][:]))
        scalar.wait_ge(p.ss.h, p.ss.n)
        p.ss.inc(scalar.activation(g["rstd"][:], g["rstd"][:], AF.Exp, scale=-0.5))
    ss_rstd = p.ss.n

    @block.vector
    def _(vector):
        vector.wait_ge(p.ss.h, ss_rstd)
        p.sv.inc(vector.scalar_tensor_tensor(g["nmrs"][:], g["mean"][:], -1.0,
                                             g["rstd"][:], ALU.mult, ALU.mult))
    sv_stats = p.sv.n

    stg = {}; ssg = {}; svg = {}
    st_c = p.st.n; ss_c = p.ss.n; sv_c3 = p.sv.n
    for i in range(NT):
        st_c += 2; stg[("g2", i)] = st_c
        st_c += 1; stg[("ogT", i)] = st_c
    for i in range(NT):
        ss_c += 2; ssg[("sig", i)] = ss_c
        ss_c += 1; ssg[("on", i)] = ss_c
    for i in range(NT):
        sv_c3 += 3; svg[("og", i)] = sv_c3
        sv_c3 += 1; svg[("ogTe", i)] = sv_c3

    @block.tensor
    def _(tensor):
        wwait(p, tensor, wo_rdy, 0)
        for i in range(NT):
            if i >= 1:
                tensor.wait_ge(p.sv.h, svg[("og", i - 1)])
            for half in range(2):
                p.st.inc(nc.tensor.matmul(
                    g["ps_s"][half][:],
                    g1_fm[0:64, i * 128:(i + 1) * 128],
                    g["wbuf"][1][0:64, 5, half * 512:(half + 1) * 512],
                    start=True, stop=True))
            tensor.wait_ge(p.sv.h, svg[("og", i)])
            if i >= 2:
                tensor.wait_ge(p.sv.h, svg[("ogTe", i - 2)] + 0)
            ins = None
            for c in range(NCH):
                ins = nc.tensor.transpose(g["ps_tp"][:, c * 128:(c + 1) * 128],
                                          og_t[i % 2][:, c * 128:(c + 1) * 128],
                                          g["ident"][:])
            p.st.inc(ins)

    @block.scalar
    def _(scalar):
        scalar.wait_ge(p.sv.h, sv_stats)
        for i in range(NT):
            scalar.wait_ge(p.st.h, stg[("g2", i)])
            p.ss.inc(scalar.activation(g["ps_s"][0][:], g["ps_s"][0][:], AF.Sigmoid))
            p.ss.inc(scalar.activation(g["ps_s"][1][:], g["ps_s"][1][:], AF.Sigmoid))
            if i >= 1:
                scalar.wait_ge(p.st.h, stg[("ogT", i - 1)])
            p.ss.inc(scalar.activation(og_t[i % 2][:], o_tm[:, i, :], AF.Identity,
                                       bias=g["nmrs"][:, i:i + 1],
                                       scale=g["rstd"][:, i:i + 1]))

    @block.vector
    def _(vector):
        for i in range(NT):
            vector.wait_ge(p.ss.h, ssg[("on", i)])
            p.sv.inc(vector.tensor_mul(og_t[i % 2][:], og_t[i % 2][:], gb_bc[:]))
            vector.wait_ge(p.sv.h, p.sv.n)
            p.sv.inc(vector.tensor_add(og_t[i % 2][:], og_t[i % 2][:], bb_bc[:]))
            vector.wait_ge(p.sv.h, p.sv.n)
            for half in range(2):
                ins = vector.tensor_tensor(
                    og_t[i % 2][:, half * 512:(half + 1) * 512],
                    og_t[i % 2][:, half * 512:(half + 1) * 512],
                    g["ps_s"][half][:], ALU.mult)
            p.sv.inc(ins)
            vector.wait_ge(p.st.h, stg[("ogT", i)])
            p.sv.inc(vector.tensor_copy(
                og_fm[:, :, i * 128:(i + 1) * 128],
                g["ps_tp"][:].rearrange("p (c t) -> p c t", c=NCH)))
    assert p.st.n == st_c
    assert p.ss.n == ss_c
    assert p.sv.n == sv_c3, (p.sv.n, sv_c3)
    sv_ogfm = p.sv.n

    st_o0 = p.st.n
    sv_o0 = p.sv.n
    ld_v = {}
    ss_entry = p.ss.n

    @block.tensor
    def _(tensor):
        tensor.wait_ge(p.sv.h, sv_ogfm)
        for ih in range(NT * 2):
            i, half = divmod(ih, 2)
            pb = g["ps_s"][ih % 2]
            if ih >= 2:
                tensor.wait_ge(p.sv.h, sv_o0 + ih - 1)
            ins = None
            for c in range(NCH):
                ins = nc.tensor.matmul(pb[:],
                                       og_fm[:, c, i * 128:(i + 1) * 128],
                                       g["wbuf"][0][:, c, half * 512:(half + 1) * 512],
                                       start=(c == 0), stop=(c == NCH - 1))
            p.st.inc(ins)

    @block.sync
    def _(sync):
        sync.wait_ge(p.ss.h, ss_entry)
        for i in range(NT):
            ld_v[i] = p.dx[i % 2].start(sync, g["xtile"][i % 2][:], xd[i])
            if i >= 1:
                sync.wait_ge(p.sv.h, sv_o0 + 2 * i)
                p.dx[(i - 1) % 2].start(sync, xd[i - 1], g["xtile"][(i - 1) % 2][:])
        sync.wait_ge(p.sv.h, sv_o0 + 2 * NT)
        p.dx[(NT - 1) % 2].start(sync, xd[NT - 1], g["xtile"][(NT - 1) % 2][:])

    @block.vector
    def _(vector):
        for ih in range(NT * 2):
            i, half = divmod(ih, 2)
            vector.wait_ge(p.st.h, st_o0 + ih + 1)
            vector.wait_ge(p.dx[i % 2].h, ld_v[i])
            sl = g["xtile"][i % 2][:, half * 512:(half + 1) * 512]
            p.sv.inc(vector.tensor_add(sl, sl, g["ps_s"][ih % 2][:]))


def lin_rev_h(hidx):
    return divmod(hidx, 2)



_PROG = None


def _constants():
    ident = np.eye(128).astype(ml_dtypes.bfloat16)
    triT = np.triu(np.ones((128, 128))).astype(ml_dtypes.bfloat16)
    a = np.arange(128)
    dmatTc = np.maximum(a[None, :] - a[:, None], 0).astype(np.float32)
    idxp1 = (a + 1.0).astype(np.float32).reshape(128, 1)
    idxrv = (127.0 - a).astype(np.float32).reshape(128, 1)
    import jax
    import jax.numpy as jnp
    with jax.default_device(jax.devices("cpu")[0]):
        inv_freq = 1.0 / 10000.0 ** (jnp.arange(0, 128, 2, dtype=jnp.float32) / 128)
        t = jnp.arange(S, dtype=jnp.float32)
        freqs = t[:, None] * inv_freq[None, :]
        emb = jnp.concatenate([freqs, freqs], axis=-1)
        rc = np.ascontiguousarray(np.asarray(jnp.cos(emb), np.float32))
        rs = np.ascontiguousarray(np.asarray(jnp.sin(emb), np.float32))
    return {"c_ident": ident, "c_triT": triT, "c_dmatTc": dmatTc,
            "c_idxp1": idxp1, "c_idxrv": idxrv,
            "c_ropecos": rc, "c_ropesin": rs}


def _base_map(inputs):
    consts = _constants()
    x_full = np.ascontiguousarray(inputs["L0_attn_a_x"], dtype=np.float32)
    ca = lambda k: np.ascontiguousarray(inputs[k], np.float32)
    base = {
        "L0A_Wq": ca("L0_attn_a_Wq"), "L0A_Wk": ca("L0_attn_a_Wk"),
        "L0A_Wv": ca("L0_attn_a_Wv"), "L0A_Wo": ca("L0_attn_a_Wo"),
        "L0A_bo": ca("L0_attn_a_bo"),
        "L0F_W": ca("L0_ffn_W"), "L0F_Wp": ca("L0_ffn_Wp"),
        "L0F_bp": ca("L0_ffn_bp"), "L0F_sim": ca("L0_ffn_sim"),
        "L0F_temp": ca("L0_ffn_temp"),
    }
    base["L0F_Wd_all"] = np.ascontiguousarray(np.concatenate(
        [inputs[f"L0_ffn_Wd{e}"] for e in range(4)], axis=0), np.float32)
    base["L0F_Wu_all"] = np.ascontiguousarray(np.concatenate(
        [inputs[f"L0_ffn_Wu{e}"] for e in range(4)], axis=1), np.float32)
    base["L0B_qkv_w"] = ca("L0_attn_b_qkv_w")
    base["L0B_qkv_b"] = ca("L0_attn_b_qkv_b")
    base["L0B_o_w"] = ca("L0_attn_b_o_w")
    base["L0B_o_b"] = ca("L0_attn_b_o_b")
    for nm in ["Wq", "Wk", "Wv", "Wo", "gamma", "beta", "Wg1", "Wg2"]:
        base[f"L1A_{nm}"] = ca(f"L1_attn_a_{nm}")
    base["L1F_Wg"] = ca("L1_ffn_w_gate")
    base["L1F_Wu"] = ca("L1_ffn_w_up")
    base["L1F_Wd"] = ca("L1_ffn_w_down")
    for nm in ["wq", "bq", "wqc", "bqc", "wqr", "bqr", "wkv", "bkv", "wkr",
               "bkr", "wkc", "bkc", "wv", "bv", "wo", "bo"]:
        base[f"L1B_{nm}"] = ca(f"L1_attn_b_{nm}")
    base.update(consts)
    return base, x_full


def kernel(**inputs):
    global _PROG
    if _PROG is None:
        _PROG = build_program()
    nc = _PROG

    base, x_full = _base_map(inputs)
    in_maps = []
    for c in range(8):
        m = dict(base)
        m["x"] = np.ascontiguousarray(x_full[c])
        in_maps.append(m)
    res = run_bass_kernel_spmd(nc, in_maps, core_ids=list(range(8)))
    out = np.stack([np.asarray(r["y"], np.float32) for r in res.results], axis=0)
    return out


# version 13 (final)
# speedup vs baseline: 1.3744x; 1.1217x over previous
from contextlib import ExitStack

import numpy as np
import ml_dtypes

import concourse.bass as bass
import concourse.mybir as mybir
from concourse.bass_utils import run_bass_kernel_spmd

F32 = mybir.dt.float32
BF16 = mybir.dt.bfloat16
AF = mybir.ActivationFunctionType
ALU = mybir.AluOpType
AX = mybir.AxisListType

S = 2048
D = 1024
NT = 16
NCH = 8
NH = 16
HD = 64

N_SUBLAYERS = 8


class Ctr:
    def __init__(self, stack, nc, name):
        self.h = stack.enter_context(nc.semaphore(name))
        self.n = 0

    def inc(self, instr, k=1):
        instr.then_inc(self.h, k)
        self.n += k
        return self.n


class DmaStream:

    def __init__(self, stack, nc, name, serialize=True):
        self.h = stack.enter_context(nc.semaphore(name))
        self.n = 0
        self.serialize = serialize

    def start(self, eng, out, in_, **kw):
        if self.serialize and self.n:
            eng.wait_ge(self.h, self.n)
        eng.dma_start(out=out, in_=in_, **kw).then_inc(self.h, 16)
        self.n += 16
        return self.n


class Arena:

    def __init__(self, p, kb):
        self.t = p.sb("arena", [128, kb * 512], BF16)
        self.off = 0

    def reset(self):
        self.off = 0

    def _take(self, n_bf16):
        pad = (n_bf16 + 15) // 16 * 16
        ap = self.t[:, self.off:self.off + n_bf16]
        self.off += pad
        assert self.off <= self.t.shape[1], f"arena overflow {self.off}"
        return ap

    def bf16(self, shape):
        assert shape[0] == 128
        n = int(np.prod(shape[1:]))
        ap = self._take(n)
        if len(shape) == 2:
            return ap
        return ap.rearrange("p (" + " ".join(f"d{i}" for i in range(1, len(shape))) + ") -> p " +
                            " ".join(f"d{i}" for i in range(1, len(shape))),
                            **{f"d{i}": shape[i] for i in range(1, len(shape))})

    def f32(self, shape):
        assert shape[0] == 128
        n = int(np.prod(shape[1:]))
        ap = self._take(2 * n).bitcast(F32)
        if len(shape) == 2:
            return ap
        return ap.rearrange("p (" + " ".join(f"d{i}" for i in range(1, len(shape))) + ") -> p " +
                            " ".join(f"d{i}" for i in range(1, len(shape))),
                            **{f"d{i}": shape[i] for i in range(1, len(shape))})

    def i8(self, shape):
        assert shape[0] == 128 and len(shape) == 2
        n = shape[1]
        return self._take((n + 1) // 2).bitcast(mybir.dt.int8)[:, 0:n]


class EngProxy:
    __slots__ = ("_eng", "_seen")

    def __init__(self, eng, seen):
        object.__setattr__(self, "_eng", eng)
        object.__setattr__(self, "_seen", seen)

    def wait_ge(self, sem, n):
        k = id(sem)
        if self._seen.get(k, -1) >= n:
            return
        self._seen[k] = n
        return self._eng.wait_ge(sem, n)

    def __getattr__(self, a):
        return getattr(self._eng, a)


class BlockProxy:
    def __init__(self, block, seen_all):
        self._block = block
        self._seen_all = seen_all

    def _wrap(self, name):
        return getattr(self._block, name)

    @property
    def vector(self):
        return self._wrap("vector")

    @property
    def scalar(self):
        return self._wrap("scalar")

    @property
    def tensor(self):
        return self._wrap("tensor")

    @property
    def sync(self):
        return self._wrap("sync")

    @property
    def gpsimd(self):
        return self._wrap("gpsimd")


class P:
    def __init__(self, nc, stack, block):
        self.nc = nc
        self.stack = stack
        self.seen = {k: {} for k in ("vector", "scalar", "tensor", "sync",
                                     "gpsimd")}
        self.block = BlockProxy(block, self.seen)
        self.sv = Ctr(stack, nc, "sv")
        self.ss = Ctr(stack, nc, "ss")
        self.st = Ctr(stack, nc, "st")
        self.sg = Ctr(stack, nc, "sg")
        self.dmisc = DmaStream(stack, nc, "dmisc", serialize=False)
        self.dx = [DmaStream(stack, nc, f"dx{i}") for i in range(2)]
        self.dw = [DmaStream(stack, nc, f"dw{i}") for i in range(2)]
        self.dlg = DmaStream(stack, nc, "dlg")
        self.dlr = [DmaStream(stack, nc, f"dlr{i}") for i in range(2)]
        self.dws = [DmaStream(stack, nc, f"dws{i}") for i in range(4)]
        self.dp = [DmaStream(stack, nc, f"dp{i}") for i in range(4)]
        self.dps = DmaStream(stack, nc, "dps")

    def sb(self, name, shape, dtype):
        return self.stack.enter_context(self.nc.sbuf_tensor(name, shape, dtype))

    def load_par(self, eng, dst, src, n=4):
        C = dst.shape[1]
        n = min(n, C)
        step = (C + n - 1) // n
        wr = []
        for i in range(n):
            c0, c1 = i * step, min(C, (i + 1) * step)
            if c0 >= c1:
                break
            s = self.dws[i % 4]
            wr.append((s.h, s.start(eng, dst[:, c0:c1], src[:, c0:c1])))
        return wr

    def ps(self, name, shape, dtype=F32):
        return self.stack.enter_context(self.nc.psum_tensor(name, shape, dtype))


def wwait(p, eng, w_ready, slot=None):
    if isinstance(w_ready, (list, tuple)):
        for s_, c_ in w_ready:
            eng.wait_ge(s_, c_)
    else:
        eng.wait_ge(p.dw[slot].h, w_ready)


def build_program():
    nc = bass.Bass("TRN2", target_bir_lowering=False)

    x_in = nc.dram_tensor("x", [S, D], F32, kind="ExternalInput").ap()
    y_out = nc.dram_tensor("y", [S, D], F32, kind="ExternalOutput").ap()
    xd = nc.dram_tensor("xd", [NT, 128, D], F32, kind="Internal").ap()

    w = {}
    shapes = [
        ("L0A_Wq", [D, D]), ("L0A_Wk", [D, D]), ("L0A_Wv", [D, D]),
        ("L0A_Wo", [D, D]), ("L0A_bo", [D]),
        ("L0F_W", [D, D]), ("L0F_Wp", [256, D]), ("L0F_bp", [256]),
        ("L0F_sim", [256, 4]), ("L0F_temp", [1]),
    ]
    shapes += [("L0F_Wd_all", [15, D]), ("L0F_Wu_all", [D, 15])]
    shapes += [("L0B_qkv_w", [D, 3 * D]), ("L0B_qkv_b", [3 * D]),
               ("L0B_o_w", [D, D]), ("L0B_o_b", [D])]
    shapes += [("L1A_Wq", [D, D]), ("L1A_Wk", [D, D]), ("L1A_Wv", [D, D]),
               ("L1A_Wo", [D, D]), ("L1A_gamma", [D]), ("L1A_beta", [D]),
               ("L1A_Wg1", [D, 64]), ("L1A_Wg2", [64, D])]
    shapes += [("L1F_Wg", [2816, D]), ("L1F_Wu", [2816, D]), ("L1F_Wd", [D, 2816])]
    shapes += [("L1B_wq", [D, 256]), ("L1B_bq", [256]),
               ("L1B_wqc", [256, 896]), ("L1B_bqc", [896]),
               ("L1B_wqr", [D, 128]), ("L1B_bqr", [128]),
               ("L1B_wkv", [D, 256]), ("L1B_bkv", [256]),
               ("L1B_wkr", [D, 128]), ("L1B_bkr", [128]),
               ("L1B_wkc", [256, 896]), ("L1B_bkc", [896]),
               ("L1B_wv", [256, D]), ("L1B_bv", [D]),
               ("L1B_wo", [D, D]), ("L1B_bo", [D]),
               ("c_ropecos", [S, 128]), ("c_ropesin", [S, 128])]
    for name, shape in shapes:
        w[name] = nc.dram_tensor(name, shape, F32, kind="ExternalInput").ap()
    w["wgT_c"] = nc.dram_tensor("wgT_c", [128, NCH, 2816], BF16, kind="Internal").ap()
    w["wuT_c"] = nc.dram_tensor("wuT_c", [128, NCH, 2816], BF16, kind="Internal").ap()
    w["wdT_c"] = nc.dram_tensor("wdT_c", [128, 22, 1024], BF16, kind="Internal").ap()

    c_ident = nc.dram_tensor("c_ident", [128, 128], BF16, kind="ExternalInput").ap()
    c_triT = nc.dram_tensor("c_triT", [128, 128], BF16, kind="ExternalInput").ap()
    w["c_dmatTc"] = nc.dram_tensor("c_dmatTc", [128, 128], F32, kind="ExternalInput").ap()
    w["c_idxp1"] = nc.dram_tensor("c_idxp1", [128, 1], F32, kind="ExternalInput").ap()
    w["c_idxrv"] = nc.dram_tensor("c_idxrv", [128, 1], F32, kind="ExternalInput").ap()

    with ExitStack() as stack:
        block = stack.enter_context(nc.Block())
        p = P(nc, stack, block)
        emit(p, x_in, y_out, xd, w, c_ident, c_triT)
    return nc


def emit(p, x_in, y_out, xd, w, c_ident, c_triT):
    nc, block = p.nc, p.block

    g = {}
    g["ident"] = p.sb("ident", [128, 128], BF16)
    g["triT"] = p.sb("triT", [128, 128], BF16)
    g["onescol"] = p.sb("onescol", [1, 128], BF16)
    g["sm"] = p.sb("sm", [128, NT], F32)
    g["sq"] = p.sb("sq", [128, NT], F32)
    g["mean"] = p.sb("mean", [128, NT], F32)
    g["var"] = p.sb("var", [128, NT], F32)
    g["rstd"] = p.sb("rstd", [128, NT], F32)
    g["nmrs"] = p.sb("nmrs", [128, NT], F32)
    g["xtile"] = [p.sb(f"xtile{i}", [128, 1024], F32) for i in range(2)]
    g["xn_tm"] = p.sb("xn_tm", [128, 2, 1024], BF16)
    g["xn_fm"] = p.sb("xn_fm", [128, NCH, S], BF16)
    g["ptb"] = [p.sb(f"ptb{i}", [128, 512], BF16) for i in range(3)]
    g["attn_fm"] = g["xn_fm"]
    _sq2 = p.sb("sq2_0", [128, 1024], BF16)
    g["sq2"] = [_sq2, _sq2]
    g["segr"] = p.sb("segr", [128, NH], F32)
    g["hmax"] = p.sb("hmax", [128, NH], F32)
    g["red1"] = p.sb("red1", [128, 2], F32)
    g["redbf"] = p.sb("redbf", [128, 2], BF16)
    g["redT"] = p.sb("redT", [1, 128], F32)
    g["csc"] = p.sb("csc", [1, 8], F32)
    g["negc8b"] = p.sb("negc8b", [1, 1], BF16)
    g["negc8"] = p.sb("negc8", [128, 1], F32)
    g["ceps"] = p.sb("ceps", [128, 1], F32)
    g["biasrow"] = p.sb("biasrow", [1, 4 * 1024], BF16)
    g["wbuf"] = [p.sb(f"wbuf{i}", [128, NCH, 1024], BF16) for i in range(2)]
    g["ps_e"] = p.ps("ps_e", [128, 512], F32)
    g["ps_s"] = [p.ps(f"ps_s{i}", [128, 512], F32) for i in range(2)]
    p.arena = Arena(p, 116)
    g["ps_pv"] = p.ps("ps_pv", [128, S], F32)
    g["ps_tp"] = p.ps("ps_tp", [128, 1024], BF16)
    g["dmatTc"] = p.sb("dmatTc", [128, 128], F32)
    g["idxp1"] = p.sb("idxp1", [128, 1], F32)
    g["idxrv"] = p.sb("idxrv", [128, 1], F32)

    @block.sync
    def _(sync):
        p.dmisc.start(sync, g["ident"][:], c_ident)
        p.dmisc.start(sync, g["triT"][:], c_triT)
        p.dmisc.start(sync, g["dmatTc"][:], w["c_dmatTc"])
        p.dmisc.start(sync, g["idxp1"][:], w["c_idxp1"])
        p.dmisc.start(sync, g["idxrv"][:], w["c_idxrv"])
        for i in range(NT):
            p.dmisc.start(sync, xd[i], x_in[i * 128:(i + 1) * 128, :])
        p.dq_consts = p.dmisc.n
    dq_x = p.dmisc.n

    @block.vector
    def _(vector):
        vector.memset(g["onescol"][:], 1.0)
        p.sv.inc(vector.memset(g["ceps"][:], 1e-5))

    rr = lambda t: (lambda: t.rearrange("(c p) o -> p c o", p=128))
    rowv = lambda t: (lambda: t.rearrange("(a d) -> a d", a=1))
    if N_SUBLAYERS >= 1:
        emit_ln(p, g, xd, dq_ready=dq_x)
        emit_mha(p, g, xd,
                 wq_src=rr(w["L0A_Wq"]), wk_src=rr(w["L0A_Wk"]),
                 wv_src=rr(w["L0A_Wv"]), wo_src=rr(w["L0A_Wo"]),
                 qkvb_src=None, ob_src=rowv(w["L0A_bo"]), scrambled=True)
    def do_ffn0():
        emit_barrier(p)
        emit_ln(p, g, xd)
        emit_ffn0(p, g, xd, W=w["L0F_W"], Wp=w["L0F_Wp"], bp=w["L0F_bp"],
                  sim=w["L0F_sim"], temp=w["L0F_temp"],
                  Wd=w["L0F_Wd_all"], Wu=w["L0F_Wu_all"])

    if N_SUBLAYERS >= 2:
        do_ffn0()
    if N_SUBLAYERS >= 3:
        emit_barrier(p)
        emit_ln(p, g, xd)
        qkv = w["L0B_qkv_w"].rearrange("(c p) o -> p c o", p=128)
        emit_mha(p, g, xd,
                 wq_src=lambda: qkv[:, :, 0:D], wk_src=lambda: qkv[:, :, D:2 * D],
                 wv_src=lambda: qkv[:, :, 2 * D:3 * D], wo_src=rr(w["L0B_o_w"]),
                 qkvb_src=rowv(w["L0B_qkv_b"]), ob_src=rowv(w["L0B_o_b"]),
                 scrambled=False)
    if N_SUBLAYERS >= 4:
        do_ffn0()
    if N_SUBLAYERS >= 5:
        emit_barrier(p)
        emit_ln(p, g, xd)
        emit_lightning(p, g, xd, Wq=w["L1A_Wq"], Wk=w["L1A_Wk"], Wv=w["L1A_Wv"],
                       Wo=w["L1A_Wo"], gamma=w["L1A_gamma"], beta=w["L1A_beta"],
                       Wg1=w["L1A_Wg1"], Wg2=w["L1A_Wg2"])

    def do_ffn1(first):
        emit_barrier(p)
        lnst = emit_ln(p, g, xd, stats_only=True)
        emit_ffn1(p, g, xd, Wg=w["L1F_Wg"], Wu=w["L1F_Wu"], Wd=w["L1F_Wd"],
                  cch=dict(wgT=w["wgT_c"], wuT=w["wuT_c"], wdT=w["wdT_c"]),
                  lnst=lnst, first=first)

    if N_SUBLAYERS >= 6:
        do_ffn1(first=True)
    if N_SUBLAYERS >= 7:
        emit_barrier(p)
        emit_ln(p, g, xd)
        emit_mla(p, g, xd,
                 W={k: w[f"L1B_{k}"] for k in
                    ["wq", "bq", "wqc", "bqc", "wqr", "bqr", "wkv", "bkv",
                     "wkr", "bkr", "wkc", "bkc", "wv", "bv", "wo", "bo"]},
                 cos_d=w["c_ropecos"], sin_d=w["c_ropesin"])
    if N_SUBLAYERS >= 8:
        do_ffn1(first=False)

    @block.sync
    def _(sync):
        sync.wait_ge(p.sv.h, p.sv.n)
        sync.wait_ge(p.ss.h, p.ss.n)
        for st_ in p.dx:
            sync.wait_ge(st_.h, st_.n)
        sync.wait_ge(p.dmisc.h, p.dmisc.n)
        for i in range(NT):
            p.dmisc.start(sync, y_out[i * 128:(i + 1) * 128, :], xd[i])
        sync.wait_ge(p.dmisc.h, p.dmisc.n)



def emit_ln(p, g, xd, dq_ready=None, stats_only=False):
    nc, block = p.nc, p.block
    xt = g["xtile"]
    nb = len(xt)
    sm, sq = g["sm"], g["sq"]

    ss0, sv0 = p.ss.n, p.sv.n
    dqv = {}

    @block.sync
    def _(sync):
        if dq_ready is not None:
            sync.wait_ge(p.dmisc.h, dq_ready)
        for i in range(NT):
            if i >= nb:
                sync.wait_ge(p.ss.h, ss0 + (i - nb + 1))
                sync.wait_ge(p.sv.h, sv0 + (i - nb + 1))
            dqv[("a", i)] = p.dx[i % nb].start(sync, xt[i % nb][:], xd[i])

    @block.scalar
    def _(scalar):
        for i in range(NT):
            scalar.wait_ge(p.dx[i % nb].h, dqv[("a", i)])
            if i >= 1:
                scalar.wait_ge(p.ss.h, ss0 + i)
            p.ss.inc(scalar.activation(g["sq2"][0][:], xt[i % nb][:], AF.Square,
                                       accum_out=sq[:, i:i + 1]))
    ss_sq = p.ss.n

    @block.vector
    def _(vector):
        for i in range(NT):
            vector.wait_ge(p.dx[i % nb].h, dqv[("a", i)])
            p.sv.inc(vector.reduce_sum(sm[:, i:i + 1].unsqueeze(2),
                                       xt[i % nb][:].rearrange("p (a b) -> p a b", a=1),
                                       AX.X))
        vector.wait_ge(p.ss.h, ss_sq)
        vector.wait_ge(p.sv.h, sv0 + NT)
        p.sv.inc(vector.tensor_scalar(g["mean"][:], sm[:], 1.0 / D, None, ALU.mult))
        p.sv.inc(vector.tensor_scalar(g["var"][:], sq[:], 1.0 / D, None, ALU.mult))
        vector.wait_ge(p.sv.h, p.sv.n)
        p.sv.inc(vector.tensor_mul(sm[:], g["mean"][:], g["mean"][:]))
        vector.wait_ge(p.sv.h, p.sv.n)
        p.sv.inc(vector.tensor_sub(g["var"][:], sm[:], g["var"][:]))
    sv_var = p.sv.n

    @block.scalar
    def _(scalar):
        scalar.wait_ge(p.sv.h, sv_var)
        p.ss.inc(scalar.activation(g["rstd"][:], g["var"][:], AF.Ln, scale=-1.0,
                                   bias=g["ceps"][:]))
        scalar.wait_ge(p.ss.h, p.ss.n)
        p.ss.inc(scalar.activation(g["rstd"][:], g["rstd"][:], AF.Exp, scale=-0.5))
    ss_rstd = p.ss.n

    @block.vector
    def _(vector):
        vector.wait_ge(p.ss.h, ss_rstd)
        p.sv.inc(vector.scalar_tensor_tensor(g["nmrs"][:], g["mean"][:], -1.0, g["rstd"][:],
                                             ALU.mult, ALU.mult))
    sv_stats = p.sv.n

    if stats_only:
        return dict(ss_sq=ss_sq, sv_red16=sv0 + NT, sv_stats=sv_stats)

    ss1 = p.ss.n
    sv_red16 = sv0 + NT

    @block.sync
    def _(sync):
        sync.wait_ge(p.ss.h, ss_sq)
        sync.wait_ge(p.sv.h, sv_red16)
        for i in range(NT):
            if i >= nb:
                sync.wait_ge(p.ss.h, ss1 + (i - nb + 1))
            dqv[("b", i)] = p.dx[i % nb].start(sync, xt[i % nb][:], xd[i])

    st0 = p.st.n

    @block.scalar
    def _(scalar):
        scalar.wait_ge(p.sv.h, sv_stats)
        for i in range(NT):
            scalar.wait_ge(p.dx[i % nb].h, dqv[("b", i)])
            if i >= 2:
                scalar.wait_ge(p.st.h, st0 + i - 1)
            p.ss.inc(scalar.activation(g["xn_tm"][:, i % 2, :], xt[i % nb][:], AF.Identity,
                                       bias=g["nmrs"][:, i:i + 1],
                                       scale=g["rstd"][:, i:i + 1]))

    sv1 = p.sv.n

    @block.tensor
    def _(tensor):
        for i in range(NT):
            tensor.wait_ge(p.ss.h, ss1 + i + 1)
            if i >= 1:
                tensor.wait_ge(p.sv.h, sv1 + i)
            ins = None
            for c in range(NCH):
                ins = nc.tensor.transpose(g["ps_tp"][:, c * 128:(c + 1) * 128],
                                          g["xn_tm"][:, i % 2, c * 128:(c + 1) * 128],
                                          g["ident"][:])
            p.st.inc(ins)

    @block.vector
    def _(vector):
        for i in range(NT):
            vector.wait_ge(p.st.h, st0 + i + 1)
            p.sv.inc(vector.tensor_copy(
                g["xn_fm"][:, :, i * 128:(i + 1) * 128],
                g["ps_tp"][:].rearrange("p (c t) -> p c t", c=NCH)))
    p.sv_xn = p.sv.n


def emit_proj_tm(p, g, wslot, dst_fn, src_fn, w_ready, add_bias=False,
                 bias_off=0, sv_pre=0, b_ready=0):
    nc, block = p.nc, p.block
    st0, sv0 = p.st.n, p.sv.n

    @block.tensor
    def _(tensor):
        wwait(p, tensor, w_ready, wslot)
        if add_bias and b_ready:
            tensor.wait_ge(p.dw[1].h, b_ready)
        tensor.wait_ge(p.sv.h, max(p.sv_xn, sv0, sv_pre))
        for ih in range(NT * 2):
            i, half = divmod(ih, 2)
            pb = g["ps_s"][ih % 2]
            if ih >= 2:
                tensor.wait_ge(p.sv.h, sv0 + ih - 1)
            ins = None
            for c in range(NCH):
                ins = nc.tensor.matmul(pb[:],
                                       g["xn_fm"][:, c, i * 128:(i + 1) * 128],
                                       g["wbuf"][wslot][:, c, half * 512:(half + 1) * 512],
                                       start=(c == 0),
                                       stop=(c == NCH - 1 and not add_bias))
            if add_bias:
                ins = nc.tensor.matmul(pb[:], g["onescol"][:],
                                       g["biasrow"][:, bias_off + half * 512:
                                                    bias_off + (half + 1) * 512],
                                       start=False, stop=True)
            p.st.inc(ins)

    @block.vector
    def _(vector):
        for ih in range(NT * 2):
            i, half = divmod(ih, 2)
            vector.wait_ge(p.st.h, st0 + ih + 1)
            p.sv.inc(vector.tensor_copy(dst_fn(i, half), src_fn(g["ps_s"][ih % 2], i, half)))
    return p.st.n, p.sv.n


def emit_cbound_pre(p, g, src_tm, col, sv_ready=0):
    nc, block = p.nc, p.block
    ss0, sv0 = p.ss.n, p.sv.n

    @block.scalar
    def _(scalar):
        scalar.wait_ge(p.sv.h, max(sv_ready, sv0))
        for i in range(NT):
            if i >= 1:
                scalar.wait_ge(p.sv.h, sv0 + 1 + 2 * i)
            p.ss.inc(scalar.activation(g["sq2"][i % 2][:], src_tm[:, i, :], AF.Square))

    @block.vector
    def _(vector):
        vector.wait_ge(p.sv.h, sv0)
        p.sv.inc(vector.memset(g["hmax"][:], 0.0))
        for i in range(NT):
            vector.wait_ge(p.ss.h, ss0 + i + 1)
            p.sv.inc(vector.tensor_reduce(
                g["segr"][:].unsqueeze(2),
                g["sq2"][i % 2][:].rearrange("p (h d) -> p h d", d=HD), AX.X, ALU.add))
            vector.wait_ge(p.sv.h, p.sv.n)
            p.sv.inc(vector.tensor_max(g["hmax"][:], g["hmax"][:], g["segr"][:]))
            vector.wait_ge(p.sv.h, p.sv.n)
        p.sv.inc(vector.reduce_max(g["red1"][:, col:col + 1].unsqueeze(2),
                                   g["hmax"][:].rearrange("p (a b) -> p a b", a=1), AX.X))
        vector.wait_ge(p.sv.h, p.sv.n)
        p.sv.inc(vector.tensor_copy(g["redbf"][:, col:col + 1],
                                    g["red1"][:, col:col + 1]))
    return p.sv.n


def emit_cbound_fin(p, g, col, sv_red):
    nc, block = p.nc, p.block
    st0 = p.st.n

    @block.tensor
    def _(tensor):
        tensor.wait_ge(p.sv.h, sv_red)
        p.st.inc(nc.tensor.matmul(g["ps_e"][0:1, col * 128:col * 128 + 128],
                                  g["redbf"][:, col:col + 1],
                                  g["ident"][:], start=True, stop=True))
    st_t = p.st.n

    @block.vector
    def _(vector):
        vector.wait_ge(p.st.h, st_t)
        vector.wait_ge(p.sv.h, p.sv.n)
        p.sv.inc(vector.tensor_copy(g["redT"][:],
                                    g["ps_e"][0:1, col * 128:col * 128 + 128]))
        vector.wait_ge(p.sv.h, p.sv.n)
        p.sv.inc(vector.reduce_max(
            g["csc"][:, col:col + 1].unsqueeze(2),
            g["redT"][:].rearrange("p (a b) -> p a b", a=1), AX.X))
    return p.sv.n


def emit_mha(p, g, xd, wq_src, wk_src, wv_src, wo_src, qkvb_src, ob_src,
             scrambled):
    nc, block = p.nc, p.block
    ar = p.arena
    ar.reset()
    g["lall"] = ar.f32([128, S])
    g["lrec"] = g["lall"]
    g["lrow_bf"] = ar.bf16([128, 1024])
    g["q_tm"] = ar.bf16([128, NT, D])
    g["k_tm"] = ar.bf16([128, NT, D])
    g["v_tm"] = ar.bf16([128, NT, NH, HD + 1])
    g["qk_fm"] = ar.bf16([128, 2, S])

    @block.vector
    def _(vector):
        p.sv.inc(vector.memset(g["v_tm"][:], 1.0))
    sv_vm = p.sv.n

    @block.gpsimd
    def _(gpsimd):
        wq_rdy = p.load_par(gpsimd, g["wbuf"][0][:], wq_src())
        g["_obr"] = p.dw[1].start(gpsimd,
                                  g["biasrow"][:, 3 * D:4 * D].unsqueeze(1),
                                  ob_src())
        qb_rdy = 0
        if qkvb_src is not None:
            qb_rdy = p.dw[1].start(gpsimd, g["biasrow"][:, 0:3 * D].unsqueeze(1),
                                   qkvb_src())
        wk_rdy = p.load_par(gpsimd, g["wbuf"][1][:], wk_src())
        g["_wrdy"] = (wq_rdy, wk_rdy, qb_rdy)
    wq_rdy, wk_rdy, qb_rdy = g.pop("_wrdy")
    has_qkvb = qkvb_src is not None

    st_q, sv_q = emit_proj_tm(p, g, 0,
                              lambda i, h: g["q_tm"][:, i, h * 512:(h + 1) * 512],
                              lambda pb, i, h: pb[:], wq_rdy,
                              add_bias=has_qkvb, bias_off=0, b_ready=qb_rdy)

    @block.gpsimd
    def _(gpsimd):
        gpsimd.wait_ge(p.st.h, st_q)
        g["_wv"] = p.load_par(gpsimd, g["wbuf"][0][:], wv_src())
    wv_rdy = g.pop("_wv")

    rq = emit_cbound_pre(p, g, g["q_tm"], 0, sv_ready=sv_q)

    st_k, sv_k = emit_proj_tm(p, g, 1,
                              lambda i, h: g["k_tm"][:, i, h * 512:(h + 1) * 512],
                              lambda pb, i, h: pb[:], wk_rdy,
                              add_bias=has_qkvb, bias_off=D)

    @block.gpsimd
    def _(gpsimd):
        gpsimd.wait_ge(p.st.h, st_k)
        g["_wo"] = p.load_par(gpsimd, g["wbuf"][1][:], wo_src())
    wo_rdy = g.pop("_wo") + [(p.dw[1].h, g.pop("_obr"))]

    st_v, sv_v = emit_proj_tm(
        p, g, 0,
        lambda i, h: g["v_tm"][:, i, h * 8:(h + 1) * 8, 0:HD],
        lambda pb, i, h: pb[:].rearrange("p (a b) -> p a b", a=8), wv_rdy,
        add_bias=has_qkvb, bias_off=2 * D, sv_pre=sv_vm, b_ready=qb_rdy)

    rk = emit_cbound_pre(p, g, g["k_tm"], 1, sv_ready=sv_k)

    wos_rdy = None
    if scrambled:
        @block.gpsimd
        def _(gpsimd):
            gpsimd.wait_ge(p.st.h, st_v)
            wov = wo_src().rearrange("p c o -> p c o")
            a_ = p.dws[0].start(gpsimd, g["wbuf"][0][0:64, :, :],
                                wov[64:128, :, :])
            b_ = p.dws[1].start(gpsimd, g["wbuf"][0][64:128, :, :],
                                wov[0:64, :, :])
            g["_wos"] = [(p.dws[0].h, a_), (p.dws[1].h, b_)]
        wos_rdy = g.pop("_wos")

    emit_attn_core(p, g, xd, wo_rdy, wos_rdy, scrambled, sv_qkv=sv_v,
                   cb=(rq, rk))


def emit_attn_core(p, g, xd, wo_rdy, wos_rdy, scrambled, sv_qkv, cb=None):
    nc, block = p.nc, p.block

    if cb is None:
        rq = emit_cbound_pre(p, g, g["q_tm"], 0, sv_ready=sv_qkv)
        rk = emit_cbound_pre(p, g, g["k_tm"], 1, sv_ready=sv_qkv)
    else:
        rq, rk = cb
    emit_cbound_fin(p, g, 0, rq)
    sv_ck = emit_cbound_fin(p, g, 1, rk)

    @block.vector
    def _(vector):
        vector.wait_ge(p.sv.h, sv_ck)
        p.sv.inc(vector.tensor_tensor(g["csc"][:, 2:3], g["csc"][:, 0:1],
                                      g["csc"][:, 1:2], ALU.mult))
    sv_cm = p.sv.n

    @block.scalar
    def _(scalar):
        scalar.wait_ge(p.sv.h, sv_cm)
        p.ss.inc(scalar.activation(g["csc"][:, 3:4], g["csc"][:, 2:3], AF.Ln))
        scalar.wait_ge(p.ss.h, p.ss.n)
        p.ss.inc(scalar.activation(g["csc"][:, 4:5], g["csc"][:, 3:4], AF.Exp, scale=0.5))
    ss_cs = p.ss.n

    @block.vector
    def _(vector):
        vector.wait_ge(p.ss.h, ss_cs)
        p.sv.inc(vector.tensor_scalar(g["csc"][:, 5:6], g["csc"][:, 4:5], -1.02 / 8.0,
                                      None, ALU.mult))
        vector.wait_ge(p.sv.h, p.sv.n)
        p.sv.inc(vector.tensor_copy(g["negc8b"][:], g["csc"][:, 5:6]))
    sv_cb = p.sv.n
    st_cb0 = p.st.n

    @block.tensor
    def _(tensor):
        tensor.wait_ge(p.sv.h, sv_cb)
        p.st.inc(nc.tensor.matmul(g["ps_s"][1][0:128, 0:1], g["onescol"][:],
                                  g["negc8b"][:], start=True, stop=True))
    st_cb = p.st.n

    @block.vector
    def _(vector):
        vector.wait_ge(p.st.h, st_cb)
        p.sv.inc(vector.tensor_copy(g["negc8"][:], g["ps_s"][1][0:128, 0:1]))
    sv_c = p.sv.n

    def head_items(qblk_range=4):
        for qblk in range(qblk_range):
            jmax = 4 * qblk + 3
            for j in range(jmax + 1):
                yield qblk, j, max(qblk * 512, j * 128), j == 0, j == jmax, j >= 4 * qblk

    st_c = p.st.n
    ss_c = p.ss.n
    sv_cn = p.sv.n
    dq_c = p.dlg.n
    st_tp = {}; st_sc = {}; st_pv = {}
    ss_exp = {}
    sv_tpe = {}; sv_tri = {}; sv_oev = {}; sv_lev = {}
    dq_lall = {}
    for pr in range(8):
        for gi in range(4):
            st_c += 1; st_tp[(pr, gi)] = st_c
        for hh in range(2):
            items_ = list(head_items())
            for bs_ in range(0, len(items_), 3):
                batch_ = items_[bs_:bs_ + 3]
                for qblk, j, qlo, first, last, diag in batch_:
                    st_c += 1; st_sc[(pr, hh, qblk, j)] = st_c
                for qblk, j, qlo, first, last, diag in batch_:
                    st_c += 1; st_pv[(pr, hh, qblk, j)] = st_c
    for pr in range(8):
        for hh in range(2):
            for qblk, j, qlo, first, last, diag in head_items():
                ss_c += 1; ss_exp[(pr, hh, qblk, j)] = ss_c
    for pr in range(8):
        for gi in range(4):
            sv_cn += 1; sv_tpe[(pr, gi)] = sv_cn
        for hh in range(2):
            for qblk, j, qlo, first, last, diag in head_items():
                if diag:
                    sv_cn += 1; sv_tri[(pr, hh, qblk, j)] = sv_cn
            sv_cn += 1; sv_oev[(pr, hh)] = sv_cn
            sv_cn += 1; sv_lev[(pr, hh)] = sv_cn
    for pr in range(8):
        for hh in range(2):
            dq_c += 16; dq_lall[(pr, hh)] = dq_c

    lin = {}
    cnt = 0
    for pr in range(8):
        for hh in range(2):
            for qblk, j, qlo, first, last, diag in head_items():
                lin[(pr, hh, qblk, j)] = cnt
                cnt += 1
    lin_rev = {v: k for k, v in lin.items()}

    pslots = [g["ps_s"][0], g["ps_s"][1], g["ps_e"]]

    @block.tensor
    def _(tensor):
        tensor.wait_ge(p.sv.h, sv_c)
        for pr in range(8):
            for gi in range(4):
                half, b = divmod(gi, 2)
                src = g["q_tm"] if half == 0 else g["k_tm"]
                if (pr, gi) != (0, 0):
                    prevk = (pr, gi - 1) if gi else (pr - 1, 3)
                    tensor.wait_ge(p.sv.h, sv_tpe[prevk])
                ins = None
                for t in range(8):
                    ins = nc.tensor.transpose(
                        g["ps_tp"][:, t * 128:(t + 1) * 128],
                        src[:, b * 8 + t, pr * 128:(pr + 1) * 128],
                        g["ident"][:])
                p.st.inc(ins)
            for hh in range(2):
                h = 2 * pr + hh
                base = hh * 64
                items = list(head_items())
                tensor.wait_ge(p.sv.h, sv_tpe[(pr, 3)])
                for bs in range(0, len(items), 3):
                    batch = items[bs:bs + 3]
                    nf = lin[(pr, hh, batch[0][0], batch[0][1])]
                    if nf >= 3:
                        tensor.wait_ge(p.ss.h, ss_exp[lin_rev[nf - 1]])
                    for qblk, j, qlo, first, last, diag in batch:
                        n = lin[(pr, hh, qblk, j)]
                        p.st.inc(nc.tensor.matmul(
                            pslots[n % 3][:, 0:qblk * 512 + 512 - qlo],
                            g["qk_fm"][base:base + 64, 1, j * 128:(j + 1) * 128],
                            g["qk_fm"][base:base + 64, 0, qlo:qblk * 512 + 512],
                            start=True, stop=True))
                    qb_, j_ = batch[-1][0], batch[-1][1]
                    tensor.wait_ge(p.ss.h, ss_exp[(pr, hh, qb_, j_)])
                    diags = [it for it in batch if it[5]]
                    if diags:
                        tensor.wait_ge(p.sv.h,
                                       sv_tri[(pr, hh, diags[-1][0],
                                               diags[-1][1])])
                    for qblk, j, qlo, first, last, diag in batch:
                        n = lin[(pr, hh, qblk, j)]
                        if first and (pr, hh) != (0, 0):
                            prevh = (pr, hh - 1) if hh else (pr - 1, 1)
                            tensor.wait_ge(p.sv.h, sv_lev[prevh])
                        p.st.inc(nc.tensor.matmul(
                            g["ps_pv"][0:65, qlo:qblk * 512 + 512],
                            g["v_tm"][:, j, h, :],
                            g["ptb"][n % 3][:, 0:qblk * 512 + 512 - qlo],
                            start=first, stop=last))

    @block.scalar
    def _(scalar):
        scalar.wait_ge(p.sv.h, sv_c)
        for pr in range(8):
            for hh in range(2):
                for qblk, j, qlo, first, last, diag in head_items():
                    n = lin[(pr, hh, qblk, j)]
                    jp = n % 3
                    wdt = qblk * 512 + 512 - qlo
                    scalar.wait_ge(p.st.h, st_sc[(pr, hh, qblk, j)])
                    p.ss.inc(scalar.activation(g["ptb"][jp][:, 0:wdt],
                                               pslots[jp][:, 0:wdt], AF.Exp,
                                               scale=0.125, bias=g["negc8"][:]))
    assert p.ss.n == ss_c

    @block.vector
    def _(vector):
        for pr in range(8):
            for gi in range(4):
                half, b = divmod(gi, 2)
                vector.wait_ge(p.st.h, st_tp[(pr, gi)])
                p.sv.inc(vector.tensor_copy(
                    g["qk_fm"][:, half, b * 1024:(b + 1) * 1024], g["ps_tp"][:]))
            for hh in range(2):
                h = 2 * pr + hh
                base = hh * 64
                for qblk, j, qlo, first, last, diag in head_items():
                    if not diag:
                        continue
                    n = lin[(pr, hh, qblk, j)]
                    jp = n % 3
                    vector.wait_ge(p.ss.h, ss_exp[(pr, hh, qblk, j)])
                    p.sv.inc(vector.tensor_tensor(g["ptb"][jp][:, 0:128],
                                                  g["ptb"][jp][:, 0:128],
                                                  g["triT"][:], ALU.mult))
                vector.wait_ge(p.st.h, st_pv[(pr, hh, 3, 15)])
                p.sv.inc(vector.tensor_copy(g["attn_fm"][base:base + 64, pr, :],
                                            g["ps_pv"][0:64, :]))
                hidx = 2 * pr + hh
                if hidx >= 1:
                    vector.wait_ge(p.dlg.h, dq_lall[lin_rev_h(hidx - 1)])
                p.sv.inc(vector.tensor_copy(g["lall"][32:33, :],
                                            g["ps_pv"][64:65, :]))
    assert p.sv.n == sv_cn

    @block.sync
    def _(sync):
        for pr in range(8):
            for hh in range(2):
                h = 2 * pr + hh
                sync.wait_ge(p.sv.h, sv_lev[(pr, hh)])
                p.dlg.start(sync, g["lall"][h:h + 1, :], g["lall"][32:33, :])
    assert p.dlg.n == dq_c

    ss_lr0 = p.ss.n

    @block.scalar
    def _(scalar):
        scalar.wait_ge(p.dlg.h, dq_c)
        p.ss.inc(scalar.activation(g["lrec"][0:NH, :], g["lall"][0:NH, :], AF.Ln))
        scalar.wait_ge(p.ss.h, p.ss.n)
        p.ss.inc(scalar.activation(g["lrec"][0:NH, :], g["lrec"][0:NH, :], AF.Exp,
                                   scale=-1.0))
    ss_lr = p.ss.n

    st_n0 = p.st.n
    sv_n0 = p.sv.n
    dlr_v = {}

    @block.gpsimd
    def _(gpsimd):
        gpsimd.wait_ge(p.ss.h, ss_lr)
        for i in range(NH * 4):
            h, qblk = divmod(i, 4)
            if i >= 2:
                gpsimd.wait_ge(p.st.h, st_n0 + i - 1)
            dlr_v[i] = p.dlr[i % 2].start(
                gpsimd, g["lrow_bf"][0:1, (i % 2) * 512:(i % 2) * 512 + 512],
                g["lrec"][h:h + 1, qblk * 512:(qblk + 1) * 512])

    @block.tensor
    def _(tensor):
        for i in range(NH * 4):
            h, qblk = divmod(i, 4)
            tensor.wait_ge(p.dlr[i % 2].h, dlr_v[i])
            if i >= 2:
                tensor.wait_ge(p.sv.h, sv_n0 + i - 1)
            p.st.inc(nc.tensor.matmul(g["ps_s"][i % 2][0:64, :],
                                      g["onescol"][:, 0:64],
                                      g["lrow_bf"][0:1, (i % 2) * 512:(i % 2) * 512 + 512],
                                      start=True, stop=True))

    @block.vector
    def _(vector):
        for i in range(NH * 4):
            h, qblk = divmod(i, 4)
            pr, hh = divmod(h, 2)
            base = hh * 64
            vector.wait_ge(p.st.h, st_n0 + i + 1)
            sl = g["attn_fm"][base:base + 64, pr, qblk * 512:(qblk + 1) * 512]
            p.sv.inc(vector.tensor_tensor(sl, sl, g["ps_s"][i % 2][0:64, :], ALU.mult))
    sv_norm = p.sv.n

    st_o0 = p.st.n
    sv_o0 = p.sv.n

    @block.tensor
    def _(tensor):
        wwait(p, tensor, wo_rdy, 1)
        if scrambled:
            wwait(p, tensor, wos_rdy, 0)
        tensor.wait_ge(p.sv.h, sv_norm)
        for ih in range(NT * 2):
            i, half = divmod(ih, 2)
            pb = g["ps_s"][ih % 2]
            if ih >= 2:
                tensor.wait_ge(p.sv.h, sv_o0 + ih - 1)
            if scrambled:
                hh, pr = i % 2, i // 2
                for c in range(NCH):
                    for bpar in range(2):
                        b0 = 2 * c + bpar
                        wsrc = g["wbuf"][1] if bpar == hh else g["wbuf"][0]
                        nc.tensor.matmul(
                            pb[:],
                            g["attn_fm"][hh * 64:hh * 64 + 64, pr, b0::16],
                            wsrc[hh * 64:(hh + 1) * 64, c,
                                 half * 512:(half + 1) * 512],
                            start=(c == 0 and bpar == 0), stop=False)
            else:
                for c in range(NCH):
                    nc.tensor.matmul(pb[:],
                                     g["attn_fm"][:, c, i * 128:(i + 1) * 128],
                                     g["wbuf"][1][:, c, half * 512:(half + 1) * 512],
                                     start=(c == 0), stop=False)
            p.st.inc(nc.tensor.matmul(pb[:], g["onescol"][:],
                                      g["biasrow"][:, 3 * D + half * 512:
                                                   3 * D + (half + 1) * 512],
                                      start=False, stop=True))

    ld_v = {}
    ss_res0 = p.ss.n

    @block.sync
    def _(sync):
        sync.wait_ge(p.ss.h, ss_res0)
        for i in range(NT):
            ld_v[i] = p.dx[i % 2].start(sync, g["xtile"][i % 2][:], xd[i])
            if i >= 1:
                sync.wait_ge(p.sv.h, sv_o0 + 2 * i)
                p.dx[(i - 1) % 2].start(sync, xd[i - 1], g["xtile"][(i - 1) % 2][:])
        sync.wait_ge(p.sv.h, sv_o0 + 2 * NT)
        p.dx[(NT - 1) % 2].start(sync, xd[NT - 1], g["xtile"][(NT - 1) % 2][:])

    @block.vector
    def _(vector):
        for ih in range(NT * 2):
            i, half = divmod(ih, 2)
            vector.wait_ge(p.st.h, st_o0 + ih + 1)
            vector.wait_ge(p.dx[i % 2].h, ld_v[i])
            sl = g["xtile"][i % 2][:, half * 512:(half + 1) * 512]
            p.sv.inc(vector.tensor_add(sl, sl, g["ps_s"][ih % 2][:]))


def emit_mla(p, g, xd, W, cos_d, sin_d):
    nc, block = p.nc, p.block
    ar = p.arena
    ar.reset()

    g["q_tm"] = ar.bf16([128, NT, D])
    g["k_tm"] = ar.bf16([128, NT, D])
    g["v_tm"] = ar.bf16([128, NT, NH, HD + 1])
    U = ar.bf16([128, 9744])
    g["qk_fm"] = U[:, 0:4096].rearrange("p (c t) -> p c t", c=2)
    g["lall"] = U[:, 4096:8192].bitcast(F32)
    g["lrec"] = g["lall"]
    g["lrow_bf"] = U[:, 8192:9216]
    c_fm = g["qk_fm"]
    wsl = [U[:, 4096:6144], U[:, 6144:8192]]
    ropebuf = U[:, 8192:9216].bitcast(F32).rearrange(
        "p (s c t) -> p s c t", s=2, c=2)
    rp = U[:, 9216:9472].bitcast(F32)
    rtmp = U[:, 9472:9728].bitcast(F32)
    bcol = U[:, 9728:9744].bitcast(F32)

    wv8 = lambda s: wsl[s].rearrange("p (c o) -> p c o", c=8)
    wv2 = lambda s: wsl[s].rearrange("p (c o) -> p c o", c=2)
    rr = lambda t: t.rearrange("(c p) o -> p c o", p=128)
    rowv = lambda t: t.rearrange("(a d) -> a d", a=1)

    stg = {}

    @block.gpsimd
    def _(gpsimd):
        stg["wqr"] = p.dw[0].start(gpsimd, wv8(0)[:, :, 0:128], rr(W["wqr"]))
        stg["wkr"] = p.dw[0].start(gpsimd, wv8(0)[:, :, 128:256], rr(W["wkr"]))
        stg["wqc"] = p.dw[0].start(gpsimd, wv2(1)[:, :, 0:896], rr(W["wqc"]))
        for oc in range(2):
            p.dw[1].start(gpsimd, bcol[:, oc:oc + 1],
                          W["bq"][oc * 128:(oc + 1) * 128].rearrange(
                              "(p a) -> p a", a=1))
            p.dw[1].start(gpsimd, bcol[:, 2 + oc:3 + oc],
                          W["bkv"][oc * 128:(oc + 1) * 128].rearrange(
                              "(p a) -> p a", a=1))
        p.dw[1].start(gpsimd, g["biasrow"][:, 0:896].unsqueeze(1), rowv(W["bqc"]))
        p.dw[1].start(gpsimd, g["biasrow"][:, 896:1792].unsqueeze(1), rowv(W["bkc"]))
        p.dw[1].start(gpsimd, g["biasrow"][:, 1792:2816].unsqueeze(1), rowv(W["bv"]))
        p.dw[1].start(gpsimd, g["biasrow"][:, 2816:2944].unsqueeze(1), rowv(W["bqr"]))
        p.dw[1].start(gpsimd, g["biasrow"][:, 2944:3072].unsqueeze(1), rowv(W["bkr"]))
        p.dw[1].start(gpsimd, g["biasrow"][:, 3 * D:4 * D].unsqueeze(1), rowv(W["bo"]))
        stg["b"] = p.dw[1].n
        stg["wo"] = p.load_par(gpsimd, g["wbuf"][1][:], rr(W["wo"]))

    def stage_load(name, dst, src, st_free):
        @block.gpsimd
        def _(gpsimd):
            gpsimd.wait_ge(p.st.h, st_free)
            stg[name] = p.dw[0].start(gpsimd, dst, src)

    def rope_proj(dst_tm, wview, w_rdy, br_off):
        st0, sv0, ss0 = p.st.n, p.sv.n, p.ss.n
        dlrv = {}

        @block.gpsimd
        def _(gpsimd):
            for i in range(NT):
                if i >= 2:
                    gpsimd.wait_ge(p.sv.h, sv0 + 6 * (i - 1))
                else:
                    gpsimd.wait_ge(p.sv.h, sv0)
                p.dlr[i % 2].start(gpsimd, ropebuf[:, i % 2, 0, :],
                                   cos_d[i * 128:(i + 1) * 128, :])
                dlrv[i] = p.dlr[i % 2].start(gpsimd, ropebuf[:, i % 2, 1, :],
                                             sin_d[i * 128:(i + 1) * 128, :])

        @block.tensor
        def _(tensor):
            tensor.wait_ge(p.dw[0].h, w_rdy)
            tensor.wait_ge(p.dw[1].h, stg["b"])
            tensor.wait_ge(p.sv.h, p.sv_xn)
            for i in range(NT):
                if i >= 1:
                    tensor.wait_ge(p.sv.h, sv0 + 6 * i - 1)
                else:
                    tensor.wait_ge(p.sv.h, sv0)
                for c in range(NCH):
                    nc.tensor.matmul(g["ps_e"][:, 0:128],
                                     g["xn_fm"][:, c, i * 128:(i + 1) * 128],
                                     wview[:, c, :],
                                     start=(c == 0), stop=False)
                p.st.inc(nc.tensor.matmul(g["ps_e"][:, 0:128], g["onescol"][:],
                                          g["biasrow"][:, br_off:br_off + 128],
                                          start=False, stop=True))

        @block.vector
        def _(vector):
            for i in range(NT):
                s = i % 2
                vector.wait_ge(p.st.h, st0 + i + 1)
                vector.wait_ge(p.dlr[s].h, dlrv[i])
                vector.wait_ge(p.ss.h, ss0 + i)
                vector.wait_ge(p.sv.h, p.sv.n)
                p.sv.inc(vector.tensor_mul(rp[:, 0:64], g["ps_e"][:, 0:64],
                                           ropebuf[:, s, 0, 0:64]))
                p.sv.inc(vector.tensor_mul(rtmp[:, 0:64], g["ps_e"][:, 64:128],
                                           ropebuf[:, s, 1, 0:64]))
                vector.wait_ge(p.sv.h, p.sv.n)
                p.sv.inc(vector.tensor_sub(rp[:, 0:64], rp[:, 0:64],
                                           rtmp[:, 0:64]))
                p.sv.inc(vector.tensor_mul(rp[:, 64:128], g["ps_e"][:, 64:128],
                                           ropebuf[:, s, 0, 64:128]))
                p.sv.inc(vector.tensor_mul(rtmp[:, 64:128], g["ps_e"][:, 0:64],
                                           ropebuf[:, s, 1, 64:128]))
                vector.wait_ge(p.sv.h, p.sv.n)
                p.sv.inc(vector.tensor_add(rp[:, 64:128], rp[:, 64:128],
                                           rtmp[:, 64:128]))

        @block.scalar
        def _(scalar):
            for i in range(NT):
                scalar.wait_ge(p.sv.h, sv0 + 6 * (i + 1))
                p.ss.inc(scalar.activation(dst_tm[:, i, 896:1024], rp[:],
                                           AF.Gelu))
        return p.ss.n

    def fm_proj_gelu(wview, w_rdy, bc_idx, st_war):
        st0, ss0 = p.st.n, p.ss.n

        @block.tensor
        def _(tensor):
            tensor.wait_ge(p.dw[0].h, w_rdy)
            tensor.wait_ge(p.sv.h, p.sv_xn)
            tensor.wait_ge(p.ss.h, ss0)
            for k in range(8):
                oc, sp = divmod(k, 4)
                pb = g["ps_s"][k % 2]
                if k >= 2:
                    tensor.wait_ge(p.ss.h, ss0 + k - 1)
                ins = None
                for c in range(NCH):
                    ins = nc.tensor.matmul(pb[:],
                                           wview[:, c, oc * 128:(oc + 1) * 128],
                                           g["xn_fm"][:, c, sp * 512:(sp + 1) * 512],
                                           start=(c == 0), stop=(c == NCH - 1))
                p.st.inc(ins)
        st_mm = p.st.n

        @block.scalar
        def _(scalar):
            scalar.wait_ge(p.dw[1].h, stg["b"])
            scalar.wait_ge(p.st.h, st_war)
            for k in range(8):
                oc, sp = divmod(k, 4)
                scalar.wait_ge(p.st.h, st0 + k + 1)
                p.ss.inc(scalar.activation(
                    c_fm[:, oc, sp * 512:(sp + 1) * 512], g["ps_s"][k % 2][:],
                    AF.Gelu, bias=bcol[:, bc_idx + oc:bc_idx + oc + 1]))
        return st_mm, p.ss.n

    def tm_proj_gelu(dst_tm, w_rdy, ss_cready, br_off):
        st0, ss0 = p.st.n, p.ss.n
        spans = [(0, 512), (512, 384)]

        @block.tensor
        def _(tensor):
            tensor.wait_ge(p.dw[0].h, w_rdy)
            tensor.wait_ge(p.ss.h, ss_cready)
            for n in range(NT * 2):
                i, hf = divmod(n, 2)
                so, sw = spans[hf]
                pb = g["ps_s"][n % 2]
                if n >= 2:
                    tensor.wait_ge(p.ss.h, ss0 + n - 1)
                for c in range(2):
                    nc.tensor.matmul(pb[:, 0:sw],
                                     c_fm[:, c, i * 128:(i + 1) * 128],
                                     wv2(1)[:, c, so:so + sw],
                                     start=(c == 0), stop=False)
                p.st.inc(nc.tensor.matmul(
                    pb[:, 0:sw], g["onescol"][:],
                    g["biasrow"][:, br_off + so:br_off + so + sw],
                    start=False, stop=True))
        st_mm = p.st.n

        @block.scalar
        def _(scalar):
            for n in range(NT * 2):
                i, hf = divmod(n, 2)
                so, sw = spans[hf]
                scalar.wait_ge(p.st.h, st0 + n + 1)
                p.ss.inc(scalar.activation(dst_tm[:, i, so:so + sw],
                                           g["ps_s"][n % 2][:, 0:sw], AF.Gelu))
        return st_mm, p.ss.n

    ss_qrope = rope_proj(g["q_tm"], wv8(0)[:, :, 0:128], stg["wqr"], 2816)
    ss_krope = rope_proj(g["k_tm"], wv8(0)[:, :, 128:256], stg["wkr"], 2944)
    stage_load("wq", wv8(0)[:, :, 0:256], rr(W["wq"]), p.st.n)
    st_cq, ss_cq = fm_proj_gelu(wv8(0)[:, :, 0:256], stg["wq"], 0, 0)
    st_qc, ss_qc = tm_proj_gelu(g["q_tm"], stg["wqc"], ss_cq, 0)
    stage_load("wkv", wv8(0)[:, :, 0:256], rr(W["wkv"]), st_cq)
    st_ckv, ss_ckv = fm_proj_gelu(wv8(0)[:, :, 0:256], stg["wkv"], 2, st_qc)
    stage_load("wkc", wv2(1)[:, :, 0:896], rr(W["wkc"]), st_qc)
    st_kc, ss_kc = tm_proj_gelu(g["k_tm"], stg["wkc"], ss_ckv, 896)
    stage_load("wv", wv2(1)[:, :, 0:1024], rr(W["wv"]), st_kc)

    @block.vector
    def _(vector):
        p.sv.inc(vector.memset(g["v_tm"][:], 1.0))
    sv_vm = p.sv.n

    st0v, sv0v, ss0v = p.st.n, p.sv.n, p.ss.n

    @block.tensor
    def _(tensor):
        tensor.wait_ge(p.dw[0].h, stg["wv"])
        tensor.wait_ge(p.ss.h, ss0v)
        for n in range(NT * 2):
            i, hf = divmod(n, 2)
            pb = g["ps_s"][n % 2]
            if n >= 2:
                tensor.wait_ge(p.sv.h, sv0v + n - 1)
            for c in range(2):
                nc.tensor.matmul(pb[:],
                                 c_fm[:, c, i * 128:(i + 1) * 128],
                                 wv2(1)[:, c, hf * 512:(hf + 1) * 512],
                                 start=(c == 0), stop=False)
            p.st.inc(nc.tensor.matmul(
                pb[:], g["onescol"][:],
                g["biasrow"][:, 1792 + hf * 512:1792 + (hf + 1) * 512],
                start=False, stop=True))

    @block.vector
    def _(vector):
        vector.wait_ge(p.sv.h, sv_vm)
        for n in range(NT * 2):
            i, hf = divmod(n, 2)
            vector.wait_ge(p.st.h, st0v + n + 1)
            p.sv.inc(vector.tensor_copy(
                g["v_tm"][:, i, hf * 8:(hf + 1) * 8, 0:HD],
                g["ps_s"][n % 2][:].rearrange("p (a b) -> p a b", a=8)))
    sv_v = p.sv.n

    emit_attn_core(p, g, xd, stg["wo"] + [(p.dw[1].h, stg["b"])], None,
                   scrambled=False, sv_qkv=sv_v)


def emit_barrier(p):
    block = p.block
    sv, ss, st, sg = p.sv.n, p.ss.n, p.st.n, p.sg.n

    @block.tensor
    def _(tensor):
        tensor.wait_ge(p.sv.h, sv); tensor.wait_ge(p.ss.h, ss); tensor.wait_ge(p.sg.h, sg)

    @block.vector
    def _(vector):
        vector.wait_ge(p.st.h, st); vector.wait_ge(p.ss.h, ss); vector.wait_ge(p.sg.h, sg)

    @block.scalar
    def _(scalar):
        scalar.wait_ge(p.st.h, st); scalar.wait_ge(p.sv.h, sv); scalar.wait_ge(p.sg.h, sg)

    @block.gpsimd
    def _(gpsimd):
        gpsimd.wait_ge(p.st.h, st); gpsimd.wait_ge(p.sv.h, sv); gpsimd.wait_ge(p.ss.h, ss)

    @block.sync
    def _(sync):
        sync.wait_ge(p.st.h, st); sync.wait_ge(p.sv.h, sv); sync.wait_ge(p.ss.h, ss)


def emit_ffn0(p, g, xd, W, Wp, bp, sim, temp, Wd, Wu):
    nc, block = p.nc, p.block
    ar = p.arena
    ar.reset()
    CLAMP = float(np.log(1.0 / 0.01))

    wp_nat = ar.bf16([128, 2, 1024])
    wpT = ar.bf16([128, 8, 256])
    wd_nat = ar.bf16([128, 1024])
    wdT = ar.bf16([128, 8, 16])
    wu_nat = ar.bf16([128, 8, 16])
    wuT = ar.bf16([128, 1024])
    simb = ar.bf16([128, 2, 4])
    simTf = ar.f32([128, 256])
    simsq = ar.bf16([128, 256])
    simnT = ar.bf16([128, 256])
    simn = ar.bf16([128, 2, 4])
    tmpv = ar.f32([128, 8])
    s_bc = ar.f32([128, 1])
    s_bf = ar.bf16([128, 16])
    c38 = ar.f32([128, 1])
    ssp = ar.f32([128, 1])
    rn = ar.f32([128, 1])
    sqd = ar.bf16([128, 256])
    p_fm = [ar.bf16([128, 2, 128]) for _ in range(2)]
    dall = [ar.f32([128, 16]) for _ in range(2)]
    lgc = ar.f32([128, 4])
    esum = ar.f32([128, 1])
    er = ar.f32([128, 1])
    lw = ar.f32([128, 4])
    rm = ar.f32([128, 1])
    msk = ar.f32([128, 4])
    ret = ar.f32([128, 4])
    t_scl = [ar.bf16([128, 16]) for _ in range(2)]
    t_fm = [ar.bf16([128, 128]) for _ in range(2)]
    pn_t = [ar.bf16([128, 256]) for _ in range(2)]

    @block.gpsimd
    def _(gpsimd):
        g["_w0"] = p.load_par(gpsimd, g["wbuf"][0][:],
                              W.rearrange("(c p) o -> p c o", p=128))
        p.dw[1].start(gpsimd, g["biasrow"][:, 0:256].unsqueeze(1),
                      bp.rearrange("(a d) -> a d", a=1))
        p.dw[1].start(gpsimd, wp_nat[:], Wp.rearrange("(c p) o -> p c o", p=128))
        p.dw[1].start(gpsimd, wd_nat[0:15, :], Wd)
        p.dw[1].start(gpsimd, wu_nat[:, :, 0:15],
                      Wu.rearrange("(c p) r -> p c r", p=128))
        p.dw[1].start(gpsimd, simb[:], sim.rearrange("(c p) e -> p c e", p=128))
        g["_w1"] = p.dw[1].start(gpsimd, tmpv[0:1, 0:1],
                                 temp.rearrange("(a d) -> a d", a=1))
    w0_rdy = g.pop("_w0")
    w1_rdy = g.pop("_w1")

    st0 = p.st.n
    sv0 = p.sv.n

    @block.tensor
    def _(tensor):
        wwait(p, tensor, w0_rdy, 0)
        tensor.wait_ge(p.dw[1].h, w1_rdy)
        tensor.wait_ge(p.sv.h, sv0)
        gidx = 0
        for dc in range(8):
            if gidx >= 1:
                tensor.wait_ge(p.sv.h, sv0 + 3 + gidx)
            for oc in range(8):
                ins = nc.tensor.transpose(g["ps_tp"][:, oc * 128:(oc + 1) * 128],
                                          g["wbuf"][0][:, oc, dc * 128:(dc + 1) * 128],
                                          g["ident"][:])
            p.st.inc(ins); gidx += 1
        for dc in range(8):
            tensor.wait_ge(p.sv.h, sv0 + 3 + gidx)
            for oc in range(2):
                ins = nc.tensor.transpose(g["ps_tp"][:, oc * 128:(oc + 1) * 128],
                                          wp_nat[:, oc, dc * 128:(dc + 1) * 128],
                                          g["ident"][:])
            p.st.inc(ins); gidx += 1
        for dc in range(8):
            tensor.wait_ge(p.sv.h, sv0 + 3 + gidx)
            ins = nc.tensor.transpose(g["ps_tp"][:, 0:15],
                                      wd_nat[0:15, dc * 128:(dc + 1) * 128],
                                      g["ident"][0:15, 0:15])
            p.st.inc(ins); gidx += 1
        for dc in range(8):
            tensor.wait_ge(p.sv.h, sv0 + 3 + gidx)
            ins = nc.tensor.transpose(g["ps_tp"][0:15, 0:128],
                                      wu_nat[:, dc, 0:15], g["ident"][:])
            p.st.inc(ins); gidx += 1
        tensor.wait_ge(p.sv.h, sv0 + 3 + gidx)
        for ch in range(2):
            ins = nc.tensor.transpose(g["ps_tp"][0:4, ch * 128:(ch + 1) * 128],
                                      simb[:, ch, :], g["ident"][:])
        p.st.inc(ins); gidx += 1

    @block.vector
    def _(vector):
        p.sv.inc(vector.memset(c38[:], 1e-38))
        p.sv.inc(vector.memset(t_scl[0][:], 0.0))
        p.sv.inc(vector.memset(t_scl[1][:], 0.0))
        gidx = 3
        for dc in range(8):
            vector.wait_ge(p.st.h, st0 + (gidx - 3) + 1)
            p.sv.inc(vector.tensor_copy(g["wbuf"][1][:, dc, :], g["ps_tp"][:]))
            gidx += 1
        for dc in range(8):
            vector.wait_ge(p.st.h, st0 + (gidx - 3) + 1)
            p.sv.inc(vector.tensor_copy(wpT[:, dc, :], g["ps_tp"][:, 0:256]))
            gidx += 1
        for dc in range(8):
            vector.wait_ge(p.st.h, st0 + (gidx - 3) + 1)
            p.sv.inc(vector.tensor_copy(wdT[:, dc, 0:15], g["ps_tp"][:, 0:15]))
            gidx += 1
        for dc in range(8):
            vector.wait_ge(p.st.h, st0 + (gidx - 3) + 1)
            p.sv.inc(vector.tensor_copy(wuT[0:15, dc * 128:(dc + 1) * 128],
                                        g["ps_tp"][0:15, 0:128]))
            gidx += 1
        vector.wait_ge(p.st.h, st0 + (gidx - 3) + 1)
        p.sv.inc(vector.tensor_copy(simTf[0:4, :], g["ps_tp"][0:4, 0:256]))
    sv_wprep = p.sv.n

    ss0 = p.ss.n

    @block.scalar
    def _(scalar):
        scalar.wait_ge(p.sv.h, sv_wprep)
        p.ss.inc(scalar.activation(simsq[0:4, :], simTf[0:4, :], AF.Square,
                                   accum_out=tmpv[0:4, 3:4]))
        scalar.wait_ge(p.ss.h, p.ss.n)
        p.ss.inc(scalar.activation(tmpv[0:4, 4:5], tmpv[0:4, 3:4], AF.Ln,
                                   bias=c38[0:4, :]))
        scalar.wait_ge(p.ss.h, p.ss.n)
        p.ss.inc(scalar.activation(tmpv[0:4, 5:6], tmpv[0:4, 4:5], AF.Exp, scale=-0.5))
        p.ss.inc(scalar.activation(tmpv[0:1, 1:2], tmpv[0:1, 0:1], AF.Identity))
    ss_sim = p.ss.n

    @block.vector
    def _(vector):
        vector.wait_ge(p.ss.h, ss_sim)
        p.sv.inc(vector.tensor_scalar(tmpv[0:4, 6:7], tmpv[0:4, 5:6], 1e12, None,
                                      ALU.min))
        p.sv.inc(vector.tensor_scalar(tmpv[0:1, 2:3], tmpv[0:1, 1:2], CLAMP, None,
                                      ALU.min))
    sv_rn4 = p.sv.n

    @block.scalar
    def _(scalar):
        scalar.wait_ge(p.sv.h, sv_rn4)
        p.ss.inc(scalar.activation(simnT[0:4, :], simTf[0:4, :], AF.Identity,
                                   scale=tmpv[0:4, 6:7]))
        p.ss.inc(scalar.activation(tmpv[0:1, 3:4], tmpv[0:1, 2:3], AF.Exp))
    ss_simn = p.ss.n

    @block.vector
    def _(vector):
        vector.wait_ge(p.ss.h, ss_simn)
        p.sv.inc(vector.tensor_copy(s_bf[0:1, 0:1], tmpv[0:1, 3:4]))
    sv_sbf = p.sv.n

    st1 = p.st.n

    @block.tensor
    def _(tensor):
        tensor.wait_ge(p.sv.h, sv_sbf)
        for ch in range(2):
            nc.tensor.transpose(g["ps_tp"][:, ch * 128:ch * 128 + 4],
                                simnT[0:4, ch * 128:(ch + 1) * 128],
                                g["ident"][0:4, 0:4])
        p.st.inc(nc.tensor.matmul(g["ps_e"][0:128, 0:1], g["onescol"][:],
                                  s_bf[0:1, 0:1], start=True, stop=True))
    st_simn = p.st.n

    @block.vector
    def _(vector):
        vector.wait_ge(p.st.h, st_simn)
        p.sv.inc(vector.tensor_copy(simn[:],
                                    g["ps_tp"][:, 0:256].rearrange(
                                        "p (c t) -> p c t", c=2)[:, :, 0:4]))
        p.sv.inc(vector.tensor_copy(s_bc[:], g["ps_e"][0:128, 0:1]))
    sv_router_w = p.sv.n

    stt = {}; sst = {}; svt = {}
    st_c = p.st.n; ss_c = p.ss.n; sv_c = p.sv.n
    for i in range(NT):
        st_c += 1; stt[("proj", i)] = st_c
        st_c += 1; stt[("dall", i)] = st_c
        st_c += 1; stt[("pnT", i)] = st_c
        st_c += 1; stt[("lg", i)] = st_c
        st_c += 1; stt[("tsT", i)] = st_c
        st_c += 2; stt[("base", i)] = st_c
    for i in range(NT):
        ss_c += 1; sst[("sq", i)] = ss_c
        ss_c += 1; sst[("ln", i)] = ss_c
        ss_c += 1; sst[("ex", i)] = ss_c
        ss_c += 1; sst[("pn", i)] = ss_c
        ss_c += 1; sst[("lw", i)] = ss_c
    for i in range(NT):
        sv_c += 1; svt[("min", i)] = sv_c
        sv_c += 1; svt[("dev", i)] = sv_c
        sv_c += 1; svt[("pnev", i)] = sv_c
        sv_c += 8; svt[("rt", i)] = sv_c
        sv_c += 1; svt[("tsev", i)] = sv_c
        sv_c += 2; svt[("res", i)] = sv_c

    ld_v = {}
    ss_entry = p.ss.n

    @block.sync
    def _(sync):
        sync.wait_ge(p.ss.h, ss_entry)
        for i in range(NT):
            ld_v[i] = p.dx[i % 2].start(sync, g["xtile"][i % 2][:], xd[i])
            if i >= 1:
                sync.wait_ge(p.sv.h, svt[("res", i - 1)])
                p.dx[(i - 1) % 2].start(sync, xd[i - 1], g["xtile"][(i - 1) % 2][:])
        sync.wait_ge(p.sv.h, svt[("res", NT - 1)])
        p.dx[(NT - 1) % 2].start(sync, xd[NT - 1], g["xtile"][(NT - 1) % 2][:])

    @block.tensor
    def _(tensor):
        tensor.wait_ge(p.sv.h, sv_router_w)
        for i in range(NT):
            if i >= 1:
                tensor.wait_ge(p.ss.h, sst[("pn", i - 1)])
            for c in range(NCH):
                nc.tensor.matmul(g["ps_e"][:, 0:256],
                                 g["xn_fm"][:, c, i * 128:(i + 1) * 128],
                                 wpT[:, c, :], start=(c == 0), stop=False)
            p.st.inc(nc.tensor.matmul(g["ps_e"][:, 0:256], g["onescol"][:],
                                      g["biasrow"][:, 0:256], start=False, stop=True))
            if i >= 1:
                tensor.wait_ge(p.sv.h, svt[("dev", i - 1)])
            ins = None
            for c in range(NCH):
                ins = nc.tensor.matmul(g["ps_e"][:, 256:271],
                                       g["xn_fm"][:, c, i * 128:(i + 1) * 128],
                                       wdT[:, c, 0:15], start=(c == 0),
                                       stop=(c == NCH - 1))
            p.st.inc(ins)
            tensor.wait_ge(p.ss.h, sst[("pn", i)])
            if i >= 1:
                tensor.wait_ge(p.sv.h, svt[("pnev", i - 1)])
            ins = None
            for ch in range(2):
                ins = nc.tensor.transpose(g["ps_tp"][:, 256 + ch * 128:256 + (ch + 1) * 128],
                                          pn_t[i % 2][:, ch * 128:(ch + 1) * 128],
                                          g["ident"][:])
            p.st.inc(ins)
            tensor.wait_ge(p.sv.h, svt[("pnev", i)])
            if i >= 1:
                tensor.wait_ge(p.sv.h, svt[("rt", i - 1)])
            ins = None
            for ch in range(2):
                ins = nc.tensor.matmul(g["ps_e"][:, 384:388],
                                       p_fm[i % 2][:, ch, :], simn[:, ch, :],
                                       start=(ch == 0), stop=(ch == 1))
            p.st.inc(ins)
            tensor.wait_ge(p.sv.h, svt[("rt", i)])
            if i >= 1:
                tensor.wait_ge(p.sv.h, svt[("tsev", i - 1)])
            p.st.inc(nc.tensor.transpose(g["ps_tp"][0:16, 512:640],
                                         t_scl[i % 2][:], g["ident"][:]))
            tensor.wait_ge(p.sv.h, svt[("tsev", i)])
            if i >= 1:
                tensor.wait_ge(p.sv.h, svt[("res", i - 1)])
            ins = None
            for half in range(2):
                for c in range(NCH):
                    nc.tensor.matmul(g["ps_s"][half][:],
                                     g["xn_fm"][:, c, i * 128:(i + 1) * 128],
                                     g["wbuf"][1][:, c, half * 512:(half + 1) * 512],
                                     start=(c == 0), stop=False)
                ins = nc.tensor.matmul(g["ps_s"][half][:], t_fm[i % 2][0:15, :],
                                       wuT[0:15, half * 512:(half + 1) * 512],
                                       start=False, stop=True)
                p.st.inc(ins)

    @block.scalar
    def _(scalar):
        scalar.wait_ge(p.sv.h, sv_router_w)
        for i in range(NT):
            scalar.wait_ge(p.st.h, stt[("proj", i)])
            if i >= 1:
                scalar.wait_ge(p.ss.h, sst[("sq", i - 1)])
            p.ss.inc(scalar.activation(sqd[:, 0:256], g["ps_e"][:, 0:256], AF.Square,
                                       accum_out=ssp[:]))
            scalar.wait_ge(p.ss.h, p.ss.n)
            p.ss.inc(scalar.activation(rn[:], ssp[:], AF.Ln, bias=c38[:]))
            scalar.wait_ge(p.ss.h, p.ss.n)
            p.ss.inc(scalar.activation(rn[:], rn[:], AF.Exp, scale=-0.5))
            scalar.wait_ge(p.sv.h, svt[("min", i)])
            if i >= 2:
                scalar.wait_ge(p.st.h, stt[("pnT", i - 2)])
            p.ss.inc(scalar.activation(pn_t[i % 2][:], g["ps_e"][:, 0:256],
                                       AF.Identity, scale=rn[:]))
            scalar.wait_ge(p.sv.h, svt[("rt", i)] - 6)
            p.ss.inc(scalar.activation(lw[:], lgc[:], AF.Exp, scale=s_bc[:],
                                       accum_out=esum[:]))

    @block.vector
    def _(vector):
        for i in range(NT):
            vector.wait_ge(p.ss.h, sst[("ex", i)])
            p.sv.inc(vector.tensor_scalar(rn[:], rn[:], 1e12, None, ALU.min))
            vector.wait_ge(p.st.h, stt[("dall", i)])
            p.sv.inc(vector.tensor_copy(dall[i % 2][:, 0:15], g["ps_e"][:, 256:271]))
            vector.wait_ge(p.st.h, stt[("pnT", i)])
            p.sv.inc(vector.tensor_copy(p_fm[i % 2][:],
                                        g["ps_tp"][:, 256:512].rearrange(
                                            "p (c t) -> p c t", c=2)))
            vector.wait_ge(p.st.h, stt[("lg", i)])
            p.sv.inc(vector.reduce_max(rm[:].unsqueeze(2),
                                       g["ps_e"][:, 384:388].rearrange(
                                           "p (a b) -> p a b", a=1), AX.X))
            vector.wait_ge(p.sv.h, p.sv.n)
            p.sv.inc(vector.tensor_scalar(lgc[:], g["ps_e"][:, 384:388], rm[:],
                                          None, ALU.subtract))
            vector.wait_ge(p.ss.h, sst[("lw", i)])
            p.sv.inc(vector.reciprocal(er[:], esum[:]))
            vector.wait_ge(p.sv.h, p.sv.n)
            p.sv.inc(vector.tensor_scalar(lw[:], lw[:], er[:], None, ALU.mult))
            vector.wait_ge(p.sv.h, p.sv.n)
            p.sv.inc(vector.reduce_max(rm[:].unsqueeze(2),
                                       lw[:].rearrange("p (a b) -> p a b", a=1), AX.X))
            vector.wait_ge(p.sv.h, p.sv.n)
            p.sv.inc(vector.tensor_scalar(msk[:], lw[:], rm[:], None, ALU.is_ge))
            vector.wait_ge(p.sv.h, p.sv.n)
            p.sv.inc(vector.tensor_mul(ret[:], lw[:], msk[:]))
            vector.wait_ge(p.sv.h, p.sv.n)
            if i >= 2:
                vector.wait_ge(p.st.h, stt[("tsT", i - 2)])
            ro = 0
            ins = None
            for e in range(4):
                r = [1, 2, 4, 8][e]
                ins = vector.tensor_scalar(t_scl[i % 2][:, ro:ro + r],
                                           dall[i % 2][:, ro:ro + r],
                                           ret[:, e:e + 1], None, ALU.mult)
                ro += r
            p.sv.inc(ins)
            vector.wait_ge(p.st.h, stt[("tsT", i)])
            if i >= 2:
                vector.wait_ge(p.st.h, stt[("base", i - 2)])
            p.sv.inc(vector.tensor_copy(t_fm[i % 2][0:16, :],
                                        g["ps_tp"][0:16, 512:640]))
            vector.wait_ge(p.st.h, stt[("base", i)])
            vector.wait_ge(p.dx[i % 2].h, ld_v[i])
            for half in range(2):
                sl = g["xtile"][i % 2][:, half * 512:(half + 1) * 512]
                p.sv.inc(vector.tensor_add(sl, sl, g["ps_s"][half][:]))
    assert p.sv.n == sv_c, (p.sv.n, sv_c)
    assert p.ss.n == ss_c, (p.ss.n, ss_c)
    assert p.st.n == st_c, (p.st.n, st_c)


QVA = 5
QVB = 12
QS = 2


def emit_quant_a(p, q, src, width, wv=()):
    nc, block = p.nc, p.block
    nb = width // 16
    sv0, ss0 = p.sv.n, p.ss.n

    @block.vector
    def _(vector):
        vector.wait_ge(p.sv.h, sv0)
        for sem, cnt in wv:
            vector.wait_ge(sem, cnt)
        p.sv.inc(vector.tensor_reduce(
            q["bm"][:, 0:nb].unsqueeze(2),
            src.rearrange("p (a b) -> p a b", b=16), AX.X, ALU.max,
            apply_absolute_value=True))
        vector.wait_ge(p.sv.h, p.sv.n)
        p.sv.inc(vector.tensor_scalar(q["sc2"][:, 0:nb], q["bm"][:, 0:nb],
                                      1.0 / 6.0, 1e-12, ALU.mult, ALU.max))
        vector.wait_ge(p.sv.h, p.sv.n)
        p.sv.inc(vector.reciprocal(q["inv"][:, 0:nb], q["sc2"][:, 0:nb]))
        vector.wait_ge(p.sv.h, p.sv.n)
        p.sv.inc(vector.tensor_copy(
            q["invx"][:, 0:width].rearrange("p (a b) -> p a b", b=16),
            q["inv"][:, 0:nb].unsqueeze(2).broadcast_to([128, nb, 16])))
        vector.wait_ge(p.sv.h, p.sv.n)
        p.sv.inc(vector.tensor_mul(src, src, q["invx"][:, 0:width]))
    sv_y = p.sv.n

    @block.scalar
    def _(scalar):
        scalar.wait_ge(p.ss.h, ss0)
        scalar.wait_ge(p.sv.h, sv_y)
        p.ss.inc(scalar.activation(q["sg"][:, 0:width], src, AF.Sign))
        p.ss.inc(scalar.activation(q["ab"][:, 0:width], src, AF.Abs))
    assert p.sv.n == sv0 + QVA and p.ss.n == ss0 + QS
    return sv_y, p.ss.n


def emit_quant_b(p, q, src, dst, width, ss_cast, wv_dst=()):
    nc, block = p.nc, p.block
    nb = width // 16
    sv0 = p.sv.n

    @block.vector
    def _(vector):
        vector.wait_ge(p.sv.h, sv0)
        vector.wait_ge(p.ss.h, ss_cast)
        tA = q["tA"][:, 0:width]
        tB = q["tB"][:, 0:width]
        tC = q["tC"][:, 0:width]
        ab = q["ab"][:, 0:width]
        p.sv.inc(vector.tensor_scalar(tA, ab, 0.25, None, ALU.is_gt))
        vector.wait_ge(p.sv.h, p.sv.n)
        p.sv.inc(vector.scalar_tensor_tensor(tA, ab, 0.75, tA, ALU.is_gt,
                                             ALU.add))
        vector.wait_ge(p.sv.h, p.sv.n)
        p.sv.inc(vector.scalar_tensor_tensor(tA, ab, 1.25, tA, ALU.is_gt,
                                             ALU.add))
        vector.wait_ge(p.sv.h, p.sv.n)
        p.sv.inc(vector.scalar_tensor_tensor(tA, ab, 1.75, tA, ALU.is_gt,
                                             ALU.add))
        p.sv.inc(vector.tensor_scalar(tB, ab, 2.5, None, ALU.is_gt))
        vector.wait_ge(p.sv.h, p.sv.n)
        p.sv.inc(vector.scalar_tensor_tensor(tB, ab, 3.5, tB, ALU.is_gt,
                                             ALU.add))
        p.sv.inc(vector.tensor_scalar(tC, ab, 5.0, 2.0, ALU.is_gt, ALU.mult))
        vector.wait_ge(p.sv.h, p.sv.n)
        p.sv.inc(vector.scalar_tensor_tensor(tB, tA, 0.5, tB, ALU.mult,
                                             ALU.add))
        vector.wait_ge(p.sv.h, p.sv.n)
        p.sv.inc(vector.tensor_add(tB, tB, tC))
        vector.wait_ge(p.sv.h, p.sv.n)
        p.sv.inc(vector.tensor_mul(tB, tB, q["sg"][:, 0:width]))
        p.sv.inc(vector.tensor_copy(
            tA.rearrange("p (a b) -> p a b", b=16),
            q["sc2"][:, 0:nb].unsqueeze(2).broadcast_to([128, nb, 16])))
        vector.wait_ge(p.sv.h, p.sv.n)
        for sem, cnt in wv_dst:
            vector.wait_ge(sem, cnt)
        p.sv.inc(vector.tensor_mul(dst, tB, tA))
    assert p.sv.n == sv0 + QVB
    return p.sv.n


def emit_quant(p, q, src, dst, width, wv=(), wv_dst=()):
    sv_y, ss_cast = emit_quant_a(p, q, src, width, wv=wv)
    sv = emit_quant_b(p, q, src, dst, width, ss_cast, wv_dst=wv_dst)
    return sv, ss_cast


def emit_ffn1(p, g, xd, Wg, Wu, Wd, cch, lnst, first):
    nc, block = p.nc, p.block
    ar = p.arena
    ar.reset()

    GROUPS = [(0, 6), (6, 6), (12, 6), (18, 4)]

    wgq = ar.bf16([128, NCH, 768])
    wuq = ar.bf16([128, NCH, 768])
    wdq = ar.bf16([128, 6, 1024])
    xq_fm = ar.bf16([128, NCH, S])
    ybuf = [ar.f32([128, 1024]) for _ in range(2)]
    qout = [ar.bf16([128, 1024]) for _ in range(2)]
    hq_fm = [ar.bf16([128, 6, 128]) for _ in range(2)]
    ttile = [ar.bf16([128, 1024]) for _ in range(2)]
    ab2 = [ar.f32([128, 1024]) for _ in range(2)]
    sg2 = [ar.bf16([128, 1024]) for _ in range(2)]
    q0 = dict(bm=ar.f32([128, 64]), sc2=ar.f32([128, 64]),
              inv=ar.f32([128, 64]), invx=ar.f32([128, 1024]),
              tA=ar.bf16([128, 1024]), tB=ar.bf16([128, 1024]),
              tC=ar.bf16([128, 1024]))
    qs = [dict(q0, ab=ab2[0], sg=sg2[0]), dict(q0, ab=ab2[1], sg=sg2[1])]

    ybuf_free = [(p.sv.n, p.ss.n)] * 2
    qout_free = [p.st.n, p.st.n]
    ttile_free = [p.dps.n, p.dps.n]
    pstp_free = [p.sv.n]

    prep_stores = p.dps.n
    if first:
        prep = []
        for rc in range(22):
            prep.append((Wg[rc * 128:(rc + 1) * 128, :], 1024,
                         cch["wgT"][:, :, rc * 128:(rc + 1) * 128], NCH))
        for rc in range(22):
            prep.append((Wu[rc * 128:(rc + 1) * 128, :], 1024,
                         cch["wuT"][:, :, rc * 128:(rc + 1) * 128], NCH))
        for rc in range(8):
            for ks in range(3):
                ksw = 768 if ks == 2 else 1024
                prep.append((Wd[rc * 128:(rc + 1) * 128,
                                ks * 1024:ks * 1024 + ksw], ksw,
                             cch["wdT"][:, ks * 8:ks * 8 + ksw // 128,
                                        rc * 128:(rc + 1) * 128], ksw // 128))
        ldv, cpv, acast, bsv = {}, {}, {}, {}

        def emit_store(sync, jj):
            wdt_, dstw_, ntr_ = prep[jj][1], prep[jj][2], prep[jj][3]
            sync.wait_ge(p.sv.h, cpv[jj])
            p.dps.start(sync, dstw_,
                        ttile[jj % 2][:, 0:wdt_].rearrange(
                            "p (c t) -> p c t", c=ntr_))
            ttile_free[jj % 2] = p.dps.n

        def prep_b(j):
            wdt, ntr = prep[j][1], prep[j][3]
            bsv[j] = emit_quant_b(p, qs[j % 2], ybuf[j % 2][:, 0:wdt],
                                  qout[j % 2][:, 0:wdt], wdt, acast[j],
                                  wv_dst=[(p.st.h, qout_free[j % 2])])

            @block.tensor
            def _(tensor, j=j, wdt=wdt, ntr=ntr, pf=pstp_free[0]):
                tensor.wait_ge(p.sv.h, max(bsv[j], pf))
                ins = None
                for c in range(ntr):
                    ins = nc.tensor.transpose(
                        g["ps_tp"][:, c * 128:(c + 1) * 128],
                        qout[j % 2][:, c * 128:(c + 1) * 128], g["ident"][:])
                p.st.inc(ins)
            qout_free[j % 2] = p.st.n
            st_tr = p.st.n

            @block.vector
            def _(vector, j=j, wdt=wdt, st_tr=st_tr, tf=ttile_free[j % 2]):
                vector.wait_ge(p.st.h, st_tr)
                vector.wait_ge(p.dps.h, tf)
                cpv[j] = p.sv.inc(vector.tensor_copy(ttile[j % 2][:, 0:wdt],
                                                     g["ps_tp"][:, 0:wdt]))
            pstp_free[0] = p.sv.n

        for j, (srcw, wdt, dstw, ntr) in enumerate(prep):
            yfs, yfss = ybuf_free[j % 2]

            @block.sync
            def _(sync, j=j, srcw=srcw, wdt=wdt, yfs=yfs, yfss=yfss):
                if j >= 2:
                    sync.wait_ge(p.sv.h, bsv[j - 2])
                sync.wait_ge(p.ss.h, yfss)
                s_ = p.dp[j % 4]
                ldv[j] = (s_.h, s_.start(sync, ybuf[j % 2][:, 0:wdt], srcw))
                if j >= 2:
                    emit_store(sync, j - 2)

            if j >= 1:
                prep_b(j - 1)
            ybuf_free[j % 2] = emit_quant_a(
                p, qs[j % 2], ybuf[j % 2][:, 0:wdt], wdt, wv=[ldv[j]])
            acast[j] = ybuf_free[j % 2][1]
        prep_b(len(prep) - 1)

        @block.sync
        def _(sync):
            emit_store(sync, len(prep) - 2)
            emit_store(sync, len(prep) - 1)
        prep_stores = p.dps.n

    dq2 = {}
    ss_x0 = p.ss.n

    @block.sync
    def _(sync):
        sync.wait_ge(p.ss.h, lnst["ss_sq"])
        sync.wait_ge(p.sv.h, lnst["sv_red16"])
        for i in range(NT):
            if i >= 2:
                sync.wait_ge(p.ss.h, ss_x0 + 3 * (i - 2) + 1)
            dq2[i] = p.dx[i % 2].start(sync, g["xtile"][i % 2][:], xd[i])

    xev, nrm, xacast, xbsv = {}, {}, {}, {}

    def x_b(i):
        xbsv[i] = emit_quant_b(p, qs[i % 2], ybuf[i % 2][:], qout[i % 2][:],
                               1024, xacast[i],
                               wv_dst=[(p.st.h, qout_free[i % 2])])

        @block.tensor
        def _(tensor, i=i, pf=pstp_free[0]):
            tensor.wait_ge(p.sv.h, max(xbsv[i], pf))
            ins = None
            for c in range(NCH):
                ins = nc.tensor.transpose(g["ps_tp"][:, c * 128:(c + 1) * 128],
                                          qout[i % 2][:, c * 128:(c + 1) * 128],
                                          g["ident"][:])
            p.st.inc(ins)
        qout_free[i % 2] = p.st.n
        st_tr = p.st.n

        @block.vector
        def _(vector, i=i, st_tr=st_tr):
            vector.wait_ge(p.st.h, st_tr)
            xev[i] = p.sv.inc(vector.tensor_copy(
                xq_fm[:, :, i * 128:(i + 1) * 128],
                g["ps_tp"][:].rearrange("p (c t) -> p c t", c=NCH)))
        pstp_free[0] = p.sv.n

    for i in range(NT):
        @block.scalar
        def _(scalar, i=i, yf=ybuf_free[i % 2][0]):
            if i == 0:
                scalar.wait_ge(p.sv.h, lnst["sv_stats"])
            scalar.wait_ge(p.dx[i % 2].h, dq2[i])
            scalar.wait_ge(p.sv.h, yf)
            if i >= 2:
                scalar.wait_ge(p.sv.h, xbsv.get(i - 2, 0))
            nrm[i] = p.ss.inc(scalar.activation(
                ybuf[i % 2][:], g["xtile"][i % 2][:], AF.Identity,
                bias=g["nmrs"][:, i:i + 1], scale=g["rstd"][:, i:i + 1]))
        assert nrm[i] == ss_x0 + 3 * i + 1

        if i >= 1:
            x_b(i - 1)
        ybuf_free[i % 2] = emit_quant_a(p, qs[i % 2], ybuf[i % 2][:], 1024,
                                        wv=[(p.ss.h, nrm[i])])
        xacast[i] = ybuf_free[i % 2][1]
    x_b(NT - 1)

    wld = {}
    guv, trh, dnv, silv, hmv, evv, resv, ldx = {}, {}, {}, {}, {}, {}, {}, {}
    gacast, gbsv = {}, {}
    war = dict(res=0, sil=0, hm=0)
    for gi, (k0, ng) in enumerate(GROUPS):
        gw = ng * 128
        subs = [(0, 512), (512, gw - 512)] if gw > 512 else [(0, gw)]

        @block.gpsimd
        def _(gpsimd, gi=gi, k0=k0, ng=ng):
            if first and gi == 0:
                gpsimd.wait_ge(p.dps.h, prep_stores)
            if gi >= 1:
                gpsimd.wait_ge(p.st.h, guv[(gi - 1, NT - 1)])
                gpsimd.wait_ge(p.st.h, dnv[(gi - 1, NT - 1)])
            wr = p.load_par(gpsimd, wgq[:, :, 0:gw],
                            cch["wgT"][:, :, k0 * 128:k0 * 128 + gw], n=2)
            wr += p.load_par(gpsimd, wuq[:, :, 0:gw],
                             cch["wuT"][:, :, k0 * 128:k0 * 128 + gw], n=2)
            c1_ = p.dw[0].start(gpsimd, wdq[:, 0:ng // 2, :],
                                cch["wdT"][:, k0:k0 + ng // 2, :])
            c2_ = p.dw[1].start(gpsimd, wdq[:, ng // 2:ng, :],
                                cch["wdT"][:, k0 + ng // 2:k0 + ng, :])
            wld[gi] = wr + [(p.dw[0].h, c1_), (p.dw[1].h, c2_)]

        for ii in range(NT + 2):
            @block.tensor
            def _(tensor, gi=gi, ii=ii, gw=gw, subs=subs,
                  sil_war=war["sil"], hm_war=war["hm"]):
                if ii < NT:
                    wwait(p, tensor, wld[gi])
                    tensor.wait_ge(p.sv.h, xev[ii])
                    if sil_war:
                        tensor.wait_ge(p.ss.h, sil_war)
                        tensor.wait_ge(p.sv.h, hm_war)
                    for wq_, off in ((wgq, 0), (wuq, 1024)):
                        ins = None
                        for so, sw in subs:
                            for c in range(NCH):
                                ins = nc.tensor.matmul(
                                    g["ps_pv"][:, off + so:off + so + sw],
                                    xq_fm[:, c, ii * 128:(ii + 1) * 128],
                                    wq_[:, c, so:so + sw],
                                    start=(c == 0), stop=(c == NCH - 1))
                        p.st.inc(ins)
                    guv[(gi, ii)] = p.st.n

            if ii < NT:
                @block.scalar
                def _(scalar, gi=gi, ii=ii, gw=gw, yf=ybuf_free[ii % 2][0]):
                    scalar.wait_ge(p.st.h, guv[(gi, ii)] - 1)
                    scalar.wait_ge(p.sv.h, yf)
                    bprev = gbsv.get((gi, ii - 2), xbsv.get(ii - 2 + NT, 0))
                    if bprev:
                        scalar.wait_ge(p.sv.h, bprev)
                    silv[(gi, ii)] = p.ss.inc(scalar.activation(
                        ybuf[ii % 2][:, 0:gw], g["ps_pv"][:, 0:gw], AF.Silu))
                war["sil"] = silv[(gi, ii)]

            if ii >= 1 and ii - 1 < NT:
                gbsv[(gi, ii - 1)] = emit_quant_b(
                    p, qs[(ii - 1) % 2], ybuf[(ii - 1) % 2][:, 0:gw],
                    qout[(ii - 1) % 2][:, 0:gw], gw, gacast[(gi, ii - 1)],
                    wv_dst=[(p.st.h, qout_free[(ii - 1) % 2])])

            if ii < NT:
                @block.vector
                def _(vector, gi=gi, ii=ii, gw=gw):
                    vector.wait_ge(p.ss.h, silv[(gi, ii)])
                    vector.wait_ge(p.st.h, guv[(gi, ii)])
                    hmv[(gi, ii)] = p.sv.inc(vector.tensor_mul(
                        ybuf[ii % 2][:, 0:gw], ybuf[ii % 2][:, 0:gw],
                        g["ps_pv"][:, 1024:1024 + gw]))
                war["hm"] = hmv[(gi, ii)]
                ybuf_free[ii % 2] = emit_quant_a(
                    p, qs[ii % 2], ybuf[ii % 2][:, 0:gw], gw)
                gacast[(gi, ii)] = ybuf_free[ii % 2][1]

            if ii >= 1 and ii - 1 < NT:
                @block.tensor
                def _(tensor, gi=gi, ii=ii, ng=ng, pf=pstp_free[0]):
                    tensor.wait_ge(p.sv.h, max(gbsv[(gi, ii - 1)], pf))
                    ins = None
                    for c in range(ng):
                        ins = nc.tensor.transpose(
                            g["ps_tp"][:, c * 128:(c + 1) * 128],
                            qout[(ii - 1) % 2][:, c * 128:(c + 1) * 128],
                            g["ident"][:])
                    p.st.inc(ins)
                    trh[(gi, ii - 1)] = p.st.n
                qout_free[(ii - 1) % 2] = trh[(gi, ii - 1)]

                @block.vector
                def _(vector, gi=gi, ii=ii, ng=ng, gw=gw):
                    vector.wait_ge(p.st.h, trh[(gi, ii - 1)])
                    if (gi, ii - 3) in dnv:
                        vector.wait_ge(p.st.h, dnv[(gi, ii - 3)])
                    evv[(gi, ii - 1)] = p.sv.inc(vector.tensor_copy(
                        hq_fm[(ii - 1) % 2][:, 0:ng, :],
                        g["ps_tp"][:, 0:gw].rearrange("p (c t) -> p c t", c=ng)))
                pstp_free[0] = evv[(gi, ii - 1)]

            if ii >= 2 and ii - 2 < NT:
                @block.vector
                def _(vector, gi=gi, ii=ii):
                    vector.wait_ge(p.st.h, dnv[(gi, ii - 2)])
                    vector.wait_ge(p.dx[(ii - 2) % 2].h, ldx[(gi, ii - 2)])
                    xt = g["xtile"][(ii - 2) % 2]
                    p.sv.inc(vector.tensor_add(xt[:, 0:512], xt[:, 0:512],
                                               g["ps_e"][:]))
                    resv[(gi, ii - 2)] = p.sv.inc(vector.tensor_add(
                        xt[:, 512:1024], xt[:, 512:1024], g["ps_s"][0][:]))
                war["res"] = resv[(gi, ii - 2)]

            if ii >= 1 and ii - 1 < NT:
                @block.tensor
                def _(tensor, gi=gi, ii=ii, ng=ng, res_war=war["res"]):
                    tensor.wait_ge(p.sv.h, evv[(gi, ii - 1)])
                    if res_war:
                        tensor.wait_ge(p.sv.h, res_war)
                    for pb, off in ((g["ps_e"], 0), (g["ps_s"][0], 512)):
                        ins = None
                        for c in range(ng):
                            ins = nc.tensor.matmul(
                                pb[:], hq_fm[(ii - 1) % 2][:, c, :],
                                wdq[:, c, off:off + 512],
                                start=(c == 0), stop=(c == ng - 1))
                        p.st.inc(ins)
                    dnv[(gi, ii - 1)] = p.st.n

            @block.sync
            def _(sync, gi=gi, ii=ii):
                if ii >= 2 and ii - 2 < NT:
                    sync.wait_ge(p.sv.h, resv[(gi, ii - 2)])
                    p.dx[(ii - 2) % 2].start(sync, xd[ii - 2],
                                             g["xtile"][(ii - 2) % 2][:])
                if ii < NT:
                    if gi == 0:
                        sync.wait_ge(p.ss.h, nrm[NT - 2 + (ii % 2)])
                    ldx[(gi, ii)] = p.dx[ii % 2].start(
                        sync, g["xtile"][ii % 2][:], xd[ii])


def emit_proj_tm_act(p, g, wslot, dst_fn, w_ready, act):
    nc, block = p.nc, p.block
    st0, ss0 = p.st.n, p.ss.n

    @block.tensor
    def _(tensor):
        wwait(p, tensor, w_ready, wslot)
        tensor.wait_ge(p.sv.h, p.sv_xn)
        tensor.wait_ge(p.ss.h, ss0)
        for ih in range(NT * 2):
            i, half = divmod(ih, 2)
            pb = g["ps_s"][ih % 2]
            if ih >= 2:
                tensor.wait_ge(p.ss.h, ss0 + ih - 1)
            ins = None
            for c in range(NCH):
                ins = nc.tensor.matmul(pb[:],
                                       g["xn_fm"][:, c, i * 128:(i + 1) * 128],
                                       g["wbuf"][wslot][:, c, half * 512:(half + 1) * 512],
                                       start=(c == 0), stop=(c == NCH - 1))
            p.st.inc(ins)

    @block.scalar
    def _(scalar):
        for ih in range(NT * 2):
            i, half = divmod(ih, 2)
            scalar.wait_ge(p.st.h, st0 + ih + 1)
            p.ss.inc(scalar.activation(dst_fn(i, half), g["ps_s"][ih % 2][:], act))
    return p.st.n, p.ss.n


def emit_proj_fm_act(p, g, wslot, dst_fm, w_ready, act):
    nc, block = p.nc, p.block
    st0, ss0 = p.st.n, p.ss.n

    @block.tensor
    def _(tensor):
        wwait(p, tensor, w_ready, wslot)
        tensor.wait_ge(p.sv.h, p.sv_xn)
        tensor.wait_ge(p.ss.h, ss0)
        for k in range(32):
            oc, sp = divmod(k, 4)
            pb = g["ps_s"][k % 2]
            if k >= 2:
                tensor.wait_ge(p.ss.h, ss0 + k - 1)
            ins = None
            for c in range(NCH):
                ins = nc.tensor.matmul(pb[:],
                                       g["wbuf"][wslot][:, c, oc * 128:(oc + 1) * 128],
                                       g["xn_fm"][:, c, sp * 512:(sp + 1) * 512],
                                       start=(c == 0), stop=(c == NCH - 1))
            p.st.inc(ins)

    @block.scalar
    def _(scalar):
        for k in range(32):
            oc, sp = divmod(k, 4)
            scalar.wait_ge(p.st.h, st0 + k + 1)
            p.ss.inc(scalar.activation(dst_fm[:, oc, sp * 512:(sp + 1) * 512],
                                       g["ps_s"][k % 2][:], act))
    return p.st.n, p.ss.n


def emit_lightning(p, g, xd, Wq, Wk, Wv, Wo, gamma, beta, Wg1, Wg2):
    nc, block = p.nc, p.block
    ar = p.arena
    ar.reset()

    def build_slopes(n):
        def po2(m):
            start = 2.0 ** (-2.0 ** (-(np.log2(m) - 3)))
            return [start * start ** i for i in range(m)]
        if np.log2(n).is_integer():
            return po2(n)
        closest = 2 ** int(np.floor(np.log2(n)))
        return po2(closest) + build_slopes(2 * closest)[0::2][:n - closest]
    s_np = np.array(build_slopes(NH), dtype=np.float32)
    s_np = s_np * (1.0 - 0 / (12 - 1) + 1e-05)
    slopes = [float(v) for v in s_np]
    lamc = [float(np.exp(np.float64(-v) * 128.0)) for v in slopes]

    q_fm = ar.bf16([128, NCH, S])
    k_tm = ar.bf16([128, NT, D])
    v_tm = ar.bf16([128, NT, D])
    kpair = ar.bf16([128, S])
    maskz = ar.bf16([128, NH, 128])
    kv_f = ar.f32([128, 8, HD])
    kv_b = ar.bf16([128, 8, HD])
    g1_fm = ar.bf16([128, S])
    _og = ar.bf16([128, 1024])
    og_t = [_og, _og]
    bb_bc = ar.bf16([128, 1024])
    wg1b = ar.bf16([128, 8, 64])
    o_tm = g["xn_fm"].rearrange("p c t -> p (c t)").rearrange(
        "p (a b) -> p a b", a=NT)
    og_fm = q_fm
    lamq = g["hmax"]
    lamk = g["segr"]
    gb_bc = g["sq2"][0]

    @block.gpsimd
    def _(gpsimd):
        g["_a"] = p.load_par(gpsimd, g["wbuf"][0][:],
                             Wq.rearrange("(c p) o -> p c o", p=128))
        p.dw[1].start(gpsimd, g["biasrow"][:, 0:D].unsqueeze(1),
                      gamma.rearrange("(a d) -> a d", a=1))
        p.dw[1].start(gpsimd, g["biasrow"][:, D:2 * D].unsqueeze(1),
                      beta.rearrange("(a d) -> a d", a=1))
        g["_b"] = p.load_par(gpsimd, g["wbuf"][1][:],
                             Wk.rearrange("(c p) o -> p c o", p=128))
    wq_rdy = g.pop("_a"); wk_rdy = g.pop("_b")

    st_q, _ = emit_proj_fm_act(p, g, 0, q_fm, wq_rdy, AF.Silu)

    @block.gpsimd
    def _(gpsimd):
        gpsimd.wait_ge(p.st.h, st_q)
        g["_a"] = p.load_par(gpsimd, g["wbuf"][0][:],
                             Wv.rearrange("(c p) o -> p c o", p=128))
    wv_rdy = g.pop("_a")

    st_k, _ = emit_proj_tm_act(p, g, 1,
                               lambda i, h: k_tm[:, i, h * 512:(h + 1) * 512],
                               wk_rdy, AF.Silu)

    @block.gpsimd
    def _(gpsimd):
        gpsimd.wait_ge(p.st.h, st_k)
        p.dw[1].start(gpsimd, wg1b[:],
                      Wg1.rearrange("(c p) o -> p c o", p=128))
        g["_b"] = p.dw[1].start(gpsimd, g["wbuf"][1][0:64, 5, :], Wg2)
    wg_rdy = g.pop("_b")

    st_v, ss_v = emit_proj_tm_act(p, g, 0,
                                  lambda i, h: v_tm[:, i, h * 512:(h + 1) * 512],
                                  wv_rdy, AF.Silu)

    @block.gpsimd
    def _(gpsimd):
        gpsimd.wait_ge(p.ss.h, ss_v)
        g["_a"] = p.load_par(gpsimd, g["wbuf"][0][:],
                             Wo.rearrange("(c p) o -> p c o", p=128))
    wo_rdy = g.pop("_a")

    ss_m0 = p.ss.n
    sv_mzb = p.sv.n

    @block.scalar
    def _(scalar):
        scalar.wait_ge(p.dmisc.h, p.dq_consts)
        for h in range(NH):
            if h >= 1:
                scalar.wait_ge(p.sv.h, sv_mzb + h)
            p.ss.inc(scalar.activation(maskz[:, h, :], g["dmatTc"][:], AF.Exp,
                                       scale=-slopes[h]))
            p.ss.inc(scalar.activation(lamq[:, h:h + 1], g["idxp1"][:], AF.Exp,
                                       scale=-slopes[h]))
            p.ss.inc(scalar.activation(lamk[:, h:h + 1], g["idxrv"][:], AF.Exp,
                                       scale=-slopes[h]))

    @block.vector
    def _(vector):
        for h in range(NH):
            vector.wait_ge(p.ss.h, ss_m0 + 3 * h + 1)
            p.sv.inc(vector.tensor_tensor(maskz[:, h, :], maskz[:, h, :],
                                          g["triT"][:], ALU.mult))
        p.sv.inc(vector.memset(kv_f[:], 0.0))
        p.sv.inc(vector.memset(kv_b[:], 0.0))
    sv_mz = p.sv.n
    assert sv_mz == sv_mzb + NH + 2

    st_g0, sv_g0 = p.st.n, p.sv.n

    @block.tensor
    def _(tensor):
        tensor.wait_ge(p.dw[1].h, wg_rdy)
        tensor.wait_ge(p.sv.h, max(p.sv_xn, sv_mz))
        for sp in range(4):
            if sp >= 1:
                tensor.wait_ge(p.sv.h, sv_g0 + sp)
            ins = None
            for c in range(NCH):
                ins = nc.tensor.matmul(g["ps_e"][0:64, 0:512],
                                       wg1b[:, c, :],
                                       g["xn_fm"][:, c, sp * 512:(sp + 1) * 512],
                                       start=(c == 0), stop=(c == NCH - 1))
            p.st.inc(ins)

    @block.vector
    def _(vector):
        for sp in range(4):
            vector.wait_ge(p.st.h, st_g0 + sp + 1)
            p.sv.inc(vector.tensor_copy(g1_fm[0:64, sp * 512:(sp + 1) * 512],
                                        g["ps_e"][0:64, 0:512]))
    sv_g1 = p.sv.n

    st_gb0 = p.st.n

    @block.tensor
    def _(tensor):
        tensor.wait_ge(p.sv.h, sv_g1)
        for j in range(4):
            if j >= 1:
                tensor.wait_ge(p.sv.h, sv_g1 + j)
            off = (0 if j < 2 else D) + (j % 2) * 512
            p.st.inc(nc.tensor.matmul(g["ps_e"][:, 0:512], g["onescol"][:],
                                      g["biasrow"][:, off:off + 512],
                                      start=True, stop=True))

    @block.vector
    def _(vector):
        for j in range(4):
            vector.wait_ge(p.st.h, st_gb0 + j + 1)
            dst = gb_bc if j < 2 else bb_bc
            p.sv.inc(vector.tensor_copy(dst[:, (j % 2) * 512:(j % 2) * 512 + 512],
                                        g["ps_e"][:, 0:512]))
    sv_gb = p.sv.n

    order = [(pr, i, hh) for pr in range(8) for i in range(NT) for hh in range(2)]
    LIGHT_ORDER = {order[k]: order[k - 1] for k in range(1, len(order))}
    LIGHT_ORDER[order[0]] = None
    LIGHT_PREV_HH = {}
    _last = {}
    for _key in order:
        LIGHT_PREV_HH[_key] = _last.get(_key[2])
        _last[_key[2]] = _key
    stc = {}; svc = {}
    st_c = p.st.n; sv_c2 = p.sv.n
    for pr in range(8):
        st_c += 2; stc[("kT", pr)] = st_c
        for i in range(NT):
            for hh in range(2):
                st_c += 1; stc[("S", pr, i, hh)] = st_c
                st_c += 1; stc[("o", pr, i, hh)] = st_c
                st_c += 1; stc[("kv", pr, i, hh)] = st_c
    for pr in range(8):
        sv_c2 += 2; svc[("kTe", pr)] = sv_c2
        for i in range(NT):
            for hh in range(2):
                sv_c2 += 1; svc[("pt", pr, i, hh)] = sv_c2
                sv_c2 += 1; svc[("kl", pr, i, hh)] = sv_c2
                sv_c2 += 2; svc[("oe", pr, i, hh)] = sv_c2
                sv_c2 += 2; svc[("kf", pr, i, hh)] = sv_c2

    @block.tensor
    def _(tensor):
        tensor.wait_ge(p.ss.h, ss_v)
        tensor.wait_ge(p.sv.h, sv_gb)
        for pr in range(8):
            for b in range(2):
                if (pr, b) != (0, 0):
                    prevk = (pr, 0) if b else (pr - 1, 1)
                    tensor.wait_ge(p.sv.h, svc[("kTe", prevk[0])] - (1 - prevk[1]))
                ins = None
                for t in range(8):
                    ins = nc.tensor.transpose(
                        g["ps_tp"][:, t * 128:(t + 1) * 128],
                        k_tm[:, b * 8 + t, pr * 128:(pr + 1) * 128],
                        g["ident"][:])
                p.st.inc(ins)
            tensor.wait_ge(p.sv.h, svc[("kTe", pr)])
            for i in range(NT):
                for hh in range(2):
                    h = 2 * pr + hh
                    base = hh * 64
                    key = (pr, i, hh)
                    prev_it = LIGHT_ORDER.get(key)
                    prev_hh = LIGHT_PREV_HH.get(key)
                    if prev_it is not None and LIGHT_ORDER.get(prev_it) is not None:
                        p2 = LIGHT_ORDER[prev_it]
                        tensor.wait_ge(p.sv.h, svc[("pt",) + p2])
                    p.st.inc(nc.tensor.matmul(
                        g["ps_s"][(i * 2 + hh) % 2][:, 0:128],
                        kpair[base:base + 64, i * 128:(i + 1) * 128],
                        q_fm[base:base + 64, pr, i * 128:(i + 1) * 128],
                        start=True, stop=True))
                    nlin = pr * 32 + i * 2 + hh
                    ob = (nlin % 2) * 512
                    kb = 1024 + (nlin % 2) * 512
                    prev2 = LIGHT_ORDER.get(prev_it) if prev_it else None
                    tensor.wait_ge(p.sv.h, svc[("pt", pr, i, hh)])
                    if i >= 1:
                        tensor.wait_ge(p.sv.h, svc[("kf", pr, i - 1, hh)])
                    if prev2 is not None:
                        tensor.wait_ge(p.sv.h, svc[("oe",) + prev2])
                    nc.tensor.matmul(g["ps_pv"][:, ob:ob + 64],
                                     g["ptb"][0][:, 0:128],
                                     v_tm[:, i, h * 64:(h + 1) * 64],
                                     start=True, stop=True)
                    p.st.inc(nc.tensor.matmul(
                        g["ps_pv"][:, ob + 64:ob + 128],
                        q_fm[base:base + 64, pr, i * 128:(i + 1) * 128],
                        kv_b[base:base + 64, pr, :],
                        start=True, stop=True))
                    tensor.wait_ge(p.sv.h, svc[("kl", pr, i, hh)])
                    if prev2 is not None:
                        tensor.wait_ge(p.sv.h, svc[("kf",) + prev2])
                    p.st.inc(nc.tensor.matmul(
                        g["ps_pv"][0:64, kb:kb + 64],
                        g["ptb"][1][:, 0:64],
                        v_tm[:, i, h * 64:(h + 1) * 64],
                        start=True, stop=True))

    @block.vector
    def _(vector):
        vector.wait_ge(p.ss.h, ss_m0 + 3 * NH)
        for pr in range(8):
            for b in range(2):
                vector.wait_ge(p.st.h, stc[("kT", pr)] - 1 + b)
                p.sv.inc(vector.tensor_copy(kpair[:, b * 1024:(b + 1) * 1024],
                                            g["ps_tp"][:]))
            for i in range(NT):
                for hh in range(2):
                    h = 2 * pr + hh
                    base = hh * 64
                    vector.wait_ge(p.st.h, stc[("S", pr, i, hh)])
                    p.sv.inc(vector.tensor_tensor(
                        g["ptb"][0][:, 0:128],
                        g["ps_s"][(i * 2 + hh) % 2][:, 0:128],
                        maskz[:, h, :], ALU.mult))
                    prev_it = LIGHT_ORDER.get((pr, i, hh))
                    if prev_it is not None:
                        vector.wait_ge(p.st.h, stc[("kv",) + prev_it])
                    p.sv.inc(vector.tensor_scalar(
                        g["ptb"][1][:, 0:64],
                        k_tm[:, i, h * 64:(h + 1) * 64],
                        lamk[:, h:h + 1], None, ALU.mult))
                    nlin = pr * 32 + i * 2 + hh
                    ob = (nlin % 2) * 512
                    kb = 1024 + (nlin % 2) * 512
                    vector.wait_ge(p.st.h, stc[("o", pr, i, hh)])
                    p.sv.inc(vector.tensor_scalar(
                        o_tm[:, i, h * 64:(h + 1) * 64],
                        g["ps_pv"][:, ob + 64:ob + 128], lamq[:, h:h + 1],
                        None, ALU.mult))
                    vector.wait_ge(p.sv.h, p.sv.n)
                    p.sv.inc(vector.tensor_add(
                        o_tm[:, i, h * 64:(h + 1) * 64],
                        o_tm[:, i, h * 64:(h + 1) * 64],
                        g["ps_pv"][:, ob:ob + 64]))
                    vector.wait_ge(p.st.h, stc[("kv", pr, i, hh)])
                    p.sv.inc(vector.scalar_tensor_tensor(
                        kv_f[base:base + 64, pr, :],
                        kv_f[base:base + 64, pr, :], lamc[h],
                        g["ps_pv"][0:64, kb:kb + 64], ALU.mult, ALU.add))
                    vector.wait_ge(p.sv.h, p.sv.n)
                    p.sv.inc(vector.tensor_copy(kv_b[base:base + 64, pr, :],
                                                kv_f[base:base + 64, pr, :]))
    assert p.st.n == st_c, (p.st.n, st_c)
    assert p.sv.n == sv_c2, (p.sv.n, sv_c2)

    sm, sq = g["sm"], g["sq"]
    ss_s0 = p.ss.n
    sv_s0 = p.sv.n

    @block.scalar
    def _(scalar):
        for i in range(NT):
            scalar.wait_ge(p.sv.h, svc[("oe", pr, i, 1)] if False else sv_c2)
            if i >= 1:
                scalar.wait_ge(p.ss.h, ss_s0 + i)
            p.ss.inc(scalar.activation(og_t[0][:], o_tm[:, i, :], AF.Square,
                                       accum_out=sq[:, i:i + 1]))
    ss_sq = p.ss.n

    @block.vector
    def _(vector):
        for i in range(NT):
            p.sv.inc(vector.reduce_sum(sm[:, i:i + 1].unsqueeze(2),
                                       o_tm[:, i, :].unsqueeze(1), AX.X))
        vector.wait_ge(p.ss.h, ss_sq)
        vector.wait_ge(p.sv.h, p.sv.n)
        p.sv.inc(vector.tensor_scalar(g["mean"][:], sm[:], 1.0 / D, None, ALU.mult))
        p.sv.inc(vector.tensor_scalar(g["var"][:], sq[:], 1.0 / D, None, ALU.mult))
        vector.wait_ge(p.sv.h, p.sv.n)
        p.sv.inc(vector.tensor_mul(sm[:], g["mean"][:], g["mean"][:]))
        vector.wait_ge(p.sv.h, p.sv.n)
        p.sv.inc(vector.tensor_sub(g["var"][:], sm[:], g["var"][:]))
    sv_var = p.sv.n

    @block.scalar
    def _(scalar):
        scalar.wait_ge(p.sv.h, sv_var)
        p.ss.inc(scalar.activation(g["rstd"][:], g["var"][:], AF.Ln, scale=-1.0,
                                   bias=g["ceps"][:]))
        scalar.wait_ge(p.ss.h, p.ss.n)
        p.ss.inc(scalar.activation(g["rstd"][:], g["rstd"][:], AF.Exp, scale=-0.5))
    ss_rstd = p.ss.n

    @block.vector
    def _(vector):
        vector.wait_ge(p.ss.h, ss_rstd)
        p.sv.inc(vector.scalar_tensor_tensor(g["nmrs"][:], g["mean"][:], -1.0,
                                             g["rstd"][:], ALU.mult, ALU.mult))
    sv_stats = p.sv.n

    stg = {}; ssg = {}; svg = {}
    st_c = p.st.n; ss_c = p.ss.n; sv_c3 = p.sv.n
    for i in range(NT):
        st_c += 2; stg[("g2", i)] = st_c
        st_c += 1; stg[("ogT", i)] = st_c
    for i in range(NT):
        ss_c += 2; ssg[("sig", i)] = ss_c
        ss_c += 1; ssg[("on", i)] = ss_c
    for i in range(NT):
        sv_c3 += 3; svg[("og", i)] = sv_c3
        sv_c3 += 1; svg[("ogTe", i)] = sv_c3

    @block.tensor
    def _(tensor):
        wwait(p, tensor, wo_rdy, 0)
        for i in range(NT):
            if i >= 1:
                tensor.wait_ge(p.sv.h, svg[("og", i - 1)])
            for half in range(2):
                p.st.inc(nc.tensor.matmul(
                    g["ps_s"][half][:],
                    g1_fm[0:64, i * 128:(i + 1) * 128],
                    g["wbuf"][1][0:64, 5, half * 512:(half + 1) * 512],
                    start=True, stop=True))
            tensor.wait_ge(p.sv.h, svg[("og", i)])
            if i >= 2:
                tensor.wait_ge(p.sv.h, svg[("ogTe", i - 2)] + 0)
            ins = None
            for c in range(NCH):
                ins = nc.tensor.transpose(g["ps_tp"][:, c * 128:(c + 1) * 128],
                                          og_t[i % 2][:, c * 128:(c + 1) * 128],
                                          g["ident"][:])
            p.st.inc(ins)

    @block.scalar
    def _(scalar):
        scalar.wait_ge(p.sv.h, sv_stats)
        for i in range(NT):
            scalar.wait_ge(p.st.h, stg[("g2", i)])
            p.ss.inc(scalar.activation(g["ps_s"][0][:], g["ps_s"][0][:], AF.Sigmoid))
            p.ss.inc(scalar.activation(g["ps_s"][1][:], g["ps_s"][1][:], AF.Sigmoid))
            if i >= 1:
                scalar.wait_ge(p.st.h, stg[("ogT", i - 1)])
            p.ss.inc(scalar.activation(og_t[i % 2][:], o_tm[:, i, :], AF.Identity,
                                       bias=g["nmrs"][:, i:i + 1],
                                       scale=g["rstd"][:, i:i + 1]))

    @block.vector
    def _(vector):
        for i in range(NT):
            vector.wait_ge(p.ss.h, ssg[("on", i)])
            p.sv.inc(vector.tensor_mul(og_t[i % 2][:], og_t[i % 2][:], gb_bc[:]))
            vector.wait_ge(p.sv.h, p.sv.n)
            p.sv.inc(vector.tensor_add(og_t[i % 2][:], og_t[i % 2][:], bb_bc[:]))
            vector.wait_ge(p.sv.h, p.sv.n)
            for half in range(2):
                ins = vector.tensor_tensor(
                    og_t[i % 2][:, half * 512:(half + 1) * 512],
                    og_t[i % 2][:, half * 512:(half + 1) * 512],
                    g["ps_s"][half][:], ALU.mult)
            p.sv.inc(ins)
            vector.wait_ge(p.st.h, stg[("ogT", i)])
            p.sv.inc(vector.tensor_copy(
                og_fm[:, :, i * 128:(i + 1) * 128],
                g["ps_tp"][:].rearrange("p (c t) -> p c t", c=NCH)))
    assert p.st.n == st_c
    assert p.ss.n == ss_c
    assert p.sv.n == sv_c3, (p.sv.n, sv_c3)
    sv_ogfm = p.sv.n

    st_o0 = p.st.n
    sv_o0 = p.sv.n
    ld_v = {}
    ss_entry = p.ss.n

    @block.tensor
    def _(tensor):
        tensor.wait_ge(p.sv.h, sv_ogfm)
        for ih in range(NT * 2):
            i, half = divmod(ih, 2)
            pb = g["ps_s"][ih % 2]
            if ih >= 2:
                tensor.wait_ge(p.sv.h, sv_o0 + ih - 1)
            ins = None
            for c in range(NCH):
                ins = nc.tensor.matmul(pb[:],
                                       og_fm[:, c, i * 128:(i + 1) * 128],
                                       g["wbuf"][0][:, c, half * 512:(half + 1) * 512],
                                       start=(c == 0), stop=(c == NCH - 1))
            p.st.inc(ins)

    @block.sync
    def _(sync):
        sync.wait_ge(p.ss.h, ss_entry)
        for i in range(NT):
            ld_v[i] = p.dx[i % 2].start(sync, g["xtile"][i % 2][:], xd[i])
            if i >= 1:
                sync.wait_ge(p.sv.h, sv_o0 + 2 * i)
                p.dx[(i - 1) % 2].start(sync, xd[i - 1], g["xtile"][(i - 1) % 2][:])
        sync.wait_ge(p.sv.h, sv_o0 + 2 * NT)
        p.dx[(NT - 1) % 2].start(sync, xd[NT - 1], g["xtile"][(NT - 1) % 2][:])

    @block.vector
    def _(vector):
        for ih in range(NT * 2):
            i, half = divmod(ih, 2)
            vector.wait_ge(p.st.h, st_o0 + ih + 1)
            vector.wait_ge(p.dx[i % 2].h, ld_v[i])
            sl = g["xtile"][i % 2][:, half * 512:(half + 1) * 512]
            p.sv.inc(vector.tensor_add(sl, sl, g["ps_s"][ih % 2][:]))


def lin_rev_h(hidx):
    return divmod(hidx, 2)



_PROG = None


def _constants():
    ident = np.eye(128).astype(ml_dtypes.bfloat16)
    triT = np.triu(np.ones((128, 128))).astype(ml_dtypes.bfloat16)
    a = np.arange(128)
    dmatTc = np.maximum(a[None, :] - a[:, None], 0).astype(np.float32)
    idxp1 = (a + 1.0).astype(np.float32).reshape(128, 1)
    idxrv = (127.0 - a).astype(np.float32).reshape(128, 1)
    import jax
    import jax.numpy as jnp
    with jax.default_device(jax.devices("cpu")[0]):
        inv_freq = 1.0 / 10000.0 ** (jnp.arange(0, 128, 2, dtype=jnp.float32) / 128)
        t = jnp.arange(S, dtype=jnp.float32)
        freqs = t[:, None] * inv_freq[None, :]
        emb = jnp.concatenate([freqs, freqs], axis=-1)
        rc = np.ascontiguousarray(np.asarray(jnp.cos(emb), np.float32))
        rs = np.ascontiguousarray(np.asarray(jnp.sin(emb), np.float32))
    return {"c_ident": ident, "c_triT": triT, "c_dmatTc": dmatTc,
            "c_idxp1": idxp1, "c_idxrv": idxrv,
            "c_ropecos": rc, "c_ropesin": rs}


def _base_map(inputs):
    consts = _constants()
    x_full = np.ascontiguousarray(inputs["L0_attn_a_x"], dtype=np.float32)
    ca = lambda k: np.ascontiguousarray(inputs[k], np.float32)
    base = {
        "L0A_Wq": ca("L0_attn_a_Wq"), "L0A_Wk": ca("L0_attn_a_Wk"),
        "L0A_Wv": ca("L0_attn_a_Wv"), "L0A_Wo": ca("L0_attn_a_Wo"),
        "L0A_bo": ca("L0_attn_a_bo"),
        "L0F_W": ca("L0_ffn_W"), "L0F_Wp": ca("L0_ffn_Wp"),
        "L0F_bp": ca("L0_ffn_bp"), "L0F_sim": ca("L0_ffn_sim"),
        "L0F_temp": ca("L0_ffn_temp"),
    }
    base["L0F_Wd_all"] = np.ascontiguousarray(np.concatenate(
        [inputs[f"L0_ffn_Wd{e}"] for e in range(4)], axis=0), np.float32)
    base["L0F_Wu_all"] = np.ascontiguousarray(np.concatenate(
        [inputs[f"L0_ffn_Wu{e}"] for e in range(4)], axis=1), np.float32)
    base["L0B_qkv_w"] = ca("L0_attn_b_qkv_w")
    base["L0B_qkv_b"] = ca("L0_attn_b_qkv_b")
    base["L0B_o_w"] = ca("L0_attn_b_o_w")
    base["L0B_o_b"] = ca("L0_attn_b_o_b")
    for nm in ["Wq", "Wk", "Wv", "Wo", "gamma", "beta", "Wg1", "Wg2"]:
        base[f"L1A_{nm}"] = ca(f"L1_attn_a_{nm}")
    base["L1F_Wg"] = ca("L1_ffn_w_gate")
    base["L1F_Wu"] = ca("L1_ffn_w_up")
    base["L1F_Wd"] = ca("L1_ffn_w_down")
    for nm in ["wq", "bq", "wqc", "bqc", "wqr", "bqr", "wkv", "bkv", "wkr",
               "bkr", "wkc", "bkc", "wv", "bv", "wo", "bo"]:
        base[f"L1B_{nm}"] = ca(f"L1_attn_b_{nm}")
    base.update(consts)
    return base, x_full


def kernel(**inputs):
    global _PROG
    if _PROG is None:
        _PROG = build_program()
    nc = _PROG

    base, x_full = _base_map(inputs)
    in_maps = []
    for c in range(8):
        m = dict(base)
        m["x"] = np.ascontiguousarray(x_full[c])
        in_maps.append(m)
    res = run_bass_kernel_spmd(nc, in_maps, core_ids=list(range(8)))
    out = np.stack([np.asarray(r["y"], np.float32) for r in res.results], axis=0)
    return out
